# version 01
# baseline (speedup 1.0000x reference)
import contextlib
import numpy as np
import concourse.bass as bass
import concourse.mybir as mybir
from concourse.bass_utils import run_bass_kernel_spmd

F32 = mybir.dt.float32
BF16 = mybir.dt.bfloat16
I32 = mybir.dt.int32
AF = mybir.ActivationFunctionType
ALU = mybir.AluOpType
AX = mybir.AxisListType

B_SZ = 4
L_SEQ = 4096
D_MODEL = 2048
D_INNER = 4096
D_CONV = 4
CONV_DIM = 4352
D_IN_PROJ = 8512
NHEADS = 64
HEADDIM = 64
CHUNK = 64
SSD_N = 128
TH = L_SEQ // 2
TE = TH + 3
NCORES = 8

DT_RANK = 64
L1_N = 32
NEXP = 16
TOPK = 1024

ENGS = ("sync", "scalar", "vector", "tensor", "gpsimd")


class Sched:

    def __init__(self, nc, ctx):
        self.nc = nc
        self.items = {e: [] for e in ENGS}
        self.sems = {}
        self.counts = {}
        self.ctx = ctx

    def sem(self, name):
        self.sems[name] = self.ctx.enter_context(self.nc.semaphore(name))
        self.counts[name] = 0
        return name

    def count(self, sem):
        return self.counts[sem]

    def op(self, eng, fn):
        self.items[eng].append(fn)

    def sig(self, eng, sem, n=1):
        h = self.sems[sem]
        self.counts[sem] += n
        self.items[eng].append(lambda e: e.drain().then_inc(h, n))
        return self.counts[sem]

    def dma(self, eng, out, in_, sem=None, **kw):
        h = self.sems[sem] if sem else None
        if sem:
            self.counts[sem] += 16
        def fn(e, out=out, in_=in_, h=h, kw=kw):
            inst = e.dma_start(out=out, in_=in_, **kw)
            if h is not None:
                inst.then_inc(h, 16)
        self.items[eng].append(fn)
        return self.counts[sem] if sem else None

    def wait(self, eng, sem, val=None):
        h = self.sems[sem]
        v = self.counts[sem] if val is None else val
        if v > 0:
            self.items[eng].append(lambda e: e.wait_ge(h, v))

    def drain(self, eng):
        self.items[eng].append(lambda e: e.drain())

    def build(self):
        block = self.ctx.enter_context(self.nc.Block())
        for engname in ENGS:
            items = self.items[engname]
            def mk(items):
                def f(eng):
                    for fn in items:
                        fn(eng)
                return f
            getattr(block, engname)(mk(items))


def make_nc():
    return bass.Bass("TRN2", target_bir_lowering=False)


def sched_ctx(nc):
    ctx = contextlib.ExitStack()
    S = Sched(nc, ctx)
    sbuf = lambda n, s, d=F32: ctx.enter_context(nc.sbuf_tensor(n, s, d))
    psum = lambda n, s, d=F32: ctx.enter_context(nc.psum_tensor(n, s, d))
    return ctx, S, sbuf, psum


def run_launch(nc, in_maps, out_names):
    res = run_bass_kernel_spmd(nc, in_maps, core_ids=list(range(NCORES)))
    return [{k: np.asarray(r[k]) for k in out_names} for r in res.results]


def id128_bf():
    import ml_dtypes
    return np.eye(128, dtype=ml_dtypes.bfloat16)


def build_A():
    nc = make_nc()
    x_in = nc.dram_tensor("x", [TE, D_MODEL], F32, kind="ExternalInput").ap()
    w_in = nc.dram_tensor("w_in", [D_MODEL, D_IN_PROJ], F32, kind="ExternalInput").ap()
    conv_w = nc.dram_tensor("conv_w", [CONV_DIM, D_CONV], F32, kind="ExternalInput").ap()
    conv_b = nc.dram_tensor("conv_b", [CONV_DIM, 1], F32, kind="ExternalInput").ap()
    dt_bias = nc.dram_tensor("dt_bias", [NHEADS, 1], F32, kind="ExternalInput").ap()
    iden = nc.dram_tensor("iden", [128, 128], BF16, kind="ExternalInput").ap()

    o_xnT = nc.dram_tensor("o_xnT", [D_MODEL, TH], BF16, kind="ExternalOutput").ap()
    o_z = nc.dram_tensor("o_z", [D_INNER, TH], F32, kind="ExternalOutput").ap()
    o_xbc = nc.dram_tensor("o_xbc", [CONV_DIM, TH], F32, kind="ExternalOutput").ap()
    o_dt = nc.dram_tensor("o_dt", [NHEADS, TH], F32, kind="ExternalOutput").ap()

    NT = 17
    NJ = 67
    TT5 = [(0, 512), (512, 512), (1024, 512), (1536, 512), (2048, 3)]

    ctx, S, sbuf, psum = sched_ctx(nc)
    with ctx:
        t_x = sbuf("t_x", [128, 2 * D_MODEL])
        t_xn = sbuf("t_xn", [128, NT * D_MODEL], BF16)
        t_sq = sbuf("t_sq", [128, NT])
        t_mu = sbuf("t_mu", [128, NT])
        t_va = sbuf("t_va", [128, NT])
        t_rs = sbuf("t_rs", [128, NT])
        t_scr = sbuf("t_scr", [128, D_MODEL])
        t_id = sbuf("t_id", [128, 128], BF16)
        t_xnT = sbuf("t_xnT", [128, 16 * TE], BF16)
        t_w = sbuf("t_w", [128, 2 * 2048], BF16)
        t_strip = [sbuf(f"t_strip{i}", [128, TE]) for i in range(2)]
        t_conv = [sbuf(f"t_conv{i}", [128, TH]) for i in range(2)]
        t_cw = sbuf("t_cw", [128, 34 * D_CONV])
        t_cb = sbuf("t_cb", [128, 34])
        t_dtb = sbuf("t_dtb", [NHEADS, 1])
        p_t = [psum(f"p_t{i}", [128, 1024], BF16) for i in range(2)]
        p_mm = [psum(f"p_mm{i}", [128, 512]) for i in range(2)]

        for s in ("ld", "lxa", "lxb", "lwa", "lwb", "vv", "ss", "tt", "st", "ev"):
            S.sem(s)

        S.dma("sync", t_id[:], iden, sem="ld")
        S.dma("sync", t_cw[:].rearrange("p (a w) -> p a w", w=D_CONV), conv_w.rearrange("(a p) w -> p a w", p=128), sem="ld")
        S.dma("sync", t_cb[:].rearrange("p (a o) -> p a o", o=1), conv_b.rearrange("(a p) o -> p a o", p=128), sem="ld", allow_slow_non_contiguous=True)
        S.dma("sync", t_dtb[:], dt_bias, sem="ld")
        for i in range(NT):
            r = min(128, TE - i * 128)
            if i >= 2:
                S.wait("sync", "vv", 2 * (i - 1))
            S.dma("sync", t_x[0:r, (i % 2) * D_MODEL:(i % 2 + 1) * D_MODEL],
                  x_in[i * 128:i * 128 + r, :], sem=("lxa" if i % 2 == 0 else "lxb"))

        for i in range(NT):
            r = min(128, TE - i * 128)
            xv = t_x[0:r, (i % 2) * D_MODEL:(i % 2 + 1) * D_MODEL]
            S.wait("scalar", "lxa" if i % 2 == 0 else "lxb", 16 * (i // 2 + 1))
            S.wait("scalar", "ld", 64)
            S.op("scalar", lambda e, xv=xv, r=r, i=i: e.activation(
                t_scr[0:r, :], xv, AF.Square, accum_out=t_sq[0:r, i:i + 1]))
            S.sig("scalar", "ss")
            S.wait("vector", "lxa" if i % 2 == 0 else "lxb", 16 * (i // 2 + 1))
            S.wait("vector", "ld", 64)
            S.op("vector", lambda e, xv=xv, r=r, i=i: e.reduce_sum(
                t_mu[0:r, i:i + 1], xv, axis=AX.X))
            S.wait("vector", "ss")
            S.drain("vector")
            S.op("vector", lambda e, r=r, i=i: e.tensor_scalar(
                t_mu[0:r, i:i + 1], t_mu[0:r, i:i + 1], 1.0 / D_MODEL, None, ALU.mult))
            S.op("vector", lambda e, r=r, i=i: e.tensor_scalar(
                t_va[0:r, i:i + 1], t_sq[0:r, i:i + 1], 1.0 / D_MODEL, None, ALU.mult))
            S.drain("vector")
            S.op("vector", lambda e, r=r, i=i: e.scalar_tensor_tensor(
                t_va[0:r, i:i + 1], t_mu[0:r, i:i + 1], t_mu[0:r, i:i + 1],
                t_va[0:r, i:i + 1], ALU.mult, ALU.subtract))
            S.drain("vector")
            S.op("vector", lambda e, r=r, i=i: e.tensor_scalar(
                t_va[0:r, i:i + 1], t_va[0:r, i:i + 1], -1.0, 1e-5, ALU.mult, ALU.add))
            S.drain("vector")
            S.op("vector", lambda e, r=r, i=i: e.reciprocal(
                t_rs[0:r, i:i + 1], t_va[0:r, i:i + 1]))
            S.sig("vector", "vv")
            S.wait("scalar", "vv")
            S.op("scalar", lambda e, r=r, i=i: e.activation(
                t_rs[0:r, i:i + 1], t_rs[0:r, i:i + 1], AF.Sqrt))
            S.sig("scalar", "ss")
            S.wait("vector", "ss")
            S.op("vector", lambda e, xv=xv, r=r, i=i: e.tensor_scalar(
                t_xn[0:r, i * D_MODEL:(i + 1) * D_MODEL], xv,
                t_mu[0:r, i:i + 1], t_rs[0:r, i:i + 1], ALU.subtract, ALU.mult))
            S.sig("vector", "vv")
        ln_vv = S.count("vv")

        S.wait("tensor", "vv", ln_vv)
        nev = 0
        for i in range(NT):
            r = min(128, TE - i * 128)
            for dtile in range(16):
                pb = p_t[nev % 2]
                if nev >= 2:
                    S.wait("tensor", "ev", nev - 1)
                S.op("tensor", lambda e, pb=pb, r=r, i=i, dtile=dtile:
                     nc.tensor.transpose(
                         pb[:, 0:r],
                         t_xn[0:r, i * D_MODEL + dtile * 128: i * D_MODEL + (dtile + 1) * 128],
                         t_id[0:r, 0:r]))
                S.sig("tensor", "tt")
                S.wait("vector", "tt")
                S.op("vector", lambda e, pb=pb, r=r, i=i, dtile=dtile: e.tensor_copy(
                    t_xnT[:, dtile * TE + i * 128: dtile * TE + i * 128 + r], pb[:, 0:r]))
                S.sig("vector", "ev")
                nev += 1
        xnT_ev = S.count("ev")

        S.wait("gpsimd", "ev", xnT_ev)
        for dtile in range(16):
            S.dma("gpsimd", o_xnT[dtile * 128:(dtile + 1) * 128, :],
                  t_xnT[:, dtile * TE + 3: dtile * TE + TE], sem="st")

        tt_after_mm = {}
        st_after = {}
        nmm = 0
        for j in range(NJ):
            jw = 128 if j < 66 else 64
            sj = j % 2
            strip = t_strip[sj]
            cvb = t_conv[sj]
            if j >= 2:
                S.wait("gpsimd", "tt", tt_after_mm[j - 2])
            S.dma("gpsimd", t_w[0:128, sj * 2048: sj * 2048 + 16 * jw].rearrange(
                      "p (a j) -> p a j", j=jw),
                  w_in[:, j * 128:j * 128 + jw].rearrange("(a p) j -> p a j", p=128),
                  sem=("lwa" if sj == 0 else "lwb"))
            S.wait("tensor", "lwa" if sj == 0 else "lwb", 16 * (j // 2 + 1))
            if j == 0:
                S.wait("tensor", "ev", xnT_ev)
            if j >= 2:
                S.wait("vector", "st")
            for ti, (tq, tw) in enumerate(TT5):
                pm = p_mm[nmm % 2]
                if nmm >= 2:
                    S.wait("tensor", "ev", xnT_ev + nmm - 1)
                for k in range(16):
                    S.op("tensor", lambda e, pm=pm, jw=jw, tw=tw, sj=sj, k=k, tq=tq:
                         nc.tensor.matmul(pm[0:jw, 0:tw],
                                          t_w[:, sj * 2048 + k * jw: sj * 2048 + (k + 1) * jw],
                                          t_xnT[:, k * TE + tq: k * TE + tq + tw],
                                          start=(k == 0), stop=(k == 15)))
                S.sig("tensor", "tt")
                S.wait("vector", "tt")
                S.op("vector", lambda e, pm=pm, strip=strip, jw=jw, tq=tq, tw=tw:
                     e.tensor_copy(strip[0:jw, tq:tq + tw], pm[0:jw, 0:tw]))
                S.sig("vector", "ev")
                nmm += 1
            tt_after_mm[j] = S.count("tt")
            S.drain("vector")
            if j < 32:
                S.wait("gpsimd", "ev")
                st_after[j] = S.dma("gpsimd", o_z[j * 128:(j + 1) * 128, :],
                                    strip[:, 3:TE], sem="st")
            elif j < 66:
                cj = j - 32
                S.op("vector", lambda e, strip=strip, cvb=cvb, cj=cj: e.tensor_scalar(
                    cvb[:], strip[:, 0:TH], t_cw[:, cj * 4:cj * 4 + 1], None, ALU.mult))
                for w in range(1, 4):
                    S.drain("vector")
                    S.op("vector", lambda e, strip=strip, cvb=cvb, cj=cj, w=w:
                         e.scalar_tensor_tensor(cvb[:], strip[:, w:w + TH],
                                                t_cw[:, cj * 4 + w:cj * 4 + w + 1],
                                                cvb[:], ALU.mult, ALU.add))
                S.drain("vector")
                S.op("vector", lambda e, cvb=cvb, cj=cj: e.tensor_scalar(
                    cvb[:], cvb[:], t_cb[:, cj:cj + 1], None, ALU.add))
                S.sig("vector", "vv")
                S.wait("scalar", "vv")
                S.op("scalar", lambda e, cvb=cvb: e.activation(cvb[:], cvb[:], AF.Silu))
                S.sig("scalar", "ss")
                S.wait("gpsimd", "ss")
                st_after[j] = S.dma("gpsimd", o_xbc[cj * 128:(cj + 1) * 128, :],
                                    cvb[:], sem="st")
            else:
                S.wait("vector", "st")
                sp0 = t_conv[0]
                sp1 = t_conv[1]
                S.op("vector", lambda e, strip=strip, sp0=sp0: e.tensor_scalar(
                    sp0[0:64, 0:TH], strip[0:64, 3:TE], t_dtb[:, 0:1], None, ALU.add))
                S.sig("vector", "vv")
                S.wait("scalar", "vv")
                S.wait("scalar", "st")
                S.op("scalar", lambda e, sp0=sp0, sp1=sp1: e.activation(sp1[0:64, 0:TH], sp0[0:64, 0:TH], AF.Abs))
                S.drain("scalar")
                S.op("scalar", lambda e, sp1=sp1: e.activation(sp1[0:64, 0:TH], sp1[0:64, 0:TH], AF.Exp, scale=-1.0))
                S.drain("scalar")
                S.op("scalar", lambda e, sp1=sp1: e.activation(sp1[0:64, 0:TH], sp1[0:64, 0:TH], AF.Ln, bias=1.0))
                S.drain("scalar")
                S.op("scalar", lambda e, sp0=sp0: e.activation(sp0[0:64, 0:TH], sp0[0:64, 0:TH], AF.Relu))
                S.sig("scalar", "ss")
                S.wait("vector", "ss")
                S.op("vector", lambda e, sp0=sp0, sp1=sp1: e.tensor_tensor(
                    sp0[0:64, 0:TH], sp0[0:64, 0:TH], sp1[0:64, 0:TH], ALU.add))
                S.sig("vector", "vv")
                S.wait("gpsimd", "vv")
                st_after[j] = S.dma("gpsimd", o_dt[:], sp0[0:64, 0:TH], sem="st")

        S.wait("gpsimd", "st")
        S.build()
    return nc


def shard_A(inputs):
    x = np.asarray(inputs["L0_attn_u"], np.float32).reshape(B_SZ * L_SEQ, D_MODEL)
    w_in = np.ascontiguousarray(inputs["L0_attn_W_in"], dtype=np.float32)
    conv_w = np.ascontiguousarray(inputs["L0_attn_conv_w"], dtype=np.float32)
    conv_b = np.ascontiguousarray(inputs["L0_attn_conv_b"], dtype=np.float32).reshape(-1, 1)
    dt_bias = np.ascontiguousarray(inputs["L0_attn_dt_bias"], dtype=np.float32).reshape(-1, 1)
    iden = id128_bf()
    maps = []
    for c in range(NCORES):
        b, h = divmod(c, 2)
        start = b * L_SEQ + h * TH
        xs = np.zeros((TE, D_MODEL), np.float32)
        lo = max(b * L_SEQ, start - 3)
        xs[3 - (start - lo):] = x[lo:start + TH]
        maps.append(dict(x=xs, w_in=w_in, conv_w=conv_w, conv_b=conv_b,
                         dt_bias=dt_bias, iden=iden))
    return maps


def kernel(**inputs):
    raise NotImplementedError("under construction")


def build_B(LQ=L_SEQ):
    nc = make_nc()
    NC64 = LQ // CHUNK
    HL = 32
    HP = 2048
    X_in = nc.dram_tensor("X", [LQ, HP], F32, kind="ExternalInput").ap()
    dtT_in = nc.dram_tensor("dtT", [HL, LQ], F32, kind="ExternalInput").ap()
    dtk_in = nc.dram_tensor("dtk", [LQ, HL], F32, kind="ExternalInput").ap()
    BT_in = nc.dram_tensor("BT", [SSD_N, LQ], F32, kind="ExternalInput").ap()
    CT_in = nc.dram_tensor("CT", [SSD_N, LQ], F32, kind="ExternalInput").ap()
    Btk_in = nc.dram_tensor("Btk", [LQ, SSD_N], F32, kind="ExternalInput").ap()
    alog_in = nc.dram_tensor("alog", [HL, 1], F32, kind="ExternalInput").ap()
    mask_in = nc.dram_tensor("mask", [CHUNK, HP], F32, kind="ExternalInput").ap()
    idf_in = nc.dram_tensor("idf", [128, 128], F32, kind="ExternalInput").ap()
    Y_out = nc.dram_tensor("Y", [LQ, HP], F32, kind="ExternalOutput").ap()
    a_scr = nc.dram_tensor("a_scr", [HL, LQ], F32, kind="Internal").ap()

    ctx, S, sbuf, psum = sched_ctx(nc)
    with ctx:
        t_X = sbuf("t_X", [CHUNK, HP])
        t_dtT = sbuf("t_dtT", [HL, LQ])
        t_ac = sbuf("t_ac", [HL, LQ])
        t_ds = sbuf("t_ds", [HL, LQ])
        t_dtk = sbuf("t_dtk", [CHUNK, NC64 * HL])
        t_BT = sbuf("t_BT", [SSD_N, LQ], BF16)
        t_CT = sbuf("t_CT", [SSD_N, LQ], BF16)
        t_Btk = sbuf("t_Btk", [CHUNK, NC64 * SSD_N], BF16)
        t_acT = sbuf("t_acT", [CHUNK, NC64 * HL])
        t_dsT = sbuf("t_dsT", [CHUNK, NC64 * HL])
        t_odT = sbuf("t_odT", [CHUNK, NC64 * HL])
        t_al = sbuf("t_al", [HL, 2])
        t_zer = sbuf("t_zer", [HL, CHUNK])
        t_one = sbuf("t_one", [1, 128])
        t_id = sbuf("t_id", [128, 128])
        t_msk = sbuf("t_msk", [CHUNK, HP], BF16)
        t_Xdt = sbuf("t_Xdt", [CHUNK, HP], BF16)
        t_Xdc = sbuf("t_Xdc", [CHUNK, HP], BF16)
        t_t2m = sbuf("t_t2m", [CHUNK, HP])
        t_arg = sbuf("t_arg", [CHUNK, HP])
        t_L = sbuf("t_L", [CHUNK, HP])
        t_Mt = sbuf("t_Mt", [CHUNK, HP], BF16)
        t_CB = sbuf("t_CB", [CHUNK, CHUNK])
        t_Y = sbuf("t_Y", [CHUNK, HP])
        t_tmp = sbuf("t_tmp", [CHUNK, 512])
        t_H = sbuf("t_H", [SSD_N, HP])
        t_Hb = sbuf("t_Hb", [SSD_N, HP], BF16)
        t_Ar = sbuf("t_Ar", [1, HP])
        t_Er = sbuf("t_Er", [1, HP])
        pb = [psum(f"pb{i}", [128, 512]) for i in range(8)]

        for s in ("ld", "lg", "lx", "la", "vd", "sa", "tp", "sy", "as_"):
            S.sem(s)

        S.dma("sync", t_dtT[:], dtT_in, sem="ld")
        S.dma("sync", t_al[:, 0:1], alog_in, sem="ld")
        S.dma("sync", t_id[:], idf_in, sem="ld")
        S.dma("sync", t_dtk[:].rearrange("p (a h) -> p a h", h=HL),
              dtk_in.rearrange("(a p) h -> p a h", p=CHUNK), sem="ld")
        S.dma("gpsimd", t_msk[:], mask_in, sem="lg")
        S.dma("gpsimd", t_BT[:], BT_in, sem="lg")
        S.dma("gpsimd", t_CT[:], CT_in, sem="lg")
        S.dma("gpsimd", t_Btk[:].rearrange("p (a n) -> p a n", n=SSD_N),
              Btk_in.rearrange("(a p) n -> p a n", p=CHUNK), sem="lg")
        ld_all = S.count("ld")
        lg_all = S.count("lg")

        S.op("vector", lambda e: e.memset(t_zer[:], 0.0))
        S.op("vector", lambda e: e.memset(t_one[:], 1.0))
        S.op("vector", lambda e: e.memset(t_H[:], 0.0))
        S.op("vector", lambda e: e.memset(t_Hb[:], 0.0))
        S.wait("scalar", "ld", ld_all)
        S.wait("scalar", "lg", lg_all)
        S.op("scalar", lambda e: e.activation(t_al[:, 1:2], t_al[:, 0:1], AF.Exp))
        S.sig("scalar", "sa")
        S.wait("vector", "sa")
        S.op("vector", lambda e: e.tensor_scalar(t_al[:, 1:2], t_al[:, 1:2], -1.0, None, ALU.mult))
        S.drain("vector")
        S.op("vector", lambda e: e.tensor_scalar(t_dtT[:], t_dtT[:], t_al[:, 1:2], None, ALU.mult))
        S.drain("vector")
        for c in range(NC64):
            sl0 = c * CHUNK
            S.op("vector", lambda e, sl0=sl0: e.tensor_tensor_scan(
                t_ac[:, sl0:sl0 + CHUNK], t_dtT[:, sl0:sl0 + CHUNK], t_zer[:],
                0.0, ALU.add, ALU.add))
        S.sig("vector", "vd")
        for c in range(NC64):
            sl0 = c * CHUNK
            S.op("vector", lambda e, sl0=sl0: e.tensor_scalar(
                t_ds[:, sl0:sl0 + CHUNK], t_ac[:, sl0:sl0 + CHUNK],
                t_ac[:, sl0 + CHUNK - 1:sl0 + CHUNK], None, ALU.subtract))
        S.sig("vector", "vd")
        S.wait("gpsimd", "vd")
        S.dma("gpsimd", a_scr, t_ac[:], sem="as_")
        S.wait("scalar", "vd")
        S.op("scalar", lambda e: e.activation(t_dtT[:], t_ac[:], AF.Exp))
        S.drain("scalar")
        S.op("scalar", lambda e: e.activation(t_ds[:], t_ds[:], AF.Exp, scale=-1.0))
        S.sig("scalar", "sa")
        S.wait("tensor", "sa")
        S.wait("tensor", "ld", ld_all)
        S.wait("tensor", "lg", lg_all)
        tr_vd = []
        ntr = 0
        for (srct, dst) in ((t_ac, t_acT), (t_ds, t_dsT), (t_dtT, t_odT)):
            for c in range(NC64):
                bank = pb[ntr % 2]
                if ntr >= 2:
                    S.wait("tensor", "vd", tr_vd[ntr - 2])
                S.op("tensor", lambda e, srct=srct, c=c, bank=bank:
                     nc.tensor.transpose(bank[0:CHUNK, 0:HL],
                                         srct[:, c * CHUNK:(c + 1) * CHUNK],
                                         t_id[0:HL, 0:HL]))
                S.sig("tensor", "tp")
                S.wait("vector", "tp")
                S.op("vector", lambda e, dst=dst, c=c, bank=bank: e.tensor_copy(
                    dst[:, c * HL:(c + 1) * HL], bank[0:CHUNK, 0:HL]))
                tr_vd.append(S.sig("vector", "vd"))
                ntr += 1
        pre_vd = S.count("vd")

        rec = {"ydiag": {}, "states": {}, "comb": {}, "hupd": {}, "mt": {},
               "arow": {}, "odf": {}, "t1": {}, "yst": {}, "lexp": {},
               "erep": {}, "ercp": {}, "xdt": {}}
        for c in range(NC64):
            sl0 = c * CHUNK
            if c >= 1:
                S.wait("sync", "vd", rec["xdt"][c - 1])
            S.dma("sync", t_X[:], X_in[sl0:sl0 + CHUNK, :], sem="lx")
            S.wait("sync", "as_", 16)
            if c >= 1:
                S.wait("sync", "vd", rec["ercp"][c - 1])
            S.dma("sync", t_Ar[0:1, :].rearrange("o (h l) -> o h l", l=CHUNK),
                  a_scr[:, sl0:sl0 + CHUNK].unsqueeze(0), sem="la")
            rec["arow"][c] = S.count("la")

            S.wait("vector", "lx")
            if c >= 1:
                S.wait("vector", "tp", rec["ydiag"][c - 1])
            dtk_sl = t_dtk[:, c * HL:(c + 1) * HL]
            S.op("vector", lambda e, dtk_sl=dtk_sl: e.tensor_tensor(
                t_Xdt[:].rearrange("l (h p) -> l h p", p=HEADDIM),
                t_X[:].rearrange("l (h p) -> l h p", p=HEADDIM),
                dtk_sl[:, :, None].broadcast_to([CHUNK, HL, HEADDIM]), ALU.mult))
            S.drain("vector")
            if c >= 1:
                S.wait("vector", "tp", rec["states"][c - 1])
            dsT_sl = t_dsT[:, c * HL:(c + 1) * HL]
            S.op("vector", lambda e, dsT_sl=dsT_sl: e.tensor_tensor(
                t_Xdc[:].rearrange("l (h p) -> l h p", p=HEADDIM),
                t_Xdt[:].rearrange("l (h p) -> l h p", p=HEADDIM),
                dsT_sl[:, :, None].broadcast_to([CHUNK, HL, HEADDIM]), ALU.mult))
            S.sig("vector", "vd")
            rec["xdt"][c] = S.count("vd")
            acT_sl = t_acT[:, c * HL:(c + 1) * HL]
            S.op("vector", lambda e, acT_sl=acT_sl: e.tensor_tensor(
                t_t2m[:].rearrange("l (h p) -> l h p", p=HEADDIM),
                acT_sl[:, :, None].broadcast_to([CHUNK, HL, HEADDIM]),
                t_msk[:].rearrange("l (h p) -> l h p", p=HEADDIM), ALU.subtract))
            S.sig("vector", "vd")

            if c >= 1:
                S.wait("tensor", "vd", rec["hupd"][c - 1])
            S.op("tensor", lambda e, sl0=sl0: nc.tensor.matmul(
                pb[4][0:CHUNK, 0:CHUNK], t_BT[:, sl0:sl0 + CHUNK],
                t_CT[:, sl0:sl0 + CHUNK], start=True, stop=True))
            S.sig("tensor", "tp")
            S.wait("vector", "tp")
            S.op("vector", lambda e: e.tensor_copy(t_CB[:], pb[4][0:CHUNK, 0:CHUNK]))
            S.sig("vector", "vd")

            if c >= 1:
                S.wait("vector", "sa", rec["lexp"][c - 1])
            S.wait("tensor", "la", rec["arow"][c])
            arg_vd = [0, 0]
            for i in range(4):
                bank = pb[2 + i % 2]
                if c >= 1 and i < 2:
                    S.wait("tensor", "vd", rec["hupd"][c - 1])
                if i >= 2:
                    S.wait("tensor", "vd", arg_vd[i - 2])
                S.op("tensor", lambda e, bank=bank, i=i: nc.tensor.matmul(
                    bank[0:CHUNK, :], t_one[:, 0:CHUNK],
                    t_Ar[0:1, i * 512:(i + 1) * 512],
                    start=True, stop=True))
                S.sig("tensor", "tp")
                S.wait("vector", "tp")
                S.op("vector", lambda e, bank=bank, i=i: e.tensor_tensor(
                    t_arg[:, i * 512:(i + 1) * 512], bank[0:CHUNK, :],
                    t_t2m[:, i * 512:(i + 1) * 512], ALU.subtract))
                v = S.sig("vector", "vd")
                if i < 2:
                    arg_vd[i] = v
            rec["t1"][c] = S.count("tp")

            S.wait("scalar", "vd")
            S.op("scalar", lambda e: e.activation(t_L[:], t_arg[:], AF.Exp))
            S.sig("scalar", "sa")
            rec["lexp"][c] = S.count("sa")
            S.wait("scalar", "la", rec["arow"][c])
            S.wait("scalar", "tp", rec["t1"][c])
            S.op("scalar", lambda e: e.activation(t_Ar[0:1, :], t_Ar[0:1, :], AF.Exp))
            S.sig("scalar", "sa")
            rec["odf"][c] = S.count("sa")

            if c >= 1:
                S.wait("vector", "tp", rec["erep"][c - 1])
            S.wait("vector", "sa", rec["odf"][c])
            S.op("vector", lambda e: e.tensor_copy(
                t_Er[:].rearrange("o (h p) -> o h p", p=HEADDIM),
                t_Ar[0:1, :].rearrange("o (h l) -> o h l", l=CHUNK)
                [:, :, CHUNK - 1:CHUNK].broadcast_to([1, HL, HEADDIM])))
            S.sig("vector", "vd")
            rec["ercp"][c] = S.count("vd")
            S.op("vector", lambda e: e.tensor_tensor(
                t_Mt[:].rearrange("l (h p) -> l h p", p=HEADDIM),
                t_L[:].rearrange("l (h p) -> l h p", p=HEADDIM),
                t_CB[:, None, :].broadcast_to([CHUNK, HL, CHUNK]), ALU.mult))
            S.sig("vector", "vd")
            rec["mt"][c] = S.count("vd")

            if c >= 1:
                S.wait("vector", "sy", rec["yst"][c - 1])
            S.wait("tensor", "vd", rec["mt"][c])
            comb_vd = [0, 0]
            for i in range(4):
                ydb = pb[0 + i % 2]
                yob = pb[2 + i % 2]
                if c >= 1 and i < 2:
                    S.wait("tensor", "vd", rec["hupd"][c - 1])
                if i >= 2:
                    S.wait("tensor", "vd", comb_vd[i - 2])
                for hh in range(8):
                    h = i * 8 + hh
                    S.op("tensor", lambda e, ydb=ydb, h=h, hh=hh: nc.tensor.matmul(
                        ydb[0:CHUNK, hh * 64:(hh + 1) * 64],
                        t_Mt[:, h * 64:(h + 1) * 64],
                        t_Xdt[:, h * 64:(h + 1) * 64],
                        start=(hh == 0), stop=(hh == 7)))
                S.sig("tensor", "tp")
                S.op("tensor", lambda e, yob=yob, sl0=sl0, i=i: nc.tensor.matmul(
                    yob[0:CHUNK, :], t_CT[:, sl0:sl0 + CHUNK],
                    t_Hb[:, i * 512:(i + 1) * 512], start=True, stop=True))
                S.sig("tensor", "tp")
                S.wait("vector", "tp")
                odT_sl = t_odT[:, c * HL + i * 8: c * HL + (i + 1) * 8]
                S.op("vector", lambda e, yob=yob, odT_sl=odT_sl: e.tensor_tensor(
                    t_tmp[:].rearrange("l (h p) -> l h p", p=HEADDIM),
                    yob[0:CHUNK, :].rearrange("l (h p) -> l h p", p=HEADDIM),
                    odT_sl[:, :, None].broadcast_to([CHUNK, 8, HEADDIM]), ALU.mult))
                S.drain("vector")
                S.op("vector", lambda e, ydb=ydb, i=i: e.tensor_tensor(
                    t_Y[:, i * 512:(i + 1) * 512], t_tmp[:], ydb[0:CHUNK, :], ALU.add))
                v = S.sig("vector", "vd")
                if i < 2:
                    comb_vd[i] = v
            rec["ydiag"][c] = S.count("tp")
            rec["comb"][c] = S.count("vd")

            S.wait("gpsimd", "vd", rec["comb"][c])
            rec["yst"][c] = S.dma("gpsimd", Y_out[sl0:sl0 + CHUNK, :], t_Y[:], sem="sy")

            S.wait("tensor", "vd", S.count("vd"))
            for i in range(4):
                stb = pb[4 + i]
                for hh in range(8):
                    h = i * 8 + hh
                    S.op("tensor", lambda e, stb=stb, h=h, hh=hh, c=c: nc.tensor.matmul(
                        stb[:, hh * 64:(hh + 1) * 64],
                        t_Btk[:, c * SSD_N:(c + 1) * SSD_N],
                        t_Xdc[:, h * 64:(h + 1) * 64],
                        start=(hh == 0), stop=(hh == 7)))
                S.sig("tensor", "tp")
            rec["states"][c] = S.count("tp")
            S.wait("tensor", "vd", rec["comb"][c])
            hup_vd = [0, 0]
            for i in range(4):
                erb = pb[0 + i % 2]
                if i >= 2:
                    S.wait("tensor", "vd", hup_vd[i - 2])
                S.op("tensor", lambda e, erb=erb, i=i: nc.tensor.matmul(
                    erb[:, :], t_one[:, :], t_Er[0:1, i * 512:(i + 1) * 512],
                    start=True, stop=True))
                S.sig("tensor", "tp")
                S.wait("vector", "tp")
                S.op("vector", lambda e, erb=erb, i=i: e.tensor_tensor(
                    t_H[:, i * 512:(i + 1) * 512], t_H[:, i * 512:(i + 1) * 512],
                    erb[:, :], ALU.mult))
                S.drain("vector")
                S.op("vector", lambda e, i=i: e.tensor_tensor(
                    t_H[:, i * 512:(i + 1) * 512], t_H[:, i * 512:(i + 1) * 512],
                    pb[4 + i][:, :], ALU.add))
                S.drain("vector")
                S.op("vector", lambda e, i=i: e.tensor_copy(
                    t_Hb[:, i * 512:(i + 1) * 512], t_H[:, i * 512:(i + 1) * 512]))
                v = S.sig("vector", "vd")
                if i < 2:
                    hup_vd[i] = v
            rec["erep"][c] = S.count("tp")
            rec["hupd"][c] = S.count("vd")

        S.wait("gpsimd", "sy")
        S.build()
    return nc


def shard_B(xbc_full, dt_full, inputs):
    A_log = np.asarray(inputs["L0_attn_A_log"], np.float32)
    D_par = np.asarray(inputs["L0_attn_D_param"], np.float32)
    l = np.arange(CHUNK)
    m = np.where(l[None, :] >= l[:, None], 0.0, -1e9).astype(np.float32)
    mask = np.ascontiguousarray(np.tile(m[:, None, :], (1, 32, 1)).reshape(CHUNK, 2048))
    idf = np.eye(128, dtype=np.float32)
    maps = []
    for c in range(NCORES):
        b, hh = divmod(c, 2)
        X = np.ascontiguousarray(xbc_full[b][:, hh * 2048:(hh + 1) * 2048])
        Bm = xbc_full[b][:, 4096:4224]
        Cm = xbc_full[b][:, 4224:4352]
        dts = dt_full[b][:, hh * 32:(hh + 1) * 32]
        maps.append(dict(
            X=X, dtT=np.ascontiguousarray(dts.T), dtk=np.ascontiguousarray(dts),
            BT=np.ascontiguousarray(Bm.T), CT=np.ascontiguousarray(Cm.T),
            Btk=np.ascontiguousarray(Bm),
            alog=np.ascontiguousarray(A_log[hh * 32:(hh + 1) * 32].reshape(-1, 1)),
            mask=mask, idf=idf))
    return maps


def build_C():
    nc = make_nc()
    YT_in = nc.dram_tensor("YT", [D_INNER, TH], F32, kind="ExternalInput").ap()
    z_in = nc.dram_tensor("z", [D_INNER, TH], F32, kind="ExternalInput").ap()
    X_in = nc.dram_tensor("Xfm", [D_INNER, TH], F32, kind="ExternalInput").ap()
    dp_in = nc.dram_tensor("dpar", [128, 32], F32, kind="ExternalInput").ap()
    nw_in = nc.dram_tensor("nw", [128, 32], F32, kind="ExternalInput").ap()
    wo_in = nc.dram_tensor("w_out", [D_INNER, D_MODEL], F32, kind="ExternalInput").ap()
    x0_in = nc.dram_tensor("x0T", [D_MODEL, TH], F32, kind="ExternalInput").ap()
    r0_out = nc.dram_tensor("r0", [D_MODEL, TH], F32, kind="ExternalOutput").ap()

    NB = 32
    ctx, S, sbuf, psum = sched_ctx(nc)
    with ctx:
        t_Y = sbuf("t_Y", [128, TH])
        t_z = sbuf("t_z", [128, TH])
        t_X = sbuf("t_X", [128, TH])
        t_dp = sbuf("t_dp", [128, 32])
        t_nw = sbuf("t_nw", [128, 32])
        t_yf = sbuf("t_yf", [128, NB * TH], BF16)
        t_sq = [sbuf(f"t_sq{i}", [128, TH]) for i in range(2)]
        t_ms = sbuf("t_ms", [1, TH])
        t_one = sbuf("t_one", [1, 128])
        t_onec = sbuf("t_onec", [128, 1])
        t_w = sbuf("t_w", [128, 2 * NB * 128], BF16)
        t_x0 = t_sq[1]
        t_ev = t_sq[0]
        p_st = [psum(f"p_st{i}", [1, 512]) for i in range(4)]
        p_rp = [psum(f"p_rp{i}", [128, 512]) for i in range(2)]
        p_mm = [psum(f"p_mm{i}", [128, 512]) for i in range(2)]

        for s in ("ld", "lwa", "lwb", "vd", "sa", "tp", "st"):
            S.sem(s)

        S.dma("sync", t_dp[:], dp_in, sem="ld")
        S.dma("sync", t_nw[:], nw_in, sem="ld")
        S.op("vector", lambda e: e.memset(t_one[:], 1.0))
        S.op("vector", lambda e: e.memset(t_onec[:], 1.0))
        S.sig("vector", "vd")
        S.wait("tensor", "vd")

        sq_tp = {}
        for blk in range(NB):
            if blk >= 1:
                S.wait("sync", "sa")
            S.dma("sync", t_Y[:], YT_in[blk * 128:(blk + 1) * 128, :], sem="ld")
            S.dma("sync", t_z[:], z_in[blk * 128:(blk + 1) * 128, :], sem="ld")
            S.dma("sync", t_X[:], X_in[blk * 128:(blk + 1) * 128, :], sem="ld")
            S.wait("vector", "ld")
            S.op("vector", lambda e, blk=blk: e.scalar_tensor_tensor(
                t_Y[:], t_X[:], t_dp[:, blk:blk + 1], t_Y[:], ALU.mult, ALU.add))
            S.sig("vector", "vd")
            S.wait("scalar", "ld")
            S.wait("scalar", "vd")
            S.op("scalar", lambda e: e.activation(t_z[:], t_z[:], AF.Silu))
            S.sig("scalar", "sa")
            S.wait("vector", "sa")
            sqb = t_sq[blk % 2]
            if blk >= 2:
                S.wait("vector", "tp", sq_tp[blk - 2])
            S.op("vector", lambda e, sqb=sqb: e.tensor_tensor(
                sqb[:], t_Y[:], t_z[:], ALU.mult))
            S.sig("vector", "vd")
            S.wait("scalar", "vd")
            S.op("scalar", lambda e, sqb=sqb, blk=blk: e.copy(
                t_yf[:, blk * TH:(blk + 1) * TH], sqb[:]))
            S.drain("scalar")
            S.op("scalar", lambda e, sqb=sqb: e.activation(sqb[:], sqb[:], AF.Square))
            S.sig("scalar", "sa")
            S.wait("tensor", "sa")
            for tt_i in range(4):
                S.op("tensor", lambda e, sqb=sqb, tt_i=tt_i, blk=blk: nc.tensor.matmul(
                    p_st[tt_i][:, :], t_onec[:, :],
                    sqb[:, tt_i * 512:(tt_i + 1) * 512],
                    start=(blk == 0), stop=(blk == NB - 1), skip_group_check=True))
            S.sig("tensor", "tp")
            sq_tp[blk] = S.count("tp")

        S.wait("vector", "tp")
        for tt_i in range(4):
            S.op("vector", lambda e, tt_i=tt_i: e.tensor_scalar(
                t_ms[:, tt_i * 512:(tt_i + 1) * 512], p_st[tt_i][:, :],
                1.0 / D_INNER, 1e-5, ALU.mult, ALU.add))
        S.drain("vector")
        S.op("vector", lambda e: e.reciprocal(t_ms[:], t_ms[:]))
        S.sig("vector", "vd")
        S.wait("scalar", "vd")
        S.op("scalar", lambda e: e.activation(t_ms[:], t_ms[:], AF.Sqrt))
        S.sig("scalar", "sa")

        S.wait("tensor", "sa")
        yn_vd = {}
        rep_done = {}
        for tt_i in range(4):
            if tt_i >= 2:
                S.wait("tensor", "vd", rep_done[tt_i - 2])
            S.op("tensor", lambda e, tt_i=tt_i: nc.tensor.matmul(
                p_rp[tt_i % 2][:, :], t_one[:, :], t_ms[:, tt_i * 512:(tt_i + 1) * 512],
                start=True, stop=True))
            S.sig("tensor", "tp")
            S.wait("vector", "tp")
            for blk in range(NB):
                sl = slice(blk * TH + tt_i * 512, blk * TH + (tt_i + 1) * 512)
                S.op("vector", lambda e, sl=sl, tt_i=tt_i: e.tensor_tensor(
                    t_yf[:, sl], t_yf[:, sl], p_rp[tt_i % 2][:, :], ALU.mult))
                S.drain("vector")
                S.op("vector", lambda e, sl=sl, blk=blk: e.tensor_scalar(
                    t_yf[:, sl], t_yf[:, sl], t_nw[:, blk:blk + 1], None, ALU.mult))
                S.drain("vector")
            rep_done[tt_i] = S.sig("vector", "vd")
        yn_vd[NB - 1] = S.count("vd")

        ev_vd = {}
        nmm = 0
        for do in range(16):
            sw = do % 2
            if do >= 2:
                S.wait("gpsimd", "tp", mm_tp[do - 2])
            S.dma("gpsimd", t_w[:, sw * NB * 128:(sw + 1) * NB * 128]
                  .rearrange("p (a j) -> p a j", j=128),
                  wo_in[:, do * 128:(do + 1) * 128].rearrange("(a p) j -> p a j", p=128),
                  sem=("lwa" if sw == 0 else "lwb"))
            if do == 0:
                S.wait("sync", "tp", sq_tp[NB - 1])
            if do >= 1:
                S.wait("sync", "vd", ev_vd[do - 1])
            S.dma("sync", t_x0[:], x0_in[do * 128:(do + 1) * 128, :], sem="ld")
            S.wait("tensor", "lwa" if sw == 0 else "lwb", 16 * (do // 2 + 1))
            S.wait("tensor", "vd", yn_vd[NB - 1])
            if do >= 1:
                S.wait("vector", "st", st_after_c[do - 1])
            for tt_i in range(4):
                pm = p_mm[nmm % 2]
                if nmm >= 2:
                    S.wait("tensor", "vd", mm_vd[nmm - 2])
                for k in range(NB):
                    S.op("tensor", lambda e, pm=pm, sw=sw, k=k, tt_i=tt_i:
                         nc.tensor.matmul(pm[:, :],
                                          t_w[:, sw * NB * 128 + k * 128:(sw) * NB * 128 + (k + 1) * 128],
                                          t_yf[:, k * TH + tt_i * 512: k * TH + tt_i * 512 + 512],
                                          start=(k == 0), stop=(k == NB - 1)))
                S.sig("tensor", "tp")
                S.wait("vector", "tp")
                S.wait("vector", "ld")
                S.op("vector", lambda e, pm=pm, tt_i=tt_i: e.tensor_tensor(
                    t_ev[:, tt_i * 512:(tt_i + 1) * 512], pm[:, :], t_x0[:, tt_i * 512:(tt_i + 1) * 512], ALU.add))
                v = S.sig("vector", "vd")
                if nmm == 0:
                    mm_vd = {}
                mm_vd[nmm] = v
                nmm += 1
            ev_vd[do] = S.count("vd")
            if do == 0:
                mm_tp = {}
            mm_tp[do] = S.count("tp")
            S.wait("gpsimd", "vd", ev_vd[do])
            if do == 0:
                st_after_c = {}
            st_after_c[do] = S.dma("gpsimd", r0_out[do * 128:(do + 1) * 128, :],
                                   t_ev[:], sem="st")

        S.wait("gpsimd", "st")
        S.build()
    return nc


def kernel(**inputs):
    raise NotImplementedError("under construction")


def build_D(NTG=16):
    nc = make_nc()
    NSL = 2048
    xnT_in = nc.dram_tensor("xnT", [D_MODEL, 16384], BF16, kind="ExternalInput").ap()
    gpk_in = nc.dram_tensor("gpk", [NSL, 1024], I32, kind="ExternalInput").ap()
    gsc_in = nc.dram_tensor("gsc", [NSL, 64], F32, kind="ExternalInput").ap()
    upk_in = nc.dram_tensor("upk", [NSL, 1024], I32, kind="ExternalInput").ap()
    usc_in = nc.dram_tensor("usc", [NSL, 64], F32, kind="ExternalInput").ap()
    dpk_in = nc.dram_tensor("dpk", [D_MODEL, 1024], I32, kind="ExternalInput").ap()
    dsc_in = nc.dram_tensor("dsc", [D_MODEL, 64], F32, kind="ExternalInput").ap()
    idb_in = nc.dram_tensor("idb", [128, 128], BF16, kind="ExternalInput").ap()
    part_out = nc.dram_tensor("part", [D_MODEL, NTG * 1024], F32, kind="ExternalOutput").ap()
    wgt_s = nc.dram_tensor("wgt_s", [D_MODEL, NSL], BF16, kind="Internal").ap()
    wut_s = nc.dram_tensor("wut_s", [D_MODEL, NSL], BF16, kind="Internal").ap()
    wdt_s = nc.dram_tensor("wdt_s", [NSL, D_MODEL], BF16, kind="Internal").ap()

    C1 = 0.044715
    C2 = 0.7978845608028654
    ctx, S, sbuf, psum = sched_ctx(nc)
    with ctx:
        t_pk = sbuf("t_pk", [128, 2 * 1024], I32)
        t_sc = sbuf("t_sc", [128, 64])
        t_w1 = sbuf("t_w1", [128, 1024], I32)
        t_w2 = sbuf("t_w2", [128, 1024], I32)
        t_w3 = sbuf("t_w3", [128, 1024], I32)
        t_dq = [sbuf(f"t_dq{i}", [128, D_MODEL], BF16) for i in range(2)]
        t_id = sbuf("t_id", [128, 128], BF16)
        t_tr = [sbuf(f"t_tr{i}", [128, 128], BF16) for i in range(2)]
        t_xn = sbuf("t_xn", [128, 16 * 1024], BF16)
        t_it = sbuf("t_it", [128, 16 * 1024], BF16)
        t_ws = [sbuf(f"t_ws{i}", [128, 16 * 128], BF16) for i in range(4)]
        t_g = sbuf("t_g", [128, 512])
        t_s1 = sbuf("t_s1", [128, 512])
        t_s3 = sbuf("t_s3", [128, 512])
        t_ev = sbuf("t_ev", [128, 2 * 512])
        p_t = [psum(f"p_t{i}", [128, 1024], BF16) for i in range(2)]
        p_g = [psum(f"p_g{i}", [128, 512]) for i in range(2)]
        p_u = [psum(f"p_u{i}", [128, 512]) for i in range(2)]
        p_d = [psum(f"p_d{i}", [128, 512]) for i in range(2)]

        for s in ("ld", "lw0", "lw1", "lw2", "lw3", "lx", "vd", "sa", "tp", "st", "sc_"):
            S.sem(s)

        S.dma("sync", t_id[:], idb_in, sem="ld")

        def decode(pk_in, sc_in, scr, scale_mult, tag, _g={"kt": 0, "vd": {}}):
            for rt in range(16):
                sl = rt % 2
                pkv = t_pk[:, sl * 1024:(sl + 1) * 1024]
                S.wait("sync", "vd")
                S.dma("sync", pkv, pk_in[rt * 128:(rt + 1) * 128, :], sem="ld")
                S.dma("sync", t_sc[:], sc_in[rt * 128:(rt + 1) * 128, :], sem="ld")
                S.wait("vector", "ld")
                dq = t_dq[rt % 2]
                if rt >= 2:
                    S.wait("vector", "tp")
                S.op("vector", lambda e: e.tensor_scalar(
                    t_sc[:], t_sc[:], scale_mult, None, ALU.mult))
                S.op("vector", lambda e, pkv=pkv: e.tensor_scalar(
                    t_w1[:], pkv, 7, 22, ALU.bitwise_and, ALU.logical_shift_left))
                S.op("vector", lambda e, pkv=pkv: e.tensor_scalar(
                    t_w2[:], pkv, 8, 28, ALU.bitwise_and, ALU.logical_shift_left))
                S.drain("vector")
                S.op("vector", lambda e: e.tensor_tensor(
                    t_w1[:], t_w1[:], t_w2[:], ALU.bitwise_or))
                S.drain("vector")
                S.op("vector", lambda e, dq=dq: e.tensor_tensor(
                    dq[:, 0:D_MODEL:2].rearrange("p (a b) -> p a b", a=64),
                    t_w1[:].bitcast(F32).rearrange("p (a b) -> p a b", a=64),
                    t_sc[:, :, None].broadcast_to([128, 64, 16]), ALU.mult))
                S.op("vector", lambda e, pkv=pkv: e.tensor_scalar(
                    t_w3[:], pkv, 4, 15, ALU.logical_shift_right, ALU.bitwise_and))
                S.drain("vector")
                S.op("vector", lambda e: e.tensor_scalar(
                    t_w1[:], t_w3[:], 7, 22, ALU.bitwise_and, ALU.logical_shift_left))
                S.op("vector", lambda e: e.tensor_scalar(
                    t_w2[:], t_w3[:], 8, 28, ALU.bitwise_and, ALU.logical_shift_left))
                S.drain("vector")
                S.op("vector", lambda e: e.tensor_tensor(
                    t_w1[:], t_w1[:], t_w2[:], ALU.bitwise_or))
                S.drain("vector")
                S.op("vector", lambda e, dq=dq: e.tensor_tensor(
                    dq[:, 1:D_MODEL:2].rearrange("p (a b) -> p a b", a=64),
                    t_w1[:].bitcast(F32).rearrange("p (a b) -> p a b", a=64),
                    t_sc[:, :, None].broadcast_to([128, 64, 16]), ALU.mult))
                S.sig("vector", "vd")
                S.wait("tensor", "vd")
                for kt in range(16):
                    gkt = _g["kt"]; _g["kt"] += 1
                    pt = p_t[gkt % 2]
                    if gkt >= 2:
                        S.wait("tensor", "vd", _g["vd"][gkt - 2])
                    S.op("tensor", lambda e, dq=dq, kt=kt, pt=pt: nc.tensor.transpose(
                        pt[:, 0:128], dq[:, kt * 128:(kt + 1) * 128], t_id[:]))
                    S.sig("tensor", "tp")
                    S.wait("vector", "tp")
                    trb = t_tr[gkt % 2]
                    if gkt >= 2:
                        S.wait("vector", "st")
                    S.op("vector", lambda e, trb=trb, pt=pt: e.tensor_copy(trb[:], pt[:, 0:128]))
                    v = S.sig("vector", "vd")
                    _g["vd"][gkt] = v
                    S.wait("gpsimd", "vd", v)
                    S.dma("gpsimd", scr[kt * 128:(kt + 1) * 128, rt * 128:(rt + 1) * 128],
                          trb[:], sem="st")
                S.sig("vector", "vd")

        decode(gpk_in, gsc_in, wgt_s, float(2.0 ** 126), "g")
        decode(upk_in, usc_in, wut_s, float(2.0 ** 126), "u")
        decode(dpk_in, dsc_in, wdt_s, float(0.5 * 2.0 ** 126), "d")
        ph1_st = S.count("st")
        ph1_vd = S.count("vd")

        S.wait("sync", "st", ph1_st)
        S.wait("scalar", "st", ph1_st)
        rec = {"it": {}, "gu": {}, "pd": {}, "ev": {}}
        ws_done = {}
        nws = 0
        nmm = 0
        for tg in range(NTG):
            if tg >= 1:
                S.wait("sync", "tp", rec["gu"][tg - 1])
            S.dma("sync", t_xn[:].rearrange("p (a t) -> p a t", t=1024),
                  xnT_in[:, tg * 1024:(tg + 1) * 1024].rearrange("(a p) t -> p a t", p=128),
                  sem="lx")
            for ntile in range(16):
                ig = nws; wsg = t_ws[nws % 4]; nws += 1
                iu = nws; wsu = t_ws[nws % 4]; nws += 1
                gsem = f"lw{ig % 4}"
                usem = f"lw{iu % 4}"
                if ig - 4 in ws_done:
                    S.wait("scalar", "tp", ws_done[ig - 4])
                if iu - 4 in ws_done:
                    S.wait("scalar", "tp", ws_done[iu - 4])
                S.dma("scalar", wsg[:].rearrange("p (a n) -> p a n", n=128),
                      wgt_s[:, ntile * 128:(ntile + 1) * 128].rearrange("(a p) n -> p a n", p=128),
                      sem=gsem)
                S.dma("scalar", wsu[:].rearrange("p (a n) -> p a n", n=128),
                      wut_s[:, ntile * 128:(ntile + 1) * 128].rearrange("(a p) n -> p a n", p=128),
                      sem=usem)
                S.wait("tensor", gsem)
                S.wait("tensor", usem)
                S.wait("tensor", "lx")
                for tt2 in range(2):
                    pg = p_g[tt2]
                    pu = p_u[tt2]
                    if nmm >= 1:
                        S.wait("tensor", "vd", rec["ev"].get(nmm - 1, 0))
                    for k in range(16):
                        S.op("tensor", lambda e, pg=pg, wsg=wsg, k=k, tt2=tt2:
                             nc.tensor.matmul(pg[:, :], wsg[:, k * 128:(k + 1) * 128],
                                              t_xn[:, k * 1024 + tt2 * 512: k * 1024 + tt2 * 512 + 512],
                                              start=(k == 0), stop=(k == 15)))
                    for k in range(16):
                        S.op("tensor", lambda e, pu=pu, wsu=wsu, k=k, tt2=tt2:
                             nc.tensor.matmul(pu[:, :], wsu[:, k * 128:(k + 1) * 128],
                                              t_xn[:, k * 1024 + tt2 * 512: k * 1024 + tt2 * 512 + 512],
                                              start=(k == 0), stop=(k == 15)))
                    S.sig("tensor", "tp")
                    S.wait("scalar", "tp")
                    S.op("scalar", lambda e, pg=pg: e.copy(t_g[:], pg[:, :]))
                    S.drain("scalar")
                    S.op("scalar", lambda e: e.activation(t_s1[:], t_g[:], AF.Square))
                    S.sig("scalar", "sa")
                    S.wait("vector", "sa")
                    S.op("vector", lambda e: e.tensor_scalar(
                        t_s1[:], t_s1[:], C1, 1.0, ALU.mult, ALU.add))
                    S.drain("vector")
                    S.op("vector", lambda e: e.tensor_tensor(t_s3[:], t_s1[:], t_g[:], ALU.mult))
                    S.sig("vector", "vd")
                    S.wait("scalar", "vd")
                    S.op("scalar", lambda e: e.activation(t_s3[:], t_s3[:], AF.Tanh, scale=C2))
                    S.sig("scalar", "sa")
                    S.wait("vector", "sa")
                    S.op("vector", lambda e, pu=pu: e.tensor_tensor(t_s1[:], t_g[:], pu[:, :], ALU.mult))
                    S.drain("vector")
                    S.op("vector", lambda e, ntile=ntile, tt2=tt2: e.scalar_tensor_tensor(
                        t_it[:, ntile * 1024 + tt2 * 512: ntile * 1024 + tt2 * 512 + 512],
                        t_s3[:], 1.0, t_s1[:], ALU.add, ALU.mult))
                    v = S.sig("vector", "vd")
                    rec["ev"][nmm] = v
                    nmm += 1
                ws_done[ig] = ws_done[iu] = S.count("tp")
            rec["gu"][tg] = S.count("tp")
            it_vd = S.count("vd")

            S.wait("tensor", "vd", it_vd)
            for htile in range(16):
                idd = nws; wsd = t_ws[nws % 4]; nws += 1
                dsem = f"lw{idd % 4}"
                if idd - 4 in ws_done:
                    S.wait("scalar", "tp", ws_done[idd - 4])
                S.dma("scalar", wsd[:].rearrange("p (a h) -> p a h", h=128),
                      wdt_s[:, htile * 128:(htile + 1) * 128].rearrange("(a p) h -> p a h", p=128),
                      sem=dsem)
                S.wait("tensor", dsem)
                for tt2 in range(2):
                    pd = p_d[tt2]
                    if (htile, tt2) != (0, 0):
                        S.wait("tensor", "vd")
                    for k in range(16):
                        S.op("tensor", lambda e, pd=pd, wsd=wsd, k=k, tt2=tt2:
                             nc.tensor.matmul(pd[:, :], wsd[:, k * 128:(k + 1) * 128],
                                              t_it[:, k * 1024 + tt2 * 512: k * 1024 + tt2 * 512 + 512],
                                              start=(k == 0), stop=(k == 15)))
                    S.sig("tensor", "tp")
                    S.wait("vector", "tp")
                    if (htile, tt2) not in ((0, 0), (0, 1)):
                        S.wait("vector", "st")
                    S.op("vector", lambda e, pd=pd, tt2=tt2: e.tensor_copy(
                        t_ev[:, tt2 * 512:(tt2 + 1) * 512], pd[:, :]))
                    S.sig("vector", "vd")
                    S.wait("gpsimd", "vd")
                    S.dma("gpsimd", part_out[htile * 128:(htile + 1) * 128,
                                             tg * 1024 + tt2 * 512: tg * 1024 + tt2 * 512 + 512],
                          t_ev[:, tt2 * 512:(tt2 + 1) * 512], sem="st")
                ws_done[idd] = S.count("tp")

        S.wait("gpsimd", "st")
        S.build()
    return nc


def shard_D(xnT_full, inputs):
    gpk = np.asarray(inputs["L0_ffn_gate_W_packed"], np.int32)
    gsc = np.asarray(inputs["L0_ffn_gate_scales"], np.float32)
    upk = np.asarray(inputs["L0_ffn_up_W_packed"], np.int32)
    usc = np.asarray(inputs["L0_ffn_up_scales"], np.float32)
    dpk = np.asarray(inputs["L0_ffn_down_W_packed"], np.int32)
    dsc = np.asarray(inputs["L0_ffn_down_scales"], np.float32)
    idb = id128_bf()
    maps = []
    for c in range(NCORES):
        maps.append(dict(
            xnT=xnT_full,
            gpk=np.ascontiguousarray(gpk[c * 2048:(c + 1) * 2048]),
            gsc=np.ascontiguousarray(gsc[c * 2048:(c + 1) * 2048]),
            upk=np.ascontiguousarray(upk[c * 2048:(c + 1) * 2048]),
            usc=np.ascontiguousarray(usc[c * 2048:(c + 1) * 2048]),
            dpk=np.ascontiguousarray(dpk[:, c * 1024:(c + 1) * 1024]),
            dsc=np.ascontiguousarray(dsc[:, c * 64:(c + 1) * 64]),
            idb=idb))
    return maps




def build_E():
    nc = make_nc()
    r0_in = nc.dram_tensor("r0", [D_MODEL, TH], F32, kind="ExternalInput").ap()
    parts_in = [nc.dram_tensor(f"pp{i}", [D_MODEL, TH], F32, kind="ExternalInput").ap()
                for i in range(8)]
    wbcT_in = nc.dram_tensor("wbcT", [D_MODEL, 128], F32, kind="ExternalInput").ap()
    wdtT_in = nc.dram_tensor("wdtT", [DT_RANK, D_MODEL], F32, kind="ExternalInput").ap()
    bdt_in = nc.dram_tensor("bdt", [128, 16], F32, kind="ExternalInput").ap()
    rw_in = nc.dram_tensor("rw", [D_MODEL, NEXP], F32, kind="ExternalInput").ap()
    rb_in = nc.dram_tensor("rb", [NEXP, 1], F32, kind="ExternalInput").ap()
    idf_in = nc.dram_tensor("idf", [128, 128], F32, kind="ExternalInput").ap()
    x1_out = nc.dram_tensor("x1", [D_MODEL, TH], F32, kind="ExternalOutput").ap()
    xn1_out = nc.dram_tensor("xn1", [D_MODEL, TH], F32, kind="ExternalOutput").ap()
    dl_out = nc.dram_tensor("dl", [D_MODEL, TH], F32, kind="ExternalOutput").ap()
    bc_out = nc.dram_tensor("bc", [64, TH], F32, kind="ExternalOutput").ap()
    s_out = nc.dram_tensor("s", [TH, NEXP], F32, kind="ExternalOutput").ap()

    ctx, S, sbuf, psum = sched_ctx(nc)
    with ctx:
        t_a = sbuf("t_a", [128, TH])
        t_b = sbuf("t_b", [128, TH])
        t_sqs = [sbuf(f"t_sqs{i}", [128, TH]) for i in range(2)]
        t_arep = sbuf("t_arep", [128, TH])
        t_brep = sbuf("t_brep", [128, TH])
        t_row = sbuf("t_row", [1, 2 * TH])
        t_one = sbuf("t_one", [1, 128])
        t_onec = sbuf("t_onec", [128, 1])
        t_xnb = sbuf("t_xnb", [128, 16 * TH], BF16)
        t_wbc = sbuf("t_wbc", [128, 16 * 128], BF16)
        t_wdt = sbuf("t_wdt", [DT_RANK, 16 * 128], BF16)
        t_dtr = sbuf("t_dtr", [DT_RANK, TH], BF16)
        t_bdt = sbuf("t_bdt", [128, 16])
        t_rw = sbuf("t_rw", [128, 16 * NEXP], BF16)
        t_rb = sbuf("t_rb", [NEXP, 1])
        t_id = sbuf("t_id", [128, 128])
        t_sp1 = sbuf("t_sp1", [128, 512])
        t_sp2 = sbuf("t_sp2", [128, 512])
        t_sm = sbuf("t_sm", [128, 16 * 16])
        t_mx = sbuf("t_mx", [128, 2])
        p_s1 = psum("p_s1", [1, 512])
        p_s2 = psum("p_s2", [1, 512])
        p_a = [psum(f"p_a{i}", [128, 512]) for i in range(2)]
        p_b = [psum(f"p_b{i}", [128, 512]) for i in range(2)]

        for s in ("ld", "lp", "vd", "sa", "tp", "st"):
            S.sem(s)

        S.dma("sync", t_id[:], idf_in, sem="ld")
        S.dma("sync", t_bdt[:], bdt_in, sem="ld")
        S.dma("sync", t_rb[:], rb_in, sem="ld")
        S.dma("gpsimd", t_wbc[:].rearrange("p (a j) -> p a j", j=128),
              wbcT_in.rearrange("(a p) j -> p a j", p=128), sem="lp")
        S.dma("gpsimd", t_wdt[:].rearrange("p (a j) -> p a j", j=128),
              wdtT_in.rearrange("p (a j) -> p a j", j=128), sem="lp")
        S.dma("gpsimd", t_rw[:].rearrange("p (a j) -> p a j", j=NEXP),
              rw_in.rearrange("(a p) j -> p a j", p=128), sem="lp")
        S.op("vector", lambda e: e.memset(t_one[:], 1.0))
        S.op("vector", lambda e: e.memset(t_onec[:], 1.0))
        S.op("vector", lambda e: e.memset(t_row[:], 0.0))
        S.sig("vector", "vd")
        S.wait("tensor", "vd")

        st_p1 = {}
        for blk in range(16):
            if blk >= 1:
                S.wait("sync", "tp", st_p1[blk - 1])
                S.wait("sync", "st")
            S.dma("sync", t_a[:], r0_in[blk * 128:(blk + 1) * 128, :], sem="ld")
            for i in range(8):
                S.dma("sync", t_b[:], parts_in[i][blk * 128:(blk + 1) * 128, :], sem="ld")
                S.wait("vector", "ld")
                S.op("vector", lambda e: e.tensor_tensor(t_a[:], t_a[:], t_b[:], ALU.add))
                S.sig("vector", "vd")
                S.wait("sync", "vd")
            S.wait("gpsimd", "vd")
            S.dma("gpsimd", x1_out[blk * 128:(blk + 1) * 128, :], t_a[:], sem="st")
            sqb = t_sqs[blk % 2]
            if blk >= 2:
                S.wait("scalar", "tp", st_p1[blk - 2])
            S.wait("scalar", "vd")
            S.op("scalar", lambda e, sqb=sqb: e.activation(sqb[:], t_a[:], AF.Square))
            S.sig("scalar", "sa")
            S.wait("tensor", "sa")
            for tt_i in range(4):
                if blk > 0 or tt_i > 0:
                    S.wait("tensor", "vd")
                S.op("tensor", lambda e, tt_i=tt_i: nc.tensor.matmul(
                    p_s1[:, :], t_onec[:, :], t_a[:, tt_i * 512:(tt_i + 1) * 512],
                    start=True, stop=True))
                S.op("tensor", lambda e, sqb=sqb, tt_i=tt_i: nc.tensor.matmul(
                    p_s2[:, :], t_onec[:, :], sqb[:, tt_i * 512:(tt_i + 1) * 512],
                    start=True, stop=True))
                S.sig("tensor", "tp")
                S.wait("vector", "tp")
                sl2 = slice(TH + tt_i * 512, TH + (tt_i + 1) * 512)
                sl = slice(tt_i * 512, (tt_i + 1) * 512)
                S.op("vector", lambda e, sl2=sl2: e.tensor_tensor(
                    t_row[0:1, sl2], t_row[0:1, sl2], p_s1[:, :], ALU.add))
                S.op("vector", lambda e, sl=sl: e.tensor_tensor(
                    t_row[0:1, sl], t_row[0:1, sl], p_s2[:, :], ALU.add))
                S.sig("vector", "vd")
            S.sig("tensor", "tp")
            st_p1[blk] = S.count("tp")

        S.drain("vector")
        S.op("vector", lambda e: e.tensor_scalar(
            t_row[0:1, 0:2 * TH], t_row[0:1, 0:2 * TH], 1.0 / D_MODEL, None, ALU.mult))
        S.drain("vector")
        S.op("vector", lambda e: e.tensor_tensor(
            t_b[0:1, 0:TH], t_row[0:1, TH:2 * TH], t_row[0:1, TH:2 * TH], ALU.mult))
        S.drain("vector")
        S.op("vector", lambda e: e.tensor_tensor(
            t_row[0:1, 0:TH], t_row[0:1, 0:TH], t_b[0:1, 0:TH], ALU.subtract))
        S.drain("vector")
        S.op("vector", lambda e: e.tensor_scalar(
            t_row[0:1, 0:TH], t_row[0:1, 0:TH], 1.0, 1e-5, ALU.mult, ALU.add))
        S.drain("vector")
        S.op("vector", lambda e: e.reciprocal(t_row[0:1, 0:TH], t_row[0:1, 0:TH]))
        S.sig("vector", "vd")
        S.wait("scalar", "vd")
        S.op("scalar", lambda e: e.activation(t_row[0:1, 0:TH], t_row[0:1, 0:TH], AF.Sqrt))
        S.sig("scalar", "sa")
        S.wait("vector", "sa")
        S.op("vector", lambda e: e.tensor_tensor(
            t_row[0:1, TH:2 * TH], t_row[0:1, TH:2 * TH], t_row[0:1, 0:TH], ALU.mult))
        S.drain("vector")
        S.op("vector", lambda e: e.tensor_scalar(
            t_row[0:1, TH:2 * TH], t_row[0:1, TH:2 * TH], -1.0, None, ALU.mult))
        S.sig("vector", "vd")
        rows_vd = S.count("vd")

        S.wait("tensor", "vd", rows_vd)
        rep_tp = {}
        for tt_i in range(4):
            if tt_i >= 2:
                S.wait("tensor", "vd", rep_ev[tt_i - 2])
            S.op("tensor", lambda e, tt_i=tt_i: nc.tensor.matmul(
                p_a[tt_i % 2][:, :], t_one[:, :], t_row[0:1, tt_i * 512:(tt_i + 1) * 512],
                start=True, stop=True))
            S.op("tensor", lambda e, tt_i=tt_i: nc.tensor.matmul(
                p_b[tt_i % 2][:, :], t_one[:, :], t_row[0:1, TH + tt_i * 512:TH + (tt_i + 1) * 512],
                start=True, stop=True))
            S.sig("tensor", "tp")
            rep_tp[tt_i] = S.count("tp")
            S.wait("vector", "tp", rep_tp[tt_i])
            S.op("vector", lambda e, tt_i=tt_i: e.tensor_copy(
                t_arep[:, tt_i * 512:(tt_i + 1) * 512], p_a[tt_i % 2][:, :]))
            S.op("vector", lambda e, tt_i=tt_i: e.tensor_copy(
                t_brep[:, tt_i * 512:(tt_i + 1) * 512], p_b[tt_i % 2][:, :]))
            v = S.sig("vector", "vd")
            if tt_i == 0:
                rep_ev = {}
            rep_ev[tt_i] = v
        rep_all = S.count("vd")
        xn_vd = {}
        for blk in range(16):
            if blk == 0:
                S.wait("sync", "tp", st_p1[15])
                S.wait("sync", "st")
            if blk >= 1:
                S.wait("sync", "vd", xn_vd[blk - 1])
            S.dma("sync", t_a[:], x1_out[blk * 128:(blk + 1) * 128, :], sem="ld")
            S.wait("vector", "ld")
            S.wait("vector", "vd", rep_all)
            S.wait("vector", "st")
            S.op("vector", lambda e: e.tensor_tensor(t_b[:], t_a[:], t_arep[:], ALU.mult))
            S.drain("vector")
            S.op("vector", lambda e: e.tensor_tensor(t_b[:], t_b[:], t_brep[:], ALU.add))
            S.drain("vector")
            S.op("vector", lambda e, blk=blk: e.tensor_copy(
                t_xnb[:, blk * TH:(blk + 1) * TH], t_b[:]))
            xn_vd[blk] = S.sig("vector", "vd")
            S.wait("gpsimd", "vd", xn_vd[blk])
            S.dma("gpsimd", xn1_out[blk * 128:(blk + 1) * 128, :], t_b[:], sem="st")
        xn_all = S.count("vd")

        S.wait("tensor", "vd", xn_all)
        S.wait("tensor", "lp")
        dbc_tp = {}
        for tt_i in range(4):
            pm = p_a[tt_i % 2]
            if tt_i >= 2:
                S.wait("tensor", "vd", dbc_vd[tt_i - 2])
            for k in range(16):
                S.op("tensor", lambda e, pm=pm, k=k, tt_i=tt_i: nc.tensor.matmul(
                    pm[:, :], t_wbc[:, k * 128:(k + 1) * 128],
                    t_xnb[:, k * TH + tt_i * 512: k * TH + tt_i * 512 + 512],
                    start=(k == 0), stop=(k == 15)))
            S.sig("tensor", "tp")
            S.wait("vector", "tp")
            sl = slice(tt_i * 512, (tt_i + 1) * 512)
            S.op("vector", lambda e, pm=pm, sl=sl: e.tensor_copy(t_dtr[:, sl], pm[0:64, :]))
            S.op("vector", lambda e, pm=pm, sl=sl: e.tensor_copy(t_sqs[0][0:64, sl], pm[64:128, :]))
            v = S.sig("vector", "vd")
            if tt_i == 0:
                dbc_vd = {}
            dbc_vd[tt_i] = v
        S.wait("gpsimd", "vd", S.count("vd"))
        S.dma("gpsimd", bc_out[:, :], t_sqs[0][0:64, 0:TH], sem="st")

        dtr_all = S.count("vd")
        S.wait("tensor", "vd", dtr_all)
        for dtile in range(16):
            for tt_i in range(4):
                pm = p_b[(dtile * 4 + tt_i) % 2]
                if dtile * 4 + tt_i >= 2:
                    S.wait("tensor", "vd", dt_vd[dtile * 4 + tt_i - 2])
                S.op("tensor", lambda e, pm=pm, dtile=dtile, tt_i=tt_i: nc.tensor.matmul(
                    pm[:, :], t_wdt[:, dtile * 128:(dtile + 1) * 128],
                    t_dtr[:, tt_i * 512:(tt_i + 1) * 512],
                    start=True, stop=True))
                S.sig("tensor", "tp")
                S.wait("vector", "tp")
                S.op("vector", lambda e, pm=pm, dtile=dtile: e.tensor_scalar(
                    t_sp1[:, :], pm[:, :], t_bdt[:, dtile:dtile + 1], None, ALU.add))
                v = S.sig("vector", "vd")
                if dtile == 0 and tt_i == 0:
                    dt_vd = {}
                dt_vd[dtile * 4 + tt_i] = v
                S.wait("scalar", "vd", v)
                S.op("scalar", lambda e: e.activation(t_sp2[:, :], t_sp1[:, :], AF.Abs))
                S.drain("scalar")
                S.op("scalar", lambda e: e.activation(t_sp2[:, :], t_sp2[:, :], AF.Exp, scale=-1.0))
                S.drain("scalar")
                S.op("scalar", lambda e: e.activation(t_sp2[:, :], t_sp2[:, :], AF.Ln, bias=1.0))
                S.drain("scalar")
                S.op("scalar", lambda e: e.activation(t_sp1[:, :], t_sp1[:, :], AF.Relu))
                S.sig("scalar", "sa")
                S.wait("vector", "sa")
                S.op("vector", lambda e: e.tensor_tensor(t_sp1[:, :], t_sp1[:, :], t_sp2[:, :], ALU.add))
                v2 = S.sig("vector", "vd")
                S.wait("gpsimd", "vd", v2)
                S.dma("gpsimd", dl_out[dtile * 128:(dtile + 1) * 128,
                                       tt_i * 512:(tt_i + 1) * 512], t_sp1[:, :], sem="st")
                S.wait("vector", "st")
                S.wait("scalar", "st")

        S.wait("tensor", "vd", S.count("vd"))
        for tt_i in range(4):
            pm = p_a[tt_i % 2]
            if tt_i >= 2:
                S.wait("tensor", "vd", rt_vd[tt_i - 2])
            for k in range(16):
                S.op("tensor", lambda e, pm=pm, k=k, tt_i=tt_i: nc.tensor.matmul(
                    pm[0:NEXP, :], t_rw[:, k * NEXP:(k + 1) * NEXP],
                    t_xnb[:, k * TH + tt_i * 512: k * TH + tt_i * 512 + 512],
                    start=(k == 0), stop=(k == 15)))
            S.sig("tensor", "tp")
            S.wait("vector", "tp")
            S.op("vector", lambda e, pm=pm: e.tensor_scalar(
                t_sp1[0:NEXP, :], pm[0:NEXP, :], t_rb[:, 0:1], None, ALU.add))
            v = S.sig("vector", "vd")
            if tt_i == 0:
                rt_vd = {}
            rt_vd[tt_i] = v
            for q in range(4):
                gq = tt_i * 4 + q
                S.wait("tensor", "vd", v)
                if gq >= 2:
                    S.wait("tensor", "vd", rtr_ev[gq - 2])
                S.op("tensor", lambda e, q=q, tt_i=tt_i, gq=gq: nc.tensor.transpose(
                    p_b[gq % 2][0:128, 0:NEXP], t_sp1[0:NEXP, q * 128:(q + 1) * 128],
                    t_id[0:NEXP, 0:NEXP]))
                S.sig("tensor", "tp")
                S.wait("vector", "tp")
                ti = tt_i * 4 + q
                S.op("vector", lambda e, gq=gq, ti=ti: e.tensor_copy(
                    t_sm[:, ti * 16:(ti + 1) * 16], p_b[gq % 2][0:128, 0:NEXP]))
                ve = S.sig("vector", "vd")
                if gq == 0:
                    rtr_ev = {}
                rtr_ev[gq] = ve
        for ti in range(16):
            sm = t_sm[:, ti * 16:(ti + 1) * 16]
            S.drain("vector")
            S.op("vector", lambda e, sm=sm: e.reduce_max(t_mx[:, 0:1], sm, axis=AX.X))
            S.drain("vector")
            S.op("vector", lambda e, sm=sm: e.tensor_scalar(
                sm, sm, t_mx[:, 0:1], None, ALU.subtract))
            S.sig("vector", "vd")
            S.wait("scalar", "vd")
            S.op("scalar", lambda e, sm=sm: e.activation(sm, sm, AF.Exp))
            S.sig("scalar", "sa")
            S.wait("vector", "sa")
            S.op("vector", lambda e, sm=sm: e.reduce_sum(t_mx[:, 1:2], sm, axis=AX.X))
            S.drain("vector")
            S.op("vector", lambda e: e.reciprocal(t_mx[:, 1:2], t_mx[:, 1:2]))
            S.drain("vector")
            S.op("vector", lambda e, sm=sm: e.tensor_scalar(
                sm, sm, t_mx[:, 1:2], None, ALU.mult))
            v = S.sig("vector", "vd")
            S.wait("gpsimd", "vd", v)
            S.dma("gpsimd", s_out[ti * 128:(ti + 1) * 128, :], sm, sem="st")

        S.wait("gpsimd", "st")
        S.build()
    return nc


def build_F(LQ=L_SEQ, BQ=B_SZ):
    nc = make_nc()
    DL = 256
    NBLK = DL // 4
    dT_in = nc.dram_tensor("dT", [BQ, DL, LQ], F32, kind="ExternalInput").ap()
    uT_in = nc.dram_tensor("uT", [BQ, DL, LQ], F32, kind="ExternalInput").ap()
    BT_in = nc.dram_tensor("BTf", [BQ, L1_N, LQ], F32, kind="ExternalInput").ap()
    CT_in = nc.dram_tensor("CTf", [BQ, L1_N, LQ], F32, kind="ExternalInput").ap()
    adn_in = nc.dram_tensor("adn", [128, NBLK], F32, kind="ExternalInput").ap()
    dcol_in = nc.dram_tensor("dcol", [128, 2], F32, kind="ExternalInput").ap()
    G_in = nc.dram_tensor("G", [128, 32 * 128], BF16, kind="ExternalInput").ap()
    y_out = nc.dram_tensor("y", [BQ, DL, LQ], F32, kind="ExternalOutput").ap()
    du_scr = nc.dram_tensor("du_scr", [DL, LQ], F32, kind="Internal").ap()

    HT = LQ // 2
    NQ = max(1, HT // 512)
    QW = HT // NQ
    ctx, S, sbuf, psum = sched_ctx(nc)
    with ctx:
        t_dc = sbuf("t_dc", [128, 2 * LQ])
        t_uc = sbuf("t_uc", [128, 2 * LQ])
        t_ud = sbuf("t_ud", [128, 2 * LQ])
        t_Br = sbuf("t_Br", [128, LQ])
        t_Cr = sbuf("t_Cr", [128, LQ])
        t_dr = sbuf("t_dr", [128, HT])
        t_ur = sbuf("t_ur", [128, HT])
        t_dA = sbuf("t_dA", [128, HT])
        t_dB = sbuf("t_dB", [128, HT])
        t_h = sbuf("t_h", [128, HT])
        t_ch = sbuf("t_ch", [128, HT], BF16)
        t_cr = sbuf("t_cr", [128, NBLK])
        t_am = sbuf("t_am", [128, NBLK])
        t_dcol = sbuf("t_dcol", [128, 2])
        t_G = sbuf("t_G", [128, 32 * 128], BF16)
        t_y = sbuf("t_y", [128, HT])
        t_z = sbuf("t_z", [128, 1])
        p_y = [psum(f"p_y{i}", [128, 512]) for i in range(4)]

        for s in ("ld", "la", "lb", "lg", "lr", "lu", "vd", "sa", "tp", "st", "du"):
            S.sem(s)

        S.dma("sync", t_am[:], adn_in, sem="la")
        S.dma("sync", t_dcol[:], dcol_in, sem="la")
        S.dma("gpsimd", t_G[:], G_in, sem="lg")
        S.op("vector", lambda e: e.memset(t_z[:], 0.0))
        S.sig("vector", "vd")
        S.wait("scalar", "la", 32)
        S.op("scalar", lambda e: e.activation(t_am[:], t_am[:], AF.Exp))
        S.sig("scalar", "sa")
        S.wait("vector", "sa")
        S.op("vector", lambda e: e.tensor_scalar(t_am[:], t_am[:], -1.0, None, ALU.mult))
        S.sig("vector", "vd")
        am_vd = S.count("vd")

        rec = {}
        nlr = 0
        nlu = 0
        for b in range(BQ):
            if b >= 1:
                S.wait("sync", "vd", rec["comp_done"])
            for i in range(2):
                S.dma("sync", t_dc[:, i * LQ:(i + 1) * LQ],
                      dT_in[b, i * 128:(i + 1) * 128, :], sem="ld")
                S.dma("sync", t_uc[:, i * LQ:(i + 1) * LQ],
                      uT_in[b, i * 128:(i + 1) * 128, :], sem="ld")
            if b >= 1:
                S.wait("sync", "vd", rec["bc_free"])
            S.dma("sync", t_Br[:],
                  BT_in[b].unsqueeze(0).broadcast_to([4, L1_N, LQ]), sem="lb")
            S.dma("sync", t_Cr[:],
                  CT_in[b].unsqueeze(0).broadcast_to([4, L1_N, LQ]), sem="lb")
            S.wait("vector", "ld")
            if b >= 1:
                S.wait("vector", "lu", rec["lu_end"])
            for i in range(2):
                S.op("vector", lambda e, i=i: e.tensor_tensor(
                    t_ud[:, i * LQ:(i + 1) * LQ], t_dc[:, i * LQ:(i + 1) * LQ],
                    t_uc[:, i * LQ:(i + 1) * LQ], ALU.mult))
            S.sig("vector", "vd")
            S.wait("gpsimd", "vd")
            if b >= 1:
                S.wait("gpsimd", "lu", rec["lu_end"])
            for i in range(2):
                S.dma("gpsimd", du_scr[i * 128:(i + 1) * 128, :],
                      t_ud[:, i * LQ:(i + 1) * LQ], sem="du")
            S.wait("vector", "du", 32 * (b + 1))
            S.wait("vector", "la", 32)
            for i in range(2):
                S.op("vector", lambda e, i=i: e.tensor_scalar(
                    t_ud[:, i * LQ:(i + 1) * LQ], t_uc[:, i * LQ:(i + 1) * LQ],
                    t_dcol[:, i:i + 1], None, ALU.mult))
            S.sig("vector", "vd")
            rec["comp_done"] = S.count("vd")

            for ht in range(2):
                t0 = ht * HT
                for j in range(NBLK):
                    first = (b == 0 and ht == 0 and j == 0)
                    if not first:
                        S.wait("sync", "sa", rec["dA_sa"])
                    nlr += 16
                    S.dma("sync", t_dr[:],
                          dT_in[b, 4 * j:4 * j + 4, t0:t0 + HT].unsqueeze(1)
                          .broadcast_to([4, 32, HT]), sem="lr")
                    S.wait("sync", "du", 32 * (b + 1))
                    if not first:
                        S.wait("sync", "vd", rec["dB_done"])
                    nlu += 16
                    S.dma("sync", t_ur[:],
                          du_scr[4 * j:4 * j + 4, t0:t0 + HT].unsqueeze(1)
                          .broadcast_to([4, 32, HT]), sem="lu")
                    S.wait("scalar", "lr", nlr)
                    if first:
                        S.wait("scalar", "vd", am_vd)
                    if not first:
                        S.wait("scalar", "vd", rec["scan_done"])
                    S.op("scalar", lambda e, j=j: e.activation(
                        t_dA[:], t_dr[:], AF.Exp, scale=t_am[:, j:j + 1]))
                    S.sig("scalar", "sa")
                    rec["dA_sa"] = S.count("sa")
                    S.wait("vector", "lu", nlu)
                    S.wait("vector", "lb")
                    if not first:
                        S.wait("vector", "tp", rec["glast"])
                    S.op("vector", lambda e, t0=t0: e.tensor_tensor(
                        t_dB[:], t_ur[:], t_Br[:, t0:t0 + HT], ALU.mult))
                    S.sig("vector", "vd")
                    rec["dB_done"] = S.count("vd")
                    S.wait("vector", "sa", rec["dA_sa"])
                    init = t_z[:, 0:1] if ht == 0 else t_cr[:, j:j + 1]
                    S.op("vector", lambda e, init=init: e.tensor_tensor_scan(
                        t_h[:], t_dA[:], t_dB[:], init, ALU.mult, ALU.add))
                    S.sig("vector", "vd")
                    rec["scan_done"] = S.count("vd")
                    if ht == 0:
                        S.op("vector", lambda e, j=j: e.tensor_copy(
                            t_cr[:, j:j + 1], t_h[:, HT - 1:HT]))
                        S.drain("vector")
                    S.op("vector", lambda e, t0=t0: e.tensor_tensor(
                        t_ch[:], t_h[:], t_Cr[:, t0:t0 + HT], ALU.mult))
                    S.sig("vector", "vd")
                    ch_vd = S.count("vd")
                    S.wait("tensor", "vd", ch_vd)
                    jj = j % 32
                    if first:
                        S.wait("tensor", "lg", 16)
                        S.wait("tensor", "vd", am_vd)
                    if jj == 0 and not first:
                        S.wait("tensor", "vd", rec.get("ylast", 0))
                    for q in range(NQ):
                        S.op("tensor", lambda e, jj=jj, q=q: nc.tensor.matmul(
                            p_y[q][:, 0:QW], t_G[:, jj * 128:(jj + 1) * 128],
                            t_ch[:, q * QW:(q + 1) * QW],
                            start=(jj == 0), stop=(jj == 31), skip_group_check=True))
                    S.sig("tensor", "tp")
                    rec["glast"] = S.count("tp")
                    if jj == 31:
                        grp = j // 32
                        S.wait("vector", "tp", rec["glast"])
                        S.wait("vector", "st")
                        for q in range(NQ):
                            S.op("vector", lambda e, q=q, grp=grp, t0=t0: e.tensor_tensor(
                                t_y[:, q * QW:(q + 1) * QW], p_y[q][:, 0:QW],
                                t_ud[:, grp * LQ + t0 + q * QW: grp * LQ + t0 + q * QW + QW],
                                ALU.add))
                        v = S.sig("vector", "vd")
                        rec["ylast"] = v
                        S.wait("gpsimd", "vd", v)
                        S.dma("gpsimd", y_out[b, grp * 128:(grp + 1) * 128, t0:t0 + HT],
                              t_y[:], sem="st")
            rec["bc_free"] = S.count("vd")
            rec["lu_end"] = nlu

        S.wait("gpsimd", "st")
        S.build()
    return nc


def shard_F(delta_fm, xn1_fm, bc_all, inputs):
    A_log = np.asarray(inputs["L1_attn_A_log"], np.float32)
    D_par = np.asarray(inputs["L1_attn_D_param"], np.float32)
    import ml_dtypes
    G = np.zeros((128, 32, 128), np.float32)
    for jj in range(32):
        for dsub in range(4):
            G[dsub * 32:(dsub + 1) * 32, jj, jj * 4 + dsub] = 1.0
    G = np.ascontiguousarray(G.reshape(128, 32 * 128)).astype(ml_dtypes.bfloat16)
    maps = []
    for c in range(NCORES):
        dsl = slice(c * 256, (c + 1) * 256)
        adn = np.ascontiguousarray(
            A_log[dsl].reshape(64, 4, 32).transpose(1, 2, 0).reshape(128, 64))
        dcol = np.ascontiguousarray(D_par[dsl].reshape(2, 128).T)
        maps.append(dict(
            dT=np.ascontiguousarray(delta_fm[:, dsl, :]),
            uT=np.ascontiguousarray(xn1_fm[:, dsl, :]),
            BTf=np.ascontiguousarray(bc_all[:, 0:32, :]),
            CTf=np.ascontiguousarray(bc_all[:, 32:64, :]),
            adn=adn, dcol=dcol, G=G))
    return maps


def build_G():
    nc = make_nc()
    x1_in = nc.dram_tensor("x1", [D_MODEL, TH], F32, kind="ExternalInput").ap()
    xn1_in = nc.dram_tensor("xn1", [D_MODEL, TH], F32, kind="ExternalInput").ap()
    ym_in = nc.dram_tensor("ym", [D_MODEL, TH], F32, kind="ExternalInput").ap()
    sdev_in = nc.dram_tensor("sdev", [128, NEXP * 128], F32, kind="ExternalInput").ap()
    ssl_in = nc.dram_tensor("ssl", [TH, NEXP], F32, kind="ExternalInput").ap()
    ew_in = nc.dram_tensor("ew", [D_MODEL, D_MODEL], F32, kind="ExternalInput").ap()
    eb_in = nc.dram_tensor("eb", [128, 16], F32, kind="ExternalInput").ap()
    out_o = nc.dram_tensor("out", [D_MODEL, TH], F32, kind="ExternalOutput").ap()
    w_scr = nc.dram_tensor("w_scr", [TH, 1], F32, kind="Internal").ap()

    ctx, S, sbuf, psum = sched_ctx(nc)
    with ctx:
        t_sd = sbuf("t_sd", [128, NEXP * 128])
        t_ss = sbuf("t_ss", [128, 16 * NEXP])
        t_msk = sbuf("t_msk", [128, NEXP * 128], I32)
        t_cnt = sbuf("t_cnt", [128, NEXP], I32)
        t_cntf = sbuf("t_cntf", [128, NEXP])
        t_crow = sbuf("t_crow", [1, NEXP])
        t_lo = sbuf("t_lo", [1, NEXP], I32)
        t_hi = sbuf("t_hi", [1, NEXP], I32)
        t_mid = sbuf("t_mid", [1, NEXP], I32)
        t_sel = sbuf("t_sel", [1, NEXP], I32)
        t_seln = sbuf("t_seln", [1, NEXP], I32)
        t_tau = sbuf("t_tau", [128, NEXP])
        t_one = sbuf("t_one", [1, 128])
        t_onec = sbuf("t_onec", [128, 1])
        t_wc = sbuf("t_wc", [128, 1])
        t_wrow = sbuf("t_wrow", [1, TH])
        t_xnb = sbuf("t_xnb", [128, 16 * TH], BF16)
        t_w = sbuf("t_w", [128, 2 * 16 * 128], BF16)
        t_x1 = sbuf("t_x1", [128, TH])
        t_ym = sbuf("t_ym", [128, TH])
        t_eb = sbuf("t_eb", [128, 16])
        t_ev = sbuf("t_ev", [128, TH])
        t_t1 = sbuf("t_t1", [128, 512])
        p_tau = psum("p_tau", [128, 512])
        p_cnt = psum("p_cnt", [1, 512])
        p_w = [psum(f"p_w{i}", [128, 512]) for i in range(4)]
        p_mm = [psum(f"p_mm{i}", [128, 512]) for i in range(2)]

        for s in ("ld", "lg", "lw0", "lw1", "lwr", "vd", "sa", "tp", "st", "wl"):
            S.sem(s)

        S.dma("sync", t_sd[:], sdev_in, sem="ld")
        S.dma("sync", t_ss[:].rearrange("p (a e) -> p a e", e=NEXP),
              ssl_in.rearrange("(a p) e -> p a e", p=128), sem="ld")
        S.dma("sync", t_eb[:], eb_in, sem="ld")
        S.dma("gpsimd", t_xnb[:].rearrange("p (a t) -> p a t", t=TH),
              xn1_in.rearrange("(a p) t -> p a t", p=128), sem="lg")
        S.op("vector", lambda e: e.memset(t_one[:], 1.0))
        S.op("vector", lambda e: e.memset(t_onec[:], 1.0))
        S.op("vector", lambda e: e.memset(t_lo[:], 0))
        S.op("vector", lambda e: e.memset(t_hi[:], 0x7F800000))
        S.sig("vector", "vd")
        S.wait("tensor", "vd")
        S.wait("vector", "ld", 48)

        for it in range(31):
            S.drain("vector")
            S.op("vector", lambda e: e.tensor_tensor(t_mid[:], t_lo[:], t_hi[:], ALU.add))
            S.drain("vector")
            S.op("vector", lambda e: e.tensor_scalar(
                t_mid[:], t_mid[:], 1, None, ALU.logical_shift_right))
            S.sig("vector", "vd")
            S.wait("tensor", "vd")
            S.op("tensor", lambda e: nc.tensor.matmul(
                p_tau[:, 0:NEXP], t_one[:, :], t_mid[:].bitcast(F32),
                start=True, stop=True))
            S.sig("tensor", "tp")
            S.wait("vector", "tp")
            S.op("vector", lambda e: e.tensor_copy(t_tau[:, :], p_tau[:, 0:NEXP]))
            S.drain("vector")
            S.op("vector", lambda e: e.tensor_tensor(
                t_msk[:].rearrange("p (e a) -> p e a", a=128),
                t_sd[:].rearrange("p (e a) -> p e a", a=128),
                t_tau[:, :, None].broadcast_to([128, NEXP, 128]), ALU.is_ge))
            S.drain("vector")
            def _cntred(e):
                with nc.allow_low_precision(reason="int count"):
                    e.tensor_reduce(t_cnt[:, :, None],
                                    t_msk[:].rearrange("p (e a) -> p e a", a=128),
                                    axis=AX.X, op=ALU.add)
            S.op("vector", _cntred)
            S.drain("vector")
            S.op("vector", lambda e: e.tensor_copy(t_cntf[:], t_cnt[:]))
            S.sig("vector", "vd")
            S.wait("tensor", "vd")
            S.op("tensor", lambda e: nc.tensor.matmul(
                p_cnt[:, 0:NEXP], t_onec[:, :], t_cntf[:], start=True, stop=True))
            S.sig("tensor", "tp")
            S.wait("vector", "tp")
            S.op("vector", lambda e: e.tensor_copy(t_crow[:], p_cnt[:, 0:NEXP]))
            S.drain("vector")
            S.op("vector", lambda e: e.tensor_scalar(
                t_sel[:], t_crow[:], float(TOPK), None, ALU.is_ge))
            S.op("vector", lambda e: e.tensor_scalar(
                t_seln[:], t_crow[:], float(TOPK), None, ALU.is_lt))
            S.drain("vector")
            S.op("vector", lambda e: e.copy_predicated(t_lo[:], t_sel[:], t_mid[:]))
            S.drain("vector")
            S.op("vector", lambda e: e.copy_predicated(t_hi[:], t_seln[:], t_mid[:]))
        S.sig("vector", "vd")
        S.wait("tensor", "vd")
        S.op("tensor", lambda e: nc.tensor.matmul(
            p_tau[:, 0:NEXP], t_one[:, :], t_lo[:].bitcast(F32), start=True, stop=True))
        S.sig("tensor", "tp")
        S.wait("vector", "tp")
        S.op("vector", lambda e: e.tensor_copy(t_tau[:, :], p_tau[:, 0:NEXP]))
        S.drain("vector")

        for ti in range(16):
            ssl = t_ss[:, ti * NEXP:(ti + 1) * NEXP]
            S.op("vector", lambda e, ssl=ssl: e.tensor_tensor(
                t_cnt[:, :], ssl, t_tau[:, :], ALU.is_ge))
            S.drain("vector")
            S.op("vector", lambda e: e.tensor_copy(t_cntf[:], t_cnt[:]))
            S.drain("vector")
            S.op("vector", lambda e, ssl=ssl: e.tensor_tensor(
                t_cntf[:], t_cntf[:], ssl, ALU.mult))
            S.drain("vector")
            S.op("vector", lambda e: e.reduce_sum(t_wc[:, 0:1], t_cntf[:], axis=AX.X))
            v = S.sig("vector", "vd")
            S.wait("gpsimd", "vd", v)
            S.dma("gpsimd", w_scr[ti * 128:(ti + 1) * 128, :], t_wc[:, 0:1], sem="wl")
            S.wait("vector", "wl")
        S.wait("sync", "wl")
        S.dma("sync", t_wrow[:], w_scr[:, 0].unsqueeze(0), sem="lwr")
        S.wait("tensor", "lwr", 16)
        for tt_i in range(4):
            S.op("tensor", lambda e, tt_i=tt_i: nc.tensor.matmul(
                p_w[tt_i][:, :], t_one[:, :], t_wrow[0:1, tt_i * 512:(tt_i + 1) * 512],
                start=True, stop=True))
        S.sig("tensor", "tp")
        wrep_tp = S.count("tp")

        S.wait("tensor", "lg")
        ev_vd = {}
        mm_vd = {}
        mm_tp = {}
        nmm = 0
        for do in range(16):
            sw = do % 2
            if do >= 2:
                S.wait("gpsimd", "tp", mm_tp[do - 2])
            S.dma("gpsimd", t_w[:, sw * 2048:(sw + 1) * 2048].rearrange("p (a j) -> p a j", j=128),
                  ew_in[:, do * 128:(do + 1) * 128].rearrange("(a p) j -> p a j", p=128),
                  sem=("lw0" if sw == 0 else "lw1"))
            if do >= 1:
                S.wait("sync", "vd", ev_vd[do - 1])
            S.dma("sync", t_x1[:], x1_in[do * 128:(do + 1) * 128, :], sem="ld")
            S.dma("sync", t_ym[:], ym_in[do * 128:(do + 1) * 128, :], sem="ld")
            S.wait("tensor", "lw0" if sw == 0 else "lw1", 16 * (do // 2 + 1))
            S.wait("tensor", "tp", wrep_tp)
            if do >= 1:
                S.wait("vector", "st")
            for tt_i in range(4):
                pm = p_mm[nmm % 2]
                if nmm >= 2:
                    S.wait("tensor", "vd", mm_vd[nmm - 2])
                for k in range(16):
                    S.op("tensor", lambda e, pm=pm, sw=sw, k=k, tt_i=tt_i:
                         nc.tensor.matmul(pm[:, :],
                                          t_w[:, sw * 2048 + k * 128:sw * 2048 + (k + 1) * 128],
                                          t_xnb[:, k * TH + tt_i * 512: k * TH + tt_i * 512 + 512],
                                          start=(k == 0), stop=(k == 15)))
                S.sig("tensor", "tp")
                S.wait("vector", "tp")
                S.wait("vector", "ld")
                S.op("vector", lambda e, pm=pm, do=do: e.tensor_scalar(
                    t_t1[:, :], pm[:, :], t_eb[:, do:do + 1], None, ALU.add))
                S.drain("vector")
                S.op("vector", lambda e, tt_i=tt_i: e.tensor_tensor(
                    t_t1[:, :], t_t1[:, :], p_w[tt_i][:, :], ALU.mult))
                S.drain("vector")
                S.op("vector", lambda e, tt_i=tt_i: e.tensor_tensor(
                    t_t1[:, :], t_t1[:, :], t_x1[:, tt_i * 512:(tt_i + 1) * 512], ALU.add))
                S.drain("vector")
                S.op("vector", lambda e, tt_i=tt_i: e.tensor_tensor(
                    t_ev[:, tt_i * 512:(tt_i + 1) * 512], t_t1[:, :],
                    t_ym[:, tt_i * 512:(tt_i + 1) * 512], ALU.add))
                v = S.sig("vector", "vd")
                mm_vd[nmm] = v
                nmm += 1
            ev_vd[do] = S.count("vd")
            mm_tp[do] = S.count("tp")
            S.wait("gpsimd", "vd", ev_vd[do])
            S.dma("gpsimd", out_o[do * 128:(do + 1) * 128, :], t_ev[:], sem="st")

        S.wait("gpsimd", "st")
        S.build()
    return nc


def kernel(**inputs):
    import ml_dtypes
    f32 = np.float32

    ncA = build_A()
    outsA = run_launch(ncA, shard_A(inputs), ["o_xnT", "o_z", "o_xbc", "o_dt"])
    xbc_full = np.zeros((B_SZ, L_SEQ, CONV_DIM), f32)
    dt_full = np.zeros((B_SZ, L_SEQ, NHEADS), f32)
    for c in range(NCORES):
        b, h = divmod(c, 2)
        s = h * TH
        xbc_full[b, s:s + TH] = outsA[c]["o_xbc"].T
        dt_full[b, s:s + TH] = outsA[c]["o_dt"].T
    xnT_full = np.concatenate([outsA[c]["o_xnT"] for c in range(NCORES)], axis=1)
    xnT_full = np.ascontiguousarray(xnT_full)

    ncB = build_B()
    outsB = run_launch(ncB, shard_B(xbc_full, dt_full, inputs), ["Y"])
    Yt_fm = np.zeros((B_SZ, D_INNER, L_SEQ), f32)
    for c in range(NCORES):
        b, hh = divmod(c, 2)
        Yt_fm[b, hh * 2048:(hh + 1) * 2048, :] = outsB[c]["Y"].T

    D_par = np.asarray(inputs["L0_attn_D_param"], f32)
    nw_in = np.asarray(inputs["L0_attn_norm_w"], f32)
    dpar = np.ascontiguousarray(np.repeat(D_par, HEADDIM).reshape(32, 128).T)
    nw = np.ascontiguousarray(nw_in.reshape(32, 128).T)
    w_out = np.ascontiguousarray(inputs["L0_attn_W_out"], dtype=f32)
    x0_flat = np.asarray(inputs["L0_attn_u"], f32).reshape(B_SZ * L_SEQ, D_MODEL)
    ncC = build_C()
    mapsC = []
    for c in range(NCORES):
        b, h = divmod(c, 2)
        tok = slice(h * TH, (h + 1) * TH)
        gtok = slice(b * L_SEQ + h * TH, b * L_SEQ + (h + 1) * TH)
        mapsC.append(dict(
            YT=np.ascontiguousarray(Yt_fm[b][:, tok]),
            z=outsA[c]["o_z"],
            Xfm=outsA[c]["o_xbc"][:D_INNER],
            dpar=dpar, nw=nw, w_out=w_out,
            x0T=np.ascontiguousarray(x0_flat[gtok].T)))
    outsC = run_launch(ncC, mapsC, ["r0"])

    ncD = build_D()
    outsD = run_launch(ncD, shard_D(xnT_full, inputs), ["part"])

    wbcT = np.ascontiguousarray(np.asarray(inputs["L1_attn_W_bc"], f32).T)
    wdtT = np.ascontiguousarray(np.asarray(inputs["L1_attn_W_dt"], f32).T)
    bdt = np.ascontiguousarray(np.asarray(inputs["L1_attn_b_dt"], f32).reshape(16, 128).T)
    rw = np.ascontiguousarray(inputs["L1_ffn_router_w"], dtype=f32)
    rb = np.ascontiguousarray(np.asarray(inputs["L1_ffn_router_b"], f32).reshape(-1, 1))
    idf = np.eye(128, dtype=f32)
    ncE = build_E()
    mapsE = []
    for c in range(NCORES):
        tok = slice(c * TH, (c + 1) * TH)
        m = dict(r0=outsC[c]["r0"], wbcT=wbcT, wdtT=wdtT, bdt=bdt, rw=rw, rb=rb, idf=idf)
        for i in range(NCORES):
            m[f"pp{i}"] = np.ascontiguousarray(outsD[i]["part"][:, tok])
        mapsE.append(m)
    outsE = run_launch(ncE, mapsE, ["x1", "xn1", "dl", "bc", "s"])

    delta_fm = np.zeros((B_SZ, D_MODEL, L_SEQ), f32)
    xn1_fm = np.zeros((B_SZ, D_MODEL, L_SEQ), f32)
    bc_all = np.zeros((B_SZ, 64, L_SEQ), f32)
    for c in range(NCORES):
        b, h = divmod(c, 2)
        tok = slice(h * TH, (h + 1) * TH)
        delta_fm[b][:, tok] = outsE[c]["dl"]
        xn1_fm[b][:, tok] = outsE[c]["xn1"]
        bc_all[b][:, tok] = outsE[c]["bc"]
    S_full = np.concatenate([outsE[c]["s"] for c in range(NCORES)], axis=0)

    ncF = build_F()
    outsF = run_launch(ncF, shard_F(delta_fm, xn1_fm, bc_all, inputs), ["y"])
    ym_fm = np.zeros((B_SZ, D_MODEL, L_SEQ), f32)
    for c in range(NCORES):
        ym_fm[:, c * 256:(c + 1) * 256, :] = outsF[c]["y"]

    sdev = np.ascontiguousarray(
        S_full.reshape(128, 128, NEXP).transpose(1, 2, 0).reshape(128, NEXP * 128))
    ew = np.ascontiguousarray(inputs["L1_ffn_expert_w"], dtype=f32)
    eb = np.ascontiguousarray(np.asarray(inputs["L1_ffn_expert_b"], f32).reshape(16, 128).T)
    ncG = build_G()
    mapsG = []
    for c in range(NCORES):
        b, h = divmod(c, 2)
        tok = slice(h * TH, (h + 1) * TH)
        gtok = slice(c * TH, (c + 1) * TH)
        mapsG.append(dict(
            x1=outsE[c]["x1"], xn1=outsE[c]["xn1"],
            ym=np.ascontiguousarray(ym_fm[b][:, tok]),
            sdev=sdev, ssl=np.ascontiguousarray(S_full[gtok]),
            ew=ew, eb=eb))
    outsG = run_launch(ncG, mapsG, ["out"])

    out = np.zeros((B_SZ * L_SEQ, D_MODEL), f32)
    for c in range(NCORES):
        out[c * TH:(c + 1) * TH] = outsG[c]["out"].T
    return out.reshape(B_SZ, L_SEQ, D_MODEL)


# version 02
# speedup vs baseline: 1.0005x; 1.0005x over previous
import contextlib
import numpy as np
import concourse.bass as bass
import concourse.mybir as mybir
from concourse.bass_utils import run_bass_kernel_spmd

F32 = mybir.dt.float32
BF16 = mybir.dt.bfloat16
I32 = mybir.dt.int32
AF = mybir.ActivationFunctionType
ALU = mybir.AluOpType
AX = mybir.AxisListType

B_SZ = 4
L_SEQ = 4096
D_MODEL = 2048
D_INNER = 4096
D_CONV = 4
CONV_DIM = 4352
D_IN_PROJ = 8512
NHEADS = 64
HEADDIM = 64
CHUNK = 64
SSD_N = 128
TH = L_SEQ // 2
TE = TH + 3
NCORES = 8

DT_RANK = 64
L1_N = 32
NEXP = 16
TOPK = 1024

ENGS = ("sync", "scalar", "vector", "tensor", "gpsimd")


class Sched:

    def __init__(self, nc, ctx):
        self.nc = nc
        self.items = {e: [] for e in ENGS}
        self.sems = {}
        self.counts = {}
        self.ctx = ctx

    def sem(self, name):
        self.sems[name] = self.ctx.enter_context(self.nc.semaphore(name))
        self.counts[name] = 0
        return name

    def count(self, sem):
        return self.counts[sem]

    def op(self, eng, fn):
        self.items[eng].append(fn)

    def sig(self, eng, sem, n=1):
        h = self.sems[sem]
        self.counts[sem] += n
        self.items[eng].append(lambda e: e.drain().then_inc(h, n))
        return self.counts[sem]

    def dma(self, eng, out, in_, sem=None, **kw):
        h = self.sems[sem] if sem else None
        if sem:
            self.counts[sem] += 16
        def fn(e, out=out, in_=in_, h=h, kw=kw):
            inst = e.dma_start(out=out, in_=in_, **kw)
            if h is not None:
                inst.then_inc(h, 16)
        self.items[eng].append(fn)
        return self.counts[sem] if sem else None

    def wait(self, eng, sem, val=None):
        h = self.sems[sem]
        v = self.counts[sem] if val is None else val
        if v > 0:
            self.items[eng].append(lambda e: e.wait_ge(h, v))

    def drain(self, eng):
        self.items[eng].append(lambda e: e.drain())

    def build(self):
        block = self.ctx.enter_context(self.nc.Block())
        for engname in ENGS:
            items = self.items[engname]
            def mk(items):
                def f(eng):
                    for fn in items:
                        fn(eng)
                return f
            getattr(block, engname)(mk(items))


def make_nc():
    return bass.Bass("TRN2", target_bir_lowering=False)


def sched_ctx(nc):
    ctx = contextlib.ExitStack()
    S = Sched(nc, ctx)
    sbuf = lambda n, s, d=F32: ctx.enter_context(nc.sbuf_tensor(n, s, d))
    psum = lambda n, s, d=F32: ctx.enter_context(nc.psum_tensor(n, s, d))
    return ctx, S, sbuf, psum


def run_launch(nc, in_maps, out_names):
    res = run_bass_kernel_spmd(nc, in_maps, core_ids=list(range(NCORES)))
    return [{k: np.asarray(r[k]) for k in out_names} for r in res.results]


def id128_bf():
    import ml_dtypes
    return np.eye(128, dtype=ml_dtypes.bfloat16)


def build_A():
    nc = make_nc()
    x_in = nc.dram_tensor("x", [TE, D_MODEL], F32, kind="ExternalInput").ap()
    w_in = nc.dram_tensor("w_in", [D_MODEL, D_IN_PROJ], F32, kind="ExternalInput").ap()
    conv_w = nc.dram_tensor("conv_w", [CONV_DIM, D_CONV], F32, kind="ExternalInput").ap()
    conv_b = nc.dram_tensor("conv_b", [CONV_DIM, 1], F32, kind="ExternalInput").ap()
    dt_bias = nc.dram_tensor("dt_bias", [NHEADS, 1], F32, kind="ExternalInput").ap()
    iden = nc.dram_tensor("iden", [128, 128], BF16, kind="ExternalInput").ap()

    o_xnT = nc.dram_tensor("o_xnT", [D_MODEL, TH], BF16, kind="ExternalOutput").ap()
    o_z = nc.dram_tensor("o_z", [D_INNER, TH], F32, kind="ExternalOutput").ap()
    o_xbc = nc.dram_tensor("o_xbc", [CONV_DIM, TH], F32, kind="ExternalOutput").ap()
    o_dt = nc.dram_tensor("o_dt", [NHEADS, TH], F32, kind="ExternalOutput").ap()

    NT = 17
    NJ = 67
    TT5 = [(0, 512), (512, 512), (1024, 512), (1536, 512), (2048, 3)]

    ctx, S, sbuf, psum = sched_ctx(nc)
    with ctx:
        t_x = sbuf("t_x", [128, 2 * D_MODEL])
        t_xn = sbuf("t_xn", [128, NT * D_MODEL], BF16)
        t_sq = sbuf("t_sq", [128, NT])
        t_mu = sbuf("t_mu", [128, NT])
        t_va = sbuf("t_va", [128, NT])
        t_rs = sbuf("t_rs", [128, NT])
        t_scr = sbuf("t_scr", [128, D_MODEL])
        t_id = sbuf("t_id", [128, 128], BF16)
        t_xnT = sbuf("t_xnT", [128, 16 * TE], BF16)
        t_w = sbuf("t_w", [128, 2 * 2048], BF16)
        t_strip = [sbuf(f"t_strip{i}", [128, TE]) for i in range(2)]
        t_conv = [sbuf(f"t_conv{i}", [128, TH]) for i in range(2)]
        t_cw = sbuf("t_cw", [128, 34 * D_CONV])
        t_cb = sbuf("t_cb", [128, 34])
        t_dtb = sbuf("t_dtb", [NHEADS, 1])
        p_t = [psum(f"p_t{i}", [128, 1024], BF16) for i in range(2)]
        p_mm = [psum(f"p_mm{i}", [128, 512]) for i in range(2)]

        for s in ("ld", "lxa", "lxb", "lwa", "lwb", "vv", "ss", "tt", "st", "ev"):
            S.sem(s)

        S.dma("sync", t_id[:], iden, sem="ld")
        S.dma("sync", t_cw[:].rearrange("p (a w) -> p a w", w=D_CONV), conv_w.rearrange("(a p) w -> p a w", p=128), sem="ld")
        S.dma("sync", t_cb[:].rearrange("p (a o) -> p a o", o=1), conv_b.rearrange("(a p) o -> p a o", p=128), sem="ld", allow_slow_non_contiguous=True)
        S.dma("sync", t_dtb[:], dt_bias, sem="ld")
        for i in range(NT):
            r = min(128, TE - i * 128)
            if i >= 2:
                S.wait("sync", "vv", 2 * (i - 1))
            S.dma("sync", t_x[0:r, (i % 2) * D_MODEL:(i % 2 + 1) * D_MODEL],
                  x_in[i * 128:i * 128 + r, :], sem=("lxa" if i % 2 == 0 else "lxb"))

        for i in range(NT):
            r = min(128, TE - i * 128)
            xv = t_x[0:r, (i % 2) * D_MODEL:(i % 2 + 1) * D_MODEL]
            S.wait("scalar", "lxa" if i % 2 == 0 else "lxb", 16 * (i // 2 + 1))
            S.wait("scalar", "ld", 64)
            S.op("scalar", lambda e, xv=xv, r=r, i=i: e.activation(
                t_scr[0:r, :], xv, AF.Square, accum_out=t_sq[0:r, i:i + 1]))
            S.sig("scalar", "ss")
            S.wait("vector", "lxa" if i % 2 == 0 else "lxb", 16 * (i // 2 + 1))
            S.wait("vector", "ld", 64)
            S.op("vector", lambda e, xv=xv, r=r, i=i: e.reduce_sum(
                t_mu[0:r, i:i + 1], xv, axis=AX.X))
            S.wait("vector", "ss")
            S.drain("vector")
            S.op("vector", lambda e, r=r, i=i: e.tensor_scalar(
                t_mu[0:r, i:i + 1], t_mu[0:r, i:i + 1], 1.0 / D_MODEL, None, ALU.mult))
            S.op("vector", lambda e, r=r, i=i: e.tensor_scalar(
                t_va[0:r, i:i + 1], t_sq[0:r, i:i + 1], 1.0 / D_MODEL, None, ALU.mult))
            S.drain("vector")
            S.op("vector", lambda e, r=r, i=i: e.scalar_tensor_tensor(
                t_va[0:r, i:i + 1], t_mu[0:r, i:i + 1], t_mu[0:r, i:i + 1],
                t_va[0:r, i:i + 1], ALU.mult, ALU.subtract))
            S.drain("vector")
            S.op("vector", lambda e, r=r, i=i: e.tensor_scalar(
                t_va[0:r, i:i + 1], t_va[0:r, i:i + 1], -1.0, 1e-5, ALU.mult, ALU.add))
            S.drain("vector")
            S.op("vector", lambda e, r=r, i=i: e.reciprocal(
                t_rs[0:r, i:i + 1], t_va[0:r, i:i + 1]))
            S.sig("vector", "vv")
            S.wait("scalar", "vv")
            S.op("scalar", lambda e, r=r, i=i: e.activation(
                t_rs[0:r, i:i + 1], t_rs[0:r, i:i + 1], AF.Sqrt))
            S.sig("scalar", "ss")
            S.wait("vector", "ss")
            S.op("vector", lambda e, xv=xv, r=r, i=i: e.tensor_scalar(
                t_xn[0:r, i * D_MODEL:(i + 1) * D_MODEL], xv,
                t_mu[0:r, i:i + 1], t_rs[0:r, i:i + 1], ALU.subtract, ALU.mult))
            S.sig("vector", "vv")
        ln_vv = S.count("vv")

        S.wait("tensor", "vv", ln_vv)
        nev = 0
        for i in range(NT):
            r = min(128, TE - i * 128)
            for dtile in range(16):
                pb = p_t[nev % 2]
                if nev >= 2:
                    S.wait("tensor", "ev", nev - 1)
                S.op("tensor", lambda e, pb=pb, r=r, i=i, dtile=dtile:
                     nc.tensor.transpose(
                         pb[:, 0:r],
                         t_xn[0:r, i * D_MODEL + dtile * 128: i * D_MODEL + (dtile + 1) * 128],
                         t_id[0:r, 0:r]))
                S.sig("tensor", "tt")
                S.wait("vector", "tt")
                S.op("vector", lambda e, pb=pb, r=r, i=i, dtile=dtile: e.tensor_copy(
                    t_xnT[:, dtile * TE + i * 128: dtile * TE + i * 128 + r], pb[:, 0:r]))
                S.sig("vector", "ev")
                nev += 1
        xnT_ev = S.count("ev")

        S.wait("gpsimd", "ev", xnT_ev)
        for dtile in range(16):
            S.dma("gpsimd", o_xnT[dtile * 128:(dtile + 1) * 128, :],
                  t_xnT[:, dtile * TE + 3: dtile * TE + TE], sem="st")

        tt_after_mm = {}
        st_after = {}
        nmm = 0
        for j in range(NJ):
            jw = 128 if j < 66 else 64
            sj = j % 2
            strip = t_strip[sj]
            cvb = t_conv[sj]
            if j >= 2:
                S.wait("gpsimd", "tt", tt_after_mm[j - 2])
            S.dma("gpsimd", t_w[0:128, sj * 2048: sj * 2048 + 16 * jw].rearrange(
                      "p (a j) -> p a j", j=jw),
                  w_in[:, j * 128:j * 128 + jw].rearrange("(a p) j -> p a j", p=128),
                  sem=("lwa" if sj == 0 else "lwb"))
            S.wait("tensor", "lwa" if sj == 0 else "lwb", 16 * (j // 2 + 1))
            if j == 0:
                S.wait("tensor", "ev", xnT_ev)
            if j >= 2:
                S.wait("vector", "st")
            for ti, (tq, tw) in enumerate(TT5):
                pm = p_mm[nmm % 2]
                if nmm >= 2:
                    S.wait("tensor", "ev", xnT_ev + nmm - 1)
                for k in range(16):
                    S.op("tensor", lambda e, pm=pm, jw=jw, tw=tw, sj=sj, k=k, tq=tq:
                         nc.tensor.matmul(pm[0:jw, 0:tw],
                                          t_w[:, sj * 2048 + k * jw: sj * 2048 + (k + 1) * jw],
                                          t_xnT[:, k * TE + tq: k * TE + tq + tw],
                                          start=(k == 0), stop=(k == 15)))
                S.sig("tensor", "tt")
                S.wait("vector", "tt")
                S.op("vector", lambda e, pm=pm, strip=strip, jw=jw, tq=tq, tw=tw:
                     e.tensor_copy(strip[0:jw, tq:tq + tw], pm[0:jw, 0:tw]))
                S.sig("vector", "ev")
                nmm += 1
            tt_after_mm[j] = S.count("tt")
            S.drain("vector")
            if j < 32:
                S.wait("gpsimd", "ev")
                st_after[j] = S.dma("gpsimd", o_z[j * 128:(j + 1) * 128, :],
                                    strip[:, 3:TE], sem="st")
            elif j < 66:
                cj = j - 32
                S.op("vector", lambda e, strip=strip, cvb=cvb, cj=cj: e.tensor_scalar(
                    cvb[:], strip[:, 0:TH], t_cw[:, cj * 4:cj * 4 + 1], None, ALU.mult))
                for w in range(1, 4):
                    S.drain("vector")
                    S.op("vector", lambda e, strip=strip, cvb=cvb, cj=cj, w=w:
                         e.scalar_tensor_tensor(cvb[:], strip[:, w:w + TH],
                                                t_cw[:, cj * 4 + w:cj * 4 + w + 1],
                                                cvb[:], ALU.mult, ALU.add))
                S.drain("vector")
                S.op("vector", lambda e, cvb=cvb, cj=cj: e.tensor_scalar(
                    cvb[:], cvb[:], t_cb[:, cj:cj + 1], None, ALU.add))
                S.sig("vector", "vv")
                S.wait("scalar", "vv")
                S.op("scalar", lambda e, cvb=cvb: e.activation(cvb[:], cvb[:], AF.Silu))
                S.sig("scalar", "ss")
                S.wait("gpsimd", "ss")
                st_after[j] = S.dma("gpsimd", o_xbc[cj * 128:(cj + 1) * 128, :],
                                    cvb[:], sem="st")
            else:
                S.wait("vector", "st")
                sp0 = t_conv[0]
                sp1 = t_conv[1]
                S.op("vector", lambda e, strip=strip, sp0=sp0: e.tensor_scalar(
                    sp0[0:64, 0:TH], strip[0:64, 3:TE], t_dtb[:, 0:1], None, ALU.add))
                S.sig("vector", "vv")
                S.wait("scalar", "vv")
                S.wait("scalar", "st")
                S.op("scalar", lambda e, sp0=sp0, sp1=sp1: e.activation(sp1[0:64, 0:TH], sp0[0:64, 0:TH], AF.Abs))
                S.drain("scalar")
                S.op("scalar", lambda e, sp1=sp1: e.activation(sp1[0:64, 0:TH], sp1[0:64, 0:TH], AF.Exp, scale=-1.0))
                S.drain("scalar")
                S.op("scalar", lambda e, sp1=sp1: e.activation(sp1[0:64, 0:TH], sp1[0:64, 0:TH], AF.Ln, bias=1.0))
                S.drain("scalar")
                S.op("scalar", lambda e, sp0=sp0: e.activation(sp0[0:64, 0:TH], sp0[0:64, 0:TH], AF.Relu))
                S.sig("scalar", "ss")
                S.wait("vector", "ss")
                S.op("vector", lambda e, sp0=sp0, sp1=sp1: e.tensor_tensor(
                    sp0[0:64, 0:TH], sp0[0:64, 0:TH], sp1[0:64, 0:TH], ALU.add))
                S.sig("vector", "vv")
                S.wait("gpsimd", "vv")
                st_after[j] = S.dma("gpsimd", o_dt[:], sp0[0:64, 0:TH], sem="st")

        S.wait("gpsimd", "st")
        S.build()
    return nc


def shard_A(inputs):
    x = np.asarray(inputs["L0_attn_u"], np.float32).reshape(B_SZ * L_SEQ, D_MODEL)
    w_in = np.ascontiguousarray(inputs["L0_attn_W_in"], dtype=np.float32)
    conv_w = np.ascontiguousarray(inputs["L0_attn_conv_w"], dtype=np.float32)
    conv_b = np.ascontiguousarray(inputs["L0_attn_conv_b"], dtype=np.float32).reshape(-1, 1)
    dt_bias = np.ascontiguousarray(inputs["L0_attn_dt_bias"], dtype=np.float32).reshape(-1, 1)
    iden = id128_bf()
    maps = []
    for c in range(NCORES):
        b, h = divmod(c, 2)
        start = b * L_SEQ + h * TH
        xs = np.zeros((TE, D_MODEL), np.float32)
        lo = max(b * L_SEQ, start - 3)
        xs[3 - (start - lo):] = x[lo:start + TH]
        maps.append(dict(x=xs, w_in=w_in, conv_w=conv_w, conv_b=conv_b,
                         dt_bias=dt_bias, iden=iden))
    return maps


def kernel(**inputs):
    raise NotImplementedError("under construction")


def build_B(LQ=L_SEQ):
    nc = make_nc()
    NC64 = LQ // CHUNK
    HL = 32
    HP = 2048
    X_in = nc.dram_tensor("X", [LQ, HP], F32, kind="ExternalInput").ap()
    dtT_in = nc.dram_tensor("dtT", [HL, LQ], F32, kind="ExternalInput").ap()
    dtk_in = nc.dram_tensor("dtk", [LQ, HL], F32, kind="ExternalInput").ap()
    BT_in = nc.dram_tensor("BT", [SSD_N, LQ], F32, kind="ExternalInput").ap()
    CT_in = nc.dram_tensor("CT", [SSD_N, LQ], F32, kind="ExternalInput").ap()
    Btk_in = nc.dram_tensor("Btk", [LQ, SSD_N], F32, kind="ExternalInput").ap()
    alog_in = nc.dram_tensor("alog", [HL, 1], F32, kind="ExternalInput").ap()
    mask_in = nc.dram_tensor("mask", [CHUNK, HP], F32, kind="ExternalInput").ap()
    idf_in = nc.dram_tensor("idf", [128, 128], F32, kind="ExternalInput").ap()
    Y_out = nc.dram_tensor("Y", [LQ, HP], F32, kind="ExternalOutput").ap()
    a_scr = nc.dram_tensor("a_scr", [HL, LQ], F32, kind="Internal").ap()

    ctx, S, sbuf, psum = sched_ctx(nc)
    with ctx:
        t_X = sbuf("t_X", [CHUNK, HP])
        t_dtT = sbuf("t_dtT", [HL, LQ])
        t_ac = sbuf("t_ac", [HL, LQ])
        t_ds = sbuf("t_ds", [HL, LQ])
        t_dtk = sbuf("t_dtk", [CHUNK, NC64 * HL])
        t_BT = sbuf("t_BT", [SSD_N, LQ], BF16)
        t_CT = sbuf("t_CT", [SSD_N, LQ], BF16)
        t_Btk = sbuf("t_Btk", [CHUNK, NC64 * SSD_N], BF16)
        t_acT = sbuf("t_acT", [CHUNK, NC64 * HL])
        t_dsT = sbuf("t_dsT", [CHUNK, NC64 * HL])
        t_odT = sbuf("t_odT", [CHUNK, NC64 * HL])
        t_al = sbuf("t_al", [HL, 2])
        t_zer = sbuf("t_zer", [HL, CHUNK])
        t_one = sbuf("t_one", [1, 128])
        t_id = sbuf("t_id", [128, 128])
        t_msk = sbuf("t_msk", [CHUNK, HP], BF16)
        t_Xdt = sbuf("t_Xdt", [CHUNK, HP], BF16)
        t_Xdc = sbuf("t_Xdc", [CHUNK, HP], BF16)
        t_t2m = sbuf("t_t2m", [CHUNK, HP])
        t_arg = sbuf("t_arg", [CHUNK, HP])
        t_L = sbuf("t_L", [CHUNK, HP])
        t_Mt = sbuf("t_Mt", [CHUNK, HP], BF16)
        t_CB = sbuf("t_CB", [CHUNK, CHUNK])
        t_Y = sbuf("t_Y", [CHUNK, HP])
        t_tmp = sbuf("t_tmp", [CHUNK, 512])
        t_H = sbuf("t_H", [SSD_N, HP])
        t_Hb = sbuf("t_Hb", [SSD_N, HP], BF16)
        t_Ar = sbuf("t_Ar", [1, HP])
        t_Er = sbuf("t_Er", [1, HP])
        pb = [psum(f"pb{i}", [128, 512]) for i in range(8)]

        for s in ("ld", "lg", "lx", "la", "vd", "sa", "tp", "sy", "as_"):
            S.sem(s)

        S.dma("sync", t_dtT[:], dtT_in, sem="ld")
        S.dma("sync", t_al[:, 0:1], alog_in, sem="ld")
        S.dma("sync", t_id[:], idf_in, sem="ld")
        S.dma("sync", t_dtk[:].rearrange("p (a h) -> p a h", h=HL),
              dtk_in.rearrange("(a p) h -> p a h", p=CHUNK), sem="ld")
        S.dma("gpsimd", t_msk[:], mask_in, sem="lg")
        S.dma("gpsimd", t_BT[:], BT_in, sem="lg")
        S.dma("gpsimd", t_CT[:], CT_in, sem="lg")
        S.dma("gpsimd", t_Btk[:].rearrange("p (a n) -> p a n", n=SSD_N),
              Btk_in.rearrange("(a p) n -> p a n", p=CHUNK), sem="lg")
        ld_all = S.count("ld")
        lg_all = S.count("lg")

        S.op("vector", lambda e: e.memset(t_zer[:], 0.0))
        S.op("vector", lambda e: e.memset(t_one[:], 1.0))
        S.op("vector", lambda e: e.memset(t_H[:], 0.0))
        S.op("vector", lambda e: e.memset(t_Hb[:], 0.0))
        S.wait("scalar", "ld", ld_all)
        S.wait("scalar", "lg", lg_all)
        S.op("scalar", lambda e: e.activation(t_al[:, 1:2], t_al[:, 0:1], AF.Exp))
        S.sig("scalar", "sa")
        S.wait("vector", "sa")
        S.op("vector", lambda e: e.tensor_scalar(t_al[:, 1:2], t_al[:, 1:2], -1.0, None, ALU.mult))
        S.drain("vector")
        S.op("vector", lambda e: e.tensor_scalar(t_dtT[:], t_dtT[:], t_al[:, 1:2], None, ALU.mult))
        S.drain("vector")
        for c in range(NC64):
            sl0 = c * CHUNK
            S.op("vector", lambda e, sl0=sl0: e.tensor_tensor_scan(
                t_ac[:, sl0:sl0 + CHUNK], t_dtT[:, sl0:sl0 + CHUNK], t_zer[:],
                0.0, ALU.add, ALU.add))
        S.sig("vector", "vd")
        for c in range(NC64):
            sl0 = c * CHUNK
            S.op("vector", lambda e, sl0=sl0: e.tensor_scalar(
                t_ds[:, sl0:sl0 + CHUNK], t_ac[:, sl0:sl0 + CHUNK],
                t_ac[:, sl0 + CHUNK - 1:sl0 + CHUNK], None, ALU.subtract))
        S.sig("vector", "vd")
        S.wait("gpsimd", "vd")
        S.dma("gpsimd", a_scr, t_ac[:], sem="as_")
        S.wait("scalar", "vd")
        S.op("scalar", lambda e: e.activation(t_dtT[:], t_ac[:], AF.Exp))
        S.drain("scalar")
        S.op("scalar", lambda e: e.activation(t_ds[:], t_ds[:], AF.Exp, scale=-1.0))
        S.sig("scalar", "sa")
        S.wait("tensor", "sa")
        S.wait("tensor", "ld", ld_all)
        S.wait("tensor", "lg", lg_all)
        tr_vd = []
        ntr = 0
        for (srct, dst) in ((t_ac, t_acT), (t_ds, t_dsT), (t_dtT, t_odT)):
            for c in range(NC64):
                bank = pb[ntr % 2]
                if ntr >= 2:
                    S.wait("tensor", "vd", tr_vd[ntr - 2])
                S.op("tensor", lambda e, srct=srct, c=c, bank=bank:
                     nc.tensor.transpose(bank[0:CHUNK, 0:HL],
                                         srct[:, c * CHUNK:(c + 1) * CHUNK],
                                         t_id[0:HL, 0:HL]))
                S.sig("tensor", "tp")
                S.wait("vector", "tp")
                S.op("vector", lambda e, dst=dst, c=c, bank=bank: e.tensor_copy(
                    dst[:, c * HL:(c + 1) * HL], bank[0:CHUNK, 0:HL]))
                tr_vd.append(S.sig("vector", "vd"))
                ntr += 1
        pre_vd = S.count("vd")

        rec = {"ydiag": {}, "states": {}, "comb": {}, "hupd": {}, "mt": {},
               "arow": {}, "odf": {}, "t1": {}, "yst": {}, "lexp": {},
               "erep": {}, "ercp": {}, "xdt": {}}
        for c in range(NC64):
            sl0 = c * CHUNK
            if c >= 1:
                S.wait("sync", "vd", rec["xdt"][c - 1])
            S.dma("sync", t_X[:], X_in[sl0:sl0 + CHUNK, :], sem="lx")
            S.wait("sync", "as_", 16)
            if c >= 1:
                S.wait("sync", "vd", rec["ercp"][c - 1])
            S.dma("sync", t_Ar[0:1, :].rearrange("o (h l) -> o h l", l=CHUNK),
                  a_scr[:, sl0:sl0 + CHUNK].unsqueeze(0), sem="la")
            rec["arow"][c] = S.count("la")

            S.wait("vector", "lx")
            if c >= 1:
                S.wait("vector", "tp", rec["ydiag"][c - 1])
            dtk_sl = t_dtk[:, c * HL:(c + 1) * HL]
            S.op("vector", lambda e, dtk_sl=dtk_sl: e.tensor_tensor(
                t_Xdt[:].rearrange("l (h p) -> l h p", p=HEADDIM),
                t_X[:].rearrange("l (h p) -> l h p", p=HEADDIM),
                dtk_sl[:, :, None].broadcast_to([CHUNK, HL, HEADDIM]), ALU.mult))
            S.drain("vector")
            if c >= 1:
                S.wait("vector", "tp", rec["states"][c - 1])
            dsT_sl = t_dsT[:, c * HL:(c + 1) * HL]
            S.op("vector", lambda e, dsT_sl=dsT_sl: e.tensor_tensor(
                t_Xdc[:].rearrange("l (h p) -> l h p", p=HEADDIM),
                t_Xdt[:].rearrange("l (h p) -> l h p", p=HEADDIM),
                dsT_sl[:, :, None].broadcast_to([CHUNK, HL, HEADDIM]), ALU.mult))
            S.sig("vector", "vd")
            rec["xdt"][c] = S.count("vd")
            acT_sl = t_acT[:, c * HL:(c + 1) * HL]
            S.op("vector", lambda e, acT_sl=acT_sl: e.tensor_tensor(
                t_t2m[:].rearrange("l (h p) -> l h p", p=HEADDIM),
                acT_sl[:, :, None].broadcast_to([CHUNK, HL, HEADDIM]),
                t_msk[:].rearrange("l (h p) -> l h p", p=HEADDIM), ALU.subtract))
            S.sig("vector", "vd")

            if c >= 1:
                S.wait("tensor", "vd", rec["hupd"][c - 1])
            S.op("tensor", lambda e, sl0=sl0: nc.tensor.matmul(
                pb[4][0:CHUNK, 0:CHUNK], t_BT[:, sl0:sl0 + CHUNK],
                t_CT[:, sl0:sl0 + CHUNK], start=True, stop=True))
            S.sig("tensor", "tp")
            S.wait("vector", "tp")
            S.op("vector", lambda e: e.tensor_copy(t_CB[:], pb[4][0:CHUNK, 0:CHUNK]))
            S.sig("vector", "vd")

            if c >= 1:
                S.wait("vector", "sa", rec["lexp"][c - 1])
            S.wait("tensor", "la", rec["arow"][c])
            arg_vd = [0, 0]
            for i in range(4):
                bank = pb[2 + i % 2]
                if c >= 1 and i < 2:
                    S.wait("tensor", "vd", rec["hupd"][c - 1])
                if i >= 2:
                    S.wait("tensor", "vd", arg_vd[i - 2])
                S.op("tensor", lambda e, bank=bank, i=i: nc.tensor.matmul(
                    bank[0:CHUNK, :], t_one[:, 0:CHUNK],
                    t_Ar[0:1, i * 512:(i + 1) * 512],
                    start=True, stop=True))
                S.sig("tensor", "tp")
                S.wait("vector", "tp")
                S.op("vector", lambda e, bank=bank, i=i: e.tensor_tensor(
                    t_arg[:, i * 512:(i + 1) * 512], bank[0:CHUNK, :],
                    t_t2m[:, i * 512:(i + 1) * 512], ALU.subtract))
                v = S.sig("vector", "vd")
                if i < 2:
                    arg_vd[i] = v
            rec["t1"][c] = S.count("tp")

            S.wait("scalar", "vd")
            S.op("scalar", lambda e: e.activation(t_L[:], t_arg[:], AF.Exp))
            S.sig("scalar", "sa")
            rec["lexp"][c] = S.count("sa")
            S.wait("scalar", "la", rec["arow"][c])
            S.wait("scalar", "tp", rec["t1"][c])
            S.op("scalar", lambda e: e.activation(t_Ar[0:1, :], t_Ar[0:1, :], AF.Exp))
            S.sig("scalar", "sa")
            rec["odf"][c] = S.count("sa")

            if c >= 1:
                S.wait("vector", "tp", rec["erep"][c - 1])
            S.wait("vector", "sa", rec["odf"][c])
            S.op("vector", lambda e: e.tensor_copy(
                t_Er[:].rearrange("o (h p) -> o h p", p=HEADDIM),
                t_Ar[0:1, :].rearrange("o (h l) -> o h l", l=CHUNK)
                [:, :, CHUNK - 1:CHUNK].broadcast_to([1, HL, HEADDIM])))
            S.sig("vector", "vd")
            rec["ercp"][c] = S.count("vd")
            S.op("vector", lambda e: e.tensor_tensor(
                t_Mt[:].rearrange("l (h p) -> l h p", p=HEADDIM),
                t_L[:].rearrange("l (h p) -> l h p", p=HEADDIM),
                t_CB[:, None, :].broadcast_to([CHUNK, HL, CHUNK]), ALU.mult))
            S.sig("vector", "vd")
            rec["mt"][c] = S.count("vd")

            if c >= 1:
                S.wait("vector", "sy", rec["yst"][c - 1])
            S.wait("tensor", "vd", rec["mt"][c])
            comb_vd = [0, 0]
            for i in range(4):
                ydb = pb[0 + i % 2]
                yob = pb[2 + i % 2]
                if c >= 1 and i < 2:
                    S.wait("tensor", "vd", rec["hupd"][c - 1])
                if i >= 2:
                    S.wait("tensor", "vd", comb_vd[i - 2])
                for hh in range(8):
                    h = i * 8 + hh
                    S.op("tensor", lambda e, ydb=ydb, h=h, hh=hh: nc.tensor.matmul(
                        ydb[0:CHUNK, hh * 64:(hh + 1) * 64],
                        t_Mt[:, h * 64:(h + 1) * 64],
                        t_Xdt[:, h * 64:(h + 1) * 64],
                        start=(hh == 0), stop=(hh == 7)))
                S.sig("tensor", "tp")
                S.op("tensor", lambda e, yob=yob, sl0=sl0, i=i: nc.tensor.matmul(
                    yob[0:CHUNK, :], t_CT[:, sl0:sl0 + CHUNK],
                    t_Hb[:, i * 512:(i + 1) * 512], start=True, stop=True))
                S.sig("tensor", "tp")
                S.wait("vector", "tp")
                odT_sl = t_odT[:, c * HL + i * 8: c * HL + (i + 1) * 8]
                S.op("vector", lambda e, yob=yob, odT_sl=odT_sl: e.tensor_tensor(
                    t_tmp[:].rearrange("l (h p) -> l h p", p=HEADDIM),
                    yob[0:CHUNK, :].rearrange("l (h p) -> l h p", p=HEADDIM),
                    odT_sl[:, :, None].broadcast_to([CHUNK, 8, HEADDIM]), ALU.mult))
                S.drain("vector")
                S.op("vector", lambda e, ydb=ydb, i=i: e.tensor_tensor(
                    t_Y[:, i * 512:(i + 1) * 512], t_tmp[:], ydb[0:CHUNK, :], ALU.add))
                v = S.sig("vector", "vd")
                if i < 2:
                    comb_vd[i] = v
            rec["ydiag"][c] = S.count("tp")
            rec["comb"][c] = S.count("vd")

            S.wait("gpsimd", "vd", rec["comb"][c])
            rec["yst"][c] = S.dma("gpsimd", Y_out[sl0:sl0 + CHUNK, :], t_Y[:], sem="sy")

            S.wait("tensor", "vd", S.count("vd"))
            for i in range(4):
                stb = pb[4 + i]
                for hh in range(8):
                    h = i * 8 + hh
                    S.op("tensor", lambda e, stb=stb, h=h, hh=hh, c=c: nc.tensor.matmul(
                        stb[:, hh * 64:(hh + 1) * 64],
                        t_Btk[:, c * SSD_N:(c + 1) * SSD_N],
                        t_Xdc[:, h * 64:(h + 1) * 64],
                        start=(hh == 0), stop=(hh == 7)))
                S.sig("tensor", "tp")
            rec["states"][c] = S.count("tp")
            S.wait("tensor", "vd", rec["comb"][c])
            hup_vd = [0, 0]
            for i in range(4):
                erb = pb[0 + i % 2]
                if i >= 2:
                    S.wait("tensor", "vd", hup_vd[i - 2])
                S.op("tensor", lambda e, erb=erb, i=i: nc.tensor.matmul(
                    erb[:, :], t_one[:, :], t_Er[0:1, i * 512:(i + 1) * 512],
                    start=True, stop=True))
                S.sig("tensor", "tp")
                S.wait("vector", "tp")
                S.op("vector", lambda e, erb=erb, i=i: e.tensor_tensor(
                    t_H[:, i * 512:(i + 1) * 512], t_H[:, i * 512:(i + 1) * 512],
                    erb[:, :], ALU.mult))
                S.drain("vector")
                S.op("vector", lambda e, i=i: e.tensor_tensor(
                    t_H[:, i * 512:(i + 1) * 512], t_H[:, i * 512:(i + 1) * 512],
                    pb[4 + i][:, :], ALU.add))
                S.drain("vector")
                S.op("vector", lambda e, i=i: e.tensor_copy(
                    t_Hb[:, i * 512:(i + 1) * 512], t_H[:, i * 512:(i + 1) * 512]))
                v = S.sig("vector", "vd")
                if i < 2:
                    hup_vd[i] = v
            rec["erep"][c] = S.count("tp")
            rec["hupd"][c] = S.count("vd")

        S.wait("gpsimd", "sy")
        S.build()
    return nc


def shard_B(xbc_full, dt_full, inputs):
    A_log = np.asarray(inputs["L0_attn_A_log"], np.float32)
    D_par = np.asarray(inputs["L0_attn_D_param"], np.float32)
    l = np.arange(CHUNK)
    m = np.where(l[None, :] >= l[:, None], 0.0, -1e9).astype(np.float32)
    mask = np.ascontiguousarray(np.tile(m[:, None, :], (1, 32, 1)).reshape(CHUNK, 2048))
    idf = np.eye(128, dtype=np.float32)
    maps = []
    for c in range(NCORES):
        b, hh = divmod(c, 2)
        X = np.ascontiguousarray(xbc_full[b][:, hh * 2048:(hh + 1) * 2048])
        Bm = xbc_full[b][:, 4096:4224]
        Cm = xbc_full[b][:, 4224:4352]
        dts = dt_full[b][:, hh * 32:(hh + 1) * 32]
        maps.append(dict(
            X=X, dtT=np.ascontiguousarray(dts.T), dtk=np.ascontiguousarray(dts),
            BT=np.ascontiguousarray(Bm.T), CT=np.ascontiguousarray(Cm.T),
            Btk=np.ascontiguousarray(Bm),
            alog=np.ascontiguousarray(A_log[hh * 32:(hh + 1) * 32].reshape(-1, 1)),
            mask=mask, idf=idf))
    return maps


def build_C():
    nc = make_nc()
    YT_in = nc.dram_tensor("YT", [D_INNER, TH], F32, kind="ExternalInput").ap()
    z_in = nc.dram_tensor("z", [D_INNER, TH], F32, kind="ExternalInput").ap()
    X_in = nc.dram_tensor("Xfm", [D_INNER, TH], F32, kind="ExternalInput").ap()
    dp_in = nc.dram_tensor("dpar", [128, 32], F32, kind="ExternalInput").ap()
    nw_in = nc.dram_tensor("nw", [128, 32], F32, kind="ExternalInput").ap()
    wo_in = nc.dram_tensor("w_out", [D_INNER, D_MODEL], F32, kind="ExternalInput").ap()
    x0_in = nc.dram_tensor("x0T", [D_MODEL, TH], F32, kind="ExternalInput").ap()
    r0_out = nc.dram_tensor("r0", [D_MODEL, TH], F32, kind="ExternalOutput").ap()

    NB = 32
    ctx, S, sbuf, psum = sched_ctx(nc)
    with ctx:
        t_Y = sbuf("t_Y", [128, TH])
        t_z = sbuf("t_z", [128, TH])
        t_X = sbuf("t_X", [128, TH])
        t_dp = sbuf("t_dp", [128, 32])
        t_nw = sbuf("t_nw", [128, 32])
        t_yf = sbuf("t_yf", [128, NB * TH], BF16)
        t_sq = [sbuf(f"t_sq{i}", [128, TH]) for i in range(2)]
        t_ms = sbuf("t_ms", [1, TH])
        t_one = sbuf("t_one", [1, 128])
        t_onec = sbuf("t_onec", [128, 1])
        t_w = sbuf("t_w", [128, 2 * NB * 128], BF16)
        t_x0 = t_sq[1]
        t_ev = t_sq[0]
        p_st = [psum(f"p_st{i}", [1, 512]) for i in range(4)]
        p_rp = [psum(f"p_rp{i}", [128, 512]) for i in range(2)]
        p_mm = [psum(f"p_mm{i}", [128, 512]) for i in range(2)]

        for s in ("ld", "lwa", "lwb", "vd", "sa", "tp", "st"):
            S.sem(s)

        S.dma("sync", t_dp[:], dp_in, sem="ld")
        S.dma("sync", t_nw[:], nw_in, sem="ld")
        S.op("vector", lambda e: e.memset(t_one[:], 1.0))
        S.op("vector", lambda e: e.memset(t_onec[:], 1.0))
        S.sig("vector", "vd")
        S.wait("tensor", "vd")

        sq_tp = {}
        for blk in range(NB):
            if blk >= 1:
                S.wait("sync", "sa")
            S.dma("sync", t_Y[:], YT_in[blk * 128:(blk + 1) * 128, :], sem="ld")
            S.dma("sync", t_z[:], z_in[blk * 128:(blk + 1) * 128, :], sem="ld")
            S.dma("sync", t_X[:], X_in[blk * 128:(blk + 1) * 128, :], sem="ld")
            S.wait("vector", "ld")
            S.op("vector", lambda e, blk=blk: e.scalar_tensor_tensor(
                t_Y[:], t_X[:], t_dp[:, blk:blk + 1], t_Y[:], ALU.mult, ALU.add))
            S.sig("vector", "vd")
            S.wait("scalar", "ld")
            S.wait("scalar", "vd")
            S.op("scalar", lambda e: e.activation(t_z[:], t_z[:], AF.Silu))
            S.sig("scalar", "sa")
            S.wait("vector", "sa")
            sqb = t_sq[blk % 2]
            if blk >= 2:
                S.wait("vector", "tp", sq_tp[blk - 2])
            S.op("vector", lambda e, sqb=sqb: e.tensor_tensor(
                sqb[:], t_Y[:], t_z[:], ALU.mult))
            S.sig("vector", "vd")
            S.wait("scalar", "vd")
            S.op("scalar", lambda e, sqb=sqb, blk=blk: e.copy(
                t_yf[:, blk * TH:(blk + 1) * TH], sqb[:]))
            S.drain("scalar")
            S.op("scalar", lambda e, sqb=sqb: e.activation(sqb[:], sqb[:], AF.Square))
            S.sig("scalar", "sa")
            S.wait("tensor", "sa")
            for tt_i in range(4):
                S.op("tensor", lambda e, sqb=sqb, tt_i=tt_i, blk=blk: nc.tensor.matmul(
                    p_st[tt_i][:, :], t_onec[:, :],
                    sqb[:, tt_i * 512:(tt_i + 1) * 512],
                    start=(blk == 0), stop=(blk == NB - 1), skip_group_check=True))
            S.sig("tensor", "tp")
            sq_tp[blk] = S.count("tp")

        S.wait("vector", "tp")
        for tt_i in range(4):
            S.op("vector", lambda e, tt_i=tt_i: e.tensor_scalar(
                t_ms[:, tt_i * 512:(tt_i + 1) * 512], p_st[tt_i][:, :],
                1.0 / D_INNER, 1e-5, ALU.mult, ALU.add))
        S.drain("vector")
        S.op("vector", lambda e: e.reciprocal(t_ms[:], t_ms[:]))
        S.sig("vector", "vd")
        S.wait("scalar", "vd")
        S.op("scalar", lambda e: e.activation(t_ms[:], t_ms[:], AF.Sqrt))
        S.sig("scalar", "sa")

        S.wait("tensor", "sa")
        yn_vd = {}
        rep_done = {}
        for tt_i in range(4):
            if tt_i >= 2:
                S.wait("tensor", "vd", rep_done[tt_i - 2])
            S.op("tensor", lambda e, tt_i=tt_i: nc.tensor.matmul(
                p_rp[tt_i % 2][:, :], t_one[:, :], t_ms[:, tt_i * 512:(tt_i + 1) * 512],
                start=True, stop=True))
            S.sig("tensor", "tp")
            S.wait("vector", "tp")
            for blk in range(NB):
                sl = slice(blk * TH + tt_i * 512, blk * TH + (tt_i + 1) * 512)
                S.op("vector", lambda e, sl=sl, tt_i=tt_i: e.tensor_tensor(
                    t_yf[:, sl], t_yf[:, sl], p_rp[tt_i % 2][:, :], ALU.mult))
                S.drain("vector")
                S.op("vector", lambda e, sl=sl, blk=blk: e.tensor_scalar(
                    t_yf[:, sl], t_yf[:, sl], t_nw[:, blk:blk + 1], None, ALU.mult))
                S.drain("vector")
            rep_done[tt_i] = S.sig("vector", "vd")
        yn_vd[NB - 1] = S.count("vd")

        ev_vd = {}
        nmm = 0
        for do in range(16):
            sw = do % 2
            if do >= 2:
                S.wait("gpsimd", "tp", mm_tp[do - 2])
            S.dma("gpsimd", t_w[:, sw * NB * 128:(sw + 1) * NB * 128]
                  .rearrange("p (a j) -> p a j", j=128),
                  wo_in[:, do * 128:(do + 1) * 128].rearrange("(a p) j -> p a j", p=128),
                  sem=("lwa" if sw == 0 else "lwb"))
            if do == 0:
                S.wait("sync", "tp", sq_tp[NB - 1])
            if do >= 1:
                S.wait("sync", "vd", ev_vd[do - 1])
            S.dma("sync", t_x0[:], x0_in[do * 128:(do + 1) * 128, :], sem="ld")
            S.wait("tensor", "lwa" if sw == 0 else "lwb", 16 * (do // 2 + 1))
            S.wait("tensor", "vd", yn_vd[NB - 1])
            if do >= 1:
                S.wait("vector", "st", st_after_c[do - 1])
            for tt_i in range(4):
                pm = p_mm[nmm % 2]
                if nmm >= 2:
                    S.wait("tensor", "vd", mm_vd[nmm - 2])
                for k in range(NB):
                    S.op("tensor", lambda e, pm=pm, sw=sw, k=k, tt_i=tt_i:
                         nc.tensor.matmul(pm[:, :],
                                          t_w[:, sw * NB * 128 + k * 128:(sw) * NB * 128 + (k + 1) * 128],
                                          t_yf[:, k * TH + tt_i * 512: k * TH + tt_i * 512 + 512],
                                          start=(k == 0), stop=(k == NB - 1)))
                S.sig("tensor", "tp")
                S.wait("vector", "tp")
                S.wait("vector", "ld")
                S.op("vector", lambda e, pm=pm, tt_i=tt_i: e.tensor_tensor(
                    t_ev[:, tt_i * 512:(tt_i + 1) * 512], pm[:, :], t_x0[:, tt_i * 512:(tt_i + 1) * 512], ALU.add))
                v = S.sig("vector", "vd")
                if nmm == 0:
                    mm_vd = {}
                mm_vd[nmm] = v
                nmm += 1
            ev_vd[do] = S.count("vd")
            if do == 0:
                mm_tp = {}
            mm_tp[do] = S.count("tp")
            S.wait("gpsimd", "vd", ev_vd[do])
            if do == 0:
                st_after_c = {}
            st_after_c[do] = S.dma("gpsimd", r0_out[do * 128:(do + 1) * 128, :],
                                   t_ev[:], sem="st")

        S.wait("gpsimd", "st")
        S.build()
    return nc


def kernel(**inputs):
    raise NotImplementedError("under construction")


def build_D(NTG=16):
    nc = make_nc()
    NSL = 2048
    xnT_in = nc.dram_tensor("xnT", [D_MODEL, 16384], BF16, kind="ExternalInput").ap()
    gpk_in = nc.dram_tensor("gpk", [NSL, 1024], I32, kind="ExternalInput").ap()
    gsc_in = nc.dram_tensor("gsc", [NSL, 64], F32, kind="ExternalInput").ap()
    upk_in = nc.dram_tensor("upk", [NSL, 1024], I32, kind="ExternalInput").ap()
    usc_in = nc.dram_tensor("usc", [NSL, 64], F32, kind="ExternalInput").ap()
    dpk_in = nc.dram_tensor("dpk", [D_MODEL, 1024], I32, kind="ExternalInput").ap()
    dsc_in = nc.dram_tensor("dsc", [D_MODEL, 64], F32, kind="ExternalInput").ap()
    idb_in = nc.dram_tensor("idb", [128, 128], BF16, kind="ExternalInput").ap()
    part_out = nc.dram_tensor("part", [D_MODEL, NTG * 1024], F32, kind="ExternalOutput").ap()
    wgt_s = nc.dram_tensor("wgt_s", [D_MODEL, NSL], BF16, kind="Internal").ap()
    wut_s = nc.dram_tensor("wut_s", [D_MODEL, NSL], BF16, kind="Internal").ap()
    wdt_s = nc.dram_tensor("wdt_s", [NSL, D_MODEL], BF16, kind="Internal").ap()

    C1 = 0.044715
    C2 = 0.7978845608028654
    ctx, S, sbuf, psum = sched_ctx(nc)
    with ctx:
        t_pk = sbuf("t_pk", [128, 2 * 1024], I32)
        t_sc = sbuf("t_sc", [128, 64])
        t_w1 = sbuf("t_w1", [128, 1024], I32)
        t_w2 = sbuf("t_w2", [128, 1024], I32)
        t_w3 = sbuf("t_w3", [128, 1024], I32)
        t_dq = [sbuf(f"t_dq{i}", [128, D_MODEL], BF16) for i in range(2)]
        t_id = sbuf("t_id", [128, 128], BF16)
        t_tr = [sbuf(f"t_tr{i}", [128, 128], BF16) for i in range(2)]
        t_xn = sbuf("t_xn", [128, 16 * 1024], BF16)
        t_it = sbuf("t_it", [128, 16 * 1024], BF16)
        t_ws = [sbuf(f"t_ws{i}", [128, 16 * 128], BF16) for i in range(4)]
        t_g = sbuf("t_g", [128, 512])
        t_s1 = sbuf("t_s1", [128, 512])
        t_s3 = sbuf("t_s3", [128, 512])
        t_ev = sbuf("t_ev", [128, 2 * 512])
        p_t = [psum(f"p_t{i}", [128, 1024], BF16) for i in range(2)]
        p_g = [psum(f"p_g{i}", [128, 512]) for i in range(2)]
        p_u = [psum(f"p_u{i}", [128, 512]) for i in range(2)]
        p_d = [psum(f"p_d{i}", [128, 512]) for i in range(2)]

        for s in ("ld", "lw0", "lw1", "lw2", "lw3", "lx", "vd", "sa", "tp", "st", "sc_"):
            S.sem(s)

        S.dma("sync", t_id[:], idb_in, sem="ld")

        def decode(pk_in, sc_in, scr, scale_mult, tag, _g={"kt": 0, "vd": {}}):
            for rt in range(16):
                sl = rt % 2
                pkv = t_pk[:, sl * 1024:(sl + 1) * 1024]
                S.wait("sync", "vd")
                S.dma("sync", pkv, pk_in[rt * 128:(rt + 1) * 128, :], sem="ld")
                S.dma("sync", t_sc[:], sc_in[rt * 128:(rt + 1) * 128, :], sem="ld")
                S.wait("vector", "ld")
                dq = t_dq[rt % 2]
                if rt >= 2:
                    S.wait("vector", "tp")
                S.op("vector", lambda e: e.tensor_scalar(
                    t_sc[:], t_sc[:], scale_mult, None, ALU.mult))
                S.op("vector", lambda e, pkv=pkv: e.tensor_scalar(
                    t_w1[:], pkv, 7, 22, ALU.bitwise_and, ALU.logical_shift_left))
                S.op("vector", lambda e, pkv=pkv: e.tensor_scalar(
                    t_w2[:], pkv, 8, 28, ALU.bitwise_and, ALU.logical_shift_left))
                S.drain("vector")
                S.op("vector", lambda e: e.tensor_tensor(
                    t_w1[:], t_w1[:], t_w2[:], ALU.bitwise_or))
                S.drain("vector")
                S.op("vector", lambda e, dq=dq: e.tensor_tensor(
                    dq[:, 0:D_MODEL:2].rearrange("p (a b) -> p a b", a=64),
                    t_w1[:].bitcast(F32).rearrange("p (a b) -> p a b", a=64),
                    t_sc[:, :, None].broadcast_to([128, 64, 16]), ALU.mult))
                S.op("vector", lambda e, pkv=pkv: e.tensor_scalar(
                    t_w3[:], pkv, 4, 15, ALU.logical_shift_right, ALU.bitwise_and))
                S.drain("vector")
                S.op("vector", lambda e: e.tensor_scalar(
                    t_w1[:], t_w3[:], 7, 22, ALU.bitwise_and, ALU.logical_shift_left))
                S.op("vector", lambda e: e.tensor_scalar(
                    t_w2[:], t_w3[:], 8, 28, ALU.bitwise_and, ALU.logical_shift_left))
                S.drain("vector")
                S.op("vector", lambda e: e.tensor_tensor(
                    t_w1[:], t_w1[:], t_w2[:], ALU.bitwise_or))
                S.drain("vector")
                S.op("vector", lambda e, dq=dq: e.tensor_tensor(
                    dq[:, 1:D_MODEL:2].rearrange("p (a b) -> p a b", a=64),
                    t_w1[:].bitcast(F32).rearrange("p (a b) -> p a b", a=64),
                    t_sc[:, :, None].broadcast_to([128, 64, 16]), ALU.mult))
                S.sig("vector", "vd")
                S.wait("tensor", "vd")
                for kt in range(16):
                    gkt = _g["kt"]; _g["kt"] += 1
                    pt = p_t[gkt % 2]
                    if gkt >= 2:
                        S.wait("tensor", "vd", _g["vd"][gkt - 2])
                    S.op("tensor", lambda e, dq=dq, kt=kt, pt=pt: nc.tensor.transpose(
                        pt[:, 0:128], dq[:, kt * 128:(kt + 1) * 128], t_id[:]))
                    S.sig("tensor", "tp")
                    S.wait("vector", "tp")
                    trb = t_tr[gkt % 2]
                    if gkt >= 2:
                        S.wait("vector", "st")
                    S.op("vector", lambda e, trb=trb, pt=pt: e.tensor_copy(trb[:], pt[:, 0:128]))
                    v = S.sig("vector", "vd")
                    _g["vd"][gkt] = v
                    S.wait("gpsimd", "vd", v)
                    S.dma("gpsimd", scr[kt * 128:(kt + 1) * 128, rt * 128:(rt + 1) * 128],
                          trb[:], sem="st")
                S.sig("vector", "vd")

        decode(gpk_in, gsc_in, wgt_s, float(2.0 ** 126), "g")
        decode(upk_in, usc_in, wut_s, float(2.0 ** 126), "u")
        decode(dpk_in, dsc_in, wdt_s, float(0.5 * 2.0 ** 126), "d")
        ph1_st = S.count("st")
        ph1_vd = S.count("vd")

        S.wait("sync", "st", ph1_st)
        S.wait("scalar", "st", ph1_st)
        rec = {"it": {}, "gu": {}, "pd": {}, "ev": {}}
        ws_done = {}
        nws = 0
        nmm = 0
        for tg in range(NTG):
            if tg >= 1:
                S.wait("sync", "tp", rec["gu"][tg - 1])
            S.dma("sync", t_xn[:].rearrange("p (a t) -> p a t", t=1024),
                  xnT_in[:, tg * 1024:(tg + 1) * 1024].rearrange("(a p) t -> p a t", p=128),
                  sem="lx")
            for ntile in range(16):
                ig = nws; wsg = t_ws[nws % 4]; nws += 1
                iu = nws; wsu = t_ws[nws % 4]; nws += 1
                gsem = f"lw{ig % 4}"
                usem = f"lw{iu % 4}"
                if ig - 4 in ws_done:
                    S.wait("scalar", "tp", ws_done[ig - 4])
                if iu - 4 in ws_done:
                    S.wait("scalar", "tp", ws_done[iu - 4])
                S.dma("scalar", wsg[:].rearrange("p (a n) -> p a n", n=128),
                      wgt_s[:, ntile * 128:(ntile + 1) * 128].rearrange("(a p) n -> p a n", p=128),
                      sem=gsem)
                S.dma("scalar", wsu[:].rearrange("p (a n) -> p a n", n=128),
                      wut_s[:, ntile * 128:(ntile + 1) * 128].rearrange("(a p) n -> p a n", p=128),
                      sem=usem)
                S.wait("tensor", gsem)
                S.wait("tensor", usem)
                S.wait("tensor", "lx")
                for tt2 in range(2):
                    pg = p_g[tt2]
                    pu = p_u[tt2]
                    if nmm >= 1:
                        S.wait("tensor", "vd", rec["ev"].get(nmm - 1, 0))
                    for k in range(16):
                        S.op("tensor", lambda e, pg=pg, wsg=wsg, k=k, tt2=tt2:
                             nc.tensor.matmul(pg[:, :], wsg[:, k * 128:(k + 1) * 128],
                                              t_xn[:, k * 1024 + tt2 * 512: k * 1024 + tt2 * 512 + 512],
                                              start=(k == 0), stop=(k == 15)))
                    for k in range(16):
                        S.op("tensor", lambda e, pu=pu, wsu=wsu, k=k, tt2=tt2:
                             nc.tensor.matmul(pu[:, :], wsu[:, k * 128:(k + 1) * 128],
                                              t_xn[:, k * 1024 + tt2 * 512: k * 1024 + tt2 * 512 + 512],
                                              start=(k == 0), stop=(k == 15)))
                    S.sig("tensor", "tp")
                    S.wait("scalar", "tp")
                    S.op("scalar", lambda e, pg=pg: e.copy(t_g[:], pg[:, :]))
                    S.drain("scalar")
                    S.op("scalar", lambda e: e.activation(t_s1[:], t_g[:], AF.Square))
                    S.sig("scalar", "sa")
                    S.wait("vector", "sa")
                    S.op("vector", lambda e: e.tensor_scalar(
                        t_s1[:], t_s1[:], C1, 1.0, ALU.mult, ALU.add))
                    S.drain("vector")
                    S.op("vector", lambda e: e.tensor_tensor(t_s3[:], t_s1[:], t_g[:], ALU.mult))
                    S.sig("vector", "vd")
                    S.wait("scalar", "vd")
                    S.op("scalar", lambda e: e.activation(t_s3[:], t_s3[:], AF.Tanh, scale=C2))
                    S.sig("scalar", "sa")
                    S.wait("vector", "sa")
                    S.op("vector", lambda e, pu=pu: e.tensor_tensor(t_s1[:], t_g[:], pu[:, :], ALU.mult))
                    S.drain("vector")
                    S.op("vector", lambda e, ntile=ntile, tt2=tt2: e.scalar_tensor_tensor(
                        t_it[:, ntile * 1024 + tt2 * 512: ntile * 1024 + tt2 * 512 + 512],
                        t_s3[:], 1.0, t_s1[:], ALU.add, ALU.mult))
                    v = S.sig("vector", "vd")
                    rec["ev"][nmm] = v
                    nmm += 1
                ws_done[ig] = ws_done[iu] = S.count("tp")
            rec["gu"][tg] = S.count("tp")
            it_vd = S.count("vd")

            S.wait("tensor", "vd", it_vd)
            for htile in range(16):
                idd = nws; wsd = t_ws[nws % 4]; nws += 1
                dsem = f"lw{idd % 4}"
                if idd - 4 in ws_done:
                    S.wait("scalar", "tp", ws_done[idd - 4])
                S.dma("scalar", wsd[:].rearrange("p (a h) -> p a h", h=128),
                      wdt_s[:, htile * 128:(htile + 1) * 128].rearrange("(a p) h -> p a h", p=128),
                      sem=dsem)
                S.wait("tensor", dsem)
                for tt2 in range(2):
                    pd = p_d[tt2]
                    if (htile, tt2) != (0, 0):
                        S.wait("tensor", "vd")
                    for k in range(16):
                        S.op("tensor", lambda e, pd=pd, wsd=wsd, k=k, tt2=tt2:
                             nc.tensor.matmul(pd[:, :], wsd[:, k * 128:(k + 1) * 128],
                                              t_it[:, k * 1024 + tt2 * 512: k * 1024 + tt2 * 512 + 512],
                                              start=(k == 0), stop=(k == 15)))
                    S.sig("tensor", "tp")
                    S.wait("vector", "tp")
                    if (htile, tt2) not in ((0, 0), (0, 1)):
                        S.wait("vector", "st")
                    S.op("vector", lambda e, pd=pd, tt2=tt2: e.tensor_copy(
                        t_ev[:, tt2 * 512:(tt2 + 1) * 512], pd[:, :]))
                    S.sig("vector", "vd")
                    S.wait("gpsimd", "vd")
                    S.dma("gpsimd", part_out[htile * 128:(htile + 1) * 128,
                                             tg * 1024 + tt2 * 512: tg * 1024 + tt2 * 512 + 512],
                          t_ev[:, tt2 * 512:(tt2 + 1) * 512], sem="st")
                ws_done[idd] = S.count("tp")

        S.wait("gpsimd", "st")
        S.build()
    return nc


def shard_D(xnT_full, inputs):
    gpk = np.asarray(inputs["L0_ffn_gate_W_packed"], np.int32)
    gsc = np.asarray(inputs["L0_ffn_gate_scales"], np.float32)
    upk = np.asarray(inputs["L0_ffn_up_W_packed"], np.int32)
    usc = np.asarray(inputs["L0_ffn_up_scales"], np.float32)
    dpk = np.asarray(inputs["L0_ffn_down_W_packed"], np.int32)
    dsc = np.asarray(inputs["L0_ffn_down_scales"], np.float32)
    idb = id128_bf()
    maps = []
    for c in range(NCORES):
        maps.append(dict(
            xnT=xnT_full,
            gpk=np.ascontiguousarray(gpk[c * 2048:(c + 1) * 2048]),
            gsc=np.ascontiguousarray(gsc[c * 2048:(c + 1) * 2048]),
            upk=np.ascontiguousarray(upk[c * 2048:(c + 1) * 2048]),
            usc=np.ascontiguousarray(usc[c * 2048:(c + 1) * 2048]),
            dpk=np.ascontiguousarray(dpk[:, c * 1024:(c + 1) * 1024]),
            dsc=np.ascontiguousarray(dsc[:, c * 64:(c + 1) * 64]),
            idb=idb))
    return maps




def build_E():
    nc = make_nc()
    r0_in = nc.dram_tensor("r0", [D_MODEL, TH], F32, kind="ExternalInput").ap()
    parts_in = [nc.dram_tensor(f"pp{i}", [D_MODEL, TH], F32, kind="ExternalInput").ap()
                for i in range(8)]
    wbcT_in = nc.dram_tensor("wbcT", [D_MODEL, 128], F32, kind="ExternalInput").ap()
    wdtT_in = nc.dram_tensor("wdtT", [DT_RANK, D_MODEL], F32, kind="ExternalInput").ap()
    bdt_in = nc.dram_tensor("bdt", [128, 16], F32, kind="ExternalInput").ap()
    rw_in = nc.dram_tensor("rw", [D_MODEL, NEXP], F32, kind="ExternalInput").ap()
    rb_in = nc.dram_tensor("rb", [NEXP, 1], F32, kind="ExternalInput").ap()
    idf_in = nc.dram_tensor("idf", [128, 128], F32, kind="ExternalInput").ap()
    x1_out = nc.dram_tensor("x1", [D_MODEL, TH], F32, kind="ExternalOutput").ap()
    xn1_out = nc.dram_tensor("xn1", [D_MODEL, TH], F32, kind="ExternalOutput").ap()
    dl_out = nc.dram_tensor("dl", [D_MODEL, TH], F32, kind="ExternalOutput").ap()
    bc_out = nc.dram_tensor("bc", [64, TH], F32, kind="ExternalOutput").ap()
    s_out = nc.dram_tensor("s", [TH, NEXP], F32, kind="ExternalOutput").ap()

    ctx, S, sbuf, psum = sched_ctx(nc)
    with ctx:
        t_a = sbuf("t_a", [128, TH])
        t_b = sbuf("t_b", [128, TH])
        t_sqs = [sbuf(f"t_sqs{i}", [128, TH]) for i in range(2)]
        t_arep = sbuf("t_arep", [128, TH])
        t_brep = sbuf("t_brep", [128, TH])
        t_row = sbuf("t_row", [1, 2 * TH])
        t_one = sbuf("t_one", [1, 128])
        t_onec = sbuf("t_onec", [128, 1])
        t_xnb = sbuf("t_xnb", [128, 16 * TH], BF16)
        t_wbc = sbuf("t_wbc", [128, 16 * 128], BF16)
        t_wdt = sbuf("t_wdt", [DT_RANK, 16 * 128], BF16)
        t_dtr = sbuf("t_dtr", [DT_RANK, TH], BF16)
        t_bdt = sbuf("t_bdt", [128, 16])
        t_rw = sbuf("t_rw", [128, 16 * NEXP], BF16)
        t_rb = sbuf("t_rb", [NEXP, 1])
        t_id = sbuf("t_id", [128, 128])
        t_sp1 = sbuf("t_sp1", [128, 512])
        t_sp2 = sbuf("t_sp2", [128, 512])
        t_sm = sbuf("t_sm", [128, 16 * 16])
        t_mx = sbuf("t_mx", [128, 2])
        p_s1 = psum("p_s1", [1, 512])
        p_s2 = psum("p_s2", [1, 512])
        p_a = [psum(f"p_a{i}", [128, 512]) for i in range(2)]
        p_b = [psum(f"p_b{i}", [128, 512]) for i in range(2)]

        for s in ("ld", "lp", "vd", "sa", "tp", "st"):
            S.sem(s)

        S.dma("sync", t_id[:], idf_in, sem="ld")
        S.dma("sync", t_bdt[:], bdt_in, sem="ld")
        S.dma("sync", t_rb[:], rb_in, sem="ld")
        S.dma("gpsimd", t_wbc[:].rearrange("p (a j) -> p a j", j=128),
              wbcT_in.rearrange("(a p) j -> p a j", p=128), sem="lp")
        S.dma("gpsimd", t_wdt[:].rearrange("p (a j) -> p a j", j=128),
              wdtT_in.rearrange("p (a j) -> p a j", j=128), sem="lp")
        S.dma("gpsimd", t_rw[:].rearrange("p (a j) -> p a j", j=NEXP),
              rw_in.rearrange("(a p) j -> p a j", p=128), sem="lp")
        S.op("vector", lambda e: e.memset(t_one[:], 1.0))
        S.op("vector", lambda e: e.memset(t_onec[:], 1.0))
        S.op("vector", lambda e: e.memset(t_row[:], 0.0))
        S.sig("vector", "vd")
        S.wait("tensor", "vd")

        st_p1 = {}
        for blk in range(16):
            if blk >= 1:
                S.wait("sync", "tp", st_p1[blk - 1])
                S.wait("sync", "st")
            S.dma("sync", t_a[:], r0_in[blk * 128:(blk + 1) * 128, :], sem="ld")
            for i in range(8):
                S.dma("sync", t_b[:], parts_in[i][blk * 128:(blk + 1) * 128, :], sem="ld")
                S.wait("vector", "ld")
                S.op("vector", lambda e: e.tensor_tensor(t_a[:], t_a[:], t_b[:], ALU.add))
                S.sig("vector", "vd")
                S.wait("sync", "vd")
            S.wait("gpsimd", "vd")
            S.dma("gpsimd", x1_out[blk * 128:(blk + 1) * 128, :], t_a[:], sem="st")
            sqb = t_sqs[blk % 2]
            if blk >= 2:
                S.wait("scalar", "tp", st_p1[blk - 2])
            S.wait("scalar", "vd")
            S.op("scalar", lambda e, sqb=sqb: e.activation(sqb[:], t_a[:], AF.Square))
            S.sig("scalar", "sa")
            S.wait("tensor", "sa")
            for tt_i in range(4):
                if blk > 0 or tt_i > 0:
                    S.wait("tensor", "vd")
                S.op("tensor", lambda e, tt_i=tt_i: nc.tensor.matmul(
                    p_s1[:, :], t_onec[:, :], t_a[:, tt_i * 512:(tt_i + 1) * 512],
                    start=True, stop=True))
                S.op("tensor", lambda e, sqb=sqb, tt_i=tt_i: nc.tensor.matmul(
                    p_s2[:, :], t_onec[:, :], sqb[:, tt_i * 512:(tt_i + 1) * 512],
                    start=True, stop=True))
                S.sig("tensor", "tp")
                S.wait("vector", "tp")
                sl2 = slice(TH + tt_i * 512, TH + (tt_i + 1) * 512)
                sl = slice(tt_i * 512, (tt_i + 1) * 512)
                S.op("vector", lambda e, sl2=sl2: e.tensor_tensor(
                    t_row[0:1, sl2], t_row[0:1, sl2], p_s1[:, :], ALU.add))
                S.op("vector", lambda e, sl=sl: e.tensor_tensor(
                    t_row[0:1, sl], t_row[0:1, sl], p_s2[:, :], ALU.add))
                S.sig("vector", "vd")
            S.sig("tensor", "tp")
            st_p1[blk] = S.count("tp")

        S.drain("vector")
        S.op("vector", lambda e: e.tensor_scalar(
            t_row[0:1, 0:2 * TH], t_row[0:1, 0:2 * TH], 1.0 / D_MODEL, None, ALU.mult))
        S.drain("vector")
        S.op("vector", lambda e: e.tensor_tensor(
            t_b[0:1, 0:TH], t_row[0:1, TH:2 * TH], t_row[0:1, TH:2 * TH], ALU.mult))
        S.drain("vector")
        S.op("vector", lambda e: e.tensor_tensor(
            t_row[0:1, 0:TH], t_row[0:1, 0:TH], t_b[0:1, 0:TH], ALU.subtract))
        S.drain("vector")
        S.op("vector", lambda e: e.tensor_scalar(
            t_row[0:1, 0:TH], t_row[0:1, 0:TH], 1.0, 1e-5, ALU.mult, ALU.add))
        S.drain("vector")
        S.op("vector", lambda e: e.reciprocal(t_row[0:1, 0:TH], t_row[0:1, 0:TH]))
        S.sig("vector", "vd")
        S.wait("scalar", "vd")
        S.op("scalar", lambda e: e.activation(t_row[0:1, 0:TH], t_row[0:1, 0:TH], AF.Sqrt))
        S.sig("scalar", "sa")
        S.wait("vector", "sa")
        S.op("vector", lambda e: e.tensor_tensor(
            t_row[0:1, TH:2 * TH], t_row[0:1, TH:2 * TH], t_row[0:1, 0:TH], ALU.mult))
        S.drain("vector")
        S.op("vector", lambda e: e.tensor_scalar(
            t_row[0:1, TH:2 * TH], t_row[0:1, TH:2 * TH], -1.0, None, ALU.mult))
        S.sig("vector", "vd")
        rows_vd = S.count("vd")

        S.wait("tensor", "vd", rows_vd)
        rep_tp = {}
        for tt_i in range(4):
            if tt_i >= 2:
                S.wait("tensor", "vd", rep_ev[tt_i - 2])
            S.op("tensor", lambda e, tt_i=tt_i: nc.tensor.matmul(
                p_a[tt_i % 2][:, :], t_one[:, :], t_row[0:1, tt_i * 512:(tt_i + 1) * 512],
                start=True, stop=True))
            S.op("tensor", lambda e, tt_i=tt_i: nc.tensor.matmul(
                p_b[tt_i % 2][:, :], t_one[:, :], t_row[0:1, TH + tt_i * 512:TH + (tt_i + 1) * 512],
                start=True, stop=True))
            S.sig("tensor", "tp")
            rep_tp[tt_i] = S.count("tp")
            S.wait("vector", "tp", rep_tp[tt_i])
            S.op("vector", lambda e, tt_i=tt_i: e.tensor_copy(
                t_arep[:, tt_i * 512:(tt_i + 1) * 512], p_a[tt_i % 2][:, :]))
            S.op("vector", lambda e, tt_i=tt_i: e.tensor_copy(
                t_brep[:, tt_i * 512:(tt_i + 1) * 512], p_b[tt_i % 2][:, :]))
            v = S.sig("vector", "vd")
            if tt_i == 0:
                rep_ev = {}
            rep_ev[tt_i] = v
        rep_all = S.count("vd")
        xn_vd = {}
        for blk in range(16):
            if blk == 0:
                S.wait("sync", "tp", st_p1[15])
                S.wait("sync", "st")
            if blk >= 1:
                S.wait("sync", "vd", xn_vd[blk - 1])
            S.dma("sync", t_a[:], x1_out[blk * 128:(blk + 1) * 128, :], sem="ld")
            S.wait("vector", "ld")
            S.wait("vector", "vd", rep_all)
            S.wait("vector", "st")
            S.op("vector", lambda e: e.tensor_tensor(t_b[:], t_a[:], t_arep[:], ALU.mult))
            S.drain("vector")
            S.op("vector", lambda e: e.tensor_tensor(t_b[:], t_b[:], t_brep[:], ALU.add))
            S.drain("vector")
            S.op("vector", lambda e, blk=blk: e.tensor_copy(
                t_xnb[:, blk * TH:(blk + 1) * TH], t_b[:]))
            xn_vd[blk] = S.sig("vector", "vd")
            S.wait("gpsimd", "vd", xn_vd[blk])
            S.dma("gpsimd", xn1_out[blk * 128:(blk + 1) * 128, :], t_b[:], sem="st")
        xn_all = S.count("vd")

        S.wait("tensor", "vd", xn_all)
        S.wait("tensor", "lp")
        dbc_tp = {}
        for tt_i in range(4):
            pm = p_a[tt_i % 2]
            if tt_i >= 2:
                S.wait("tensor", "vd", dbc_vd[tt_i - 2])
            for k in range(16):
                S.op("tensor", lambda e, pm=pm, k=k, tt_i=tt_i: nc.tensor.matmul(
                    pm[:, :], t_wbc[:, k * 128:(k + 1) * 128],
                    t_xnb[:, k * TH + tt_i * 512: k * TH + tt_i * 512 + 512],
                    start=(k == 0), stop=(k == 15)))
            S.sig("tensor", "tp")
            S.wait("vector", "tp")
            sl = slice(tt_i * 512, (tt_i + 1) * 512)
            S.op("vector", lambda e, pm=pm, sl=sl: e.tensor_copy(t_dtr[:, sl], pm[0:64, :]))
            S.op("vector", lambda e, pm=pm, sl=sl: e.tensor_copy(t_sqs[0][0:64, sl], pm[64:128, :]))
            v = S.sig("vector", "vd")
            if tt_i == 0:
                dbc_vd = {}
            dbc_vd[tt_i] = v
        S.wait("gpsimd", "vd", S.count("vd"))
        S.dma("gpsimd", bc_out[:, :], t_sqs[0][0:64, 0:TH], sem="st")

        dtr_all = S.count("vd")
        S.wait("tensor", "vd", dtr_all)
        for dtile in range(16):
            for tt_i in range(4):
                pm = p_b[(dtile * 4 + tt_i) % 2]
                if dtile * 4 + tt_i >= 2:
                    S.wait("tensor", "vd", dt_vd[dtile * 4 + tt_i - 2])
                S.op("tensor", lambda e, pm=pm, dtile=dtile, tt_i=tt_i: nc.tensor.matmul(
                    pm[:, :], t_wdt[:, dtile * 128:(dtile + 1) * 128],
                    t_dtr[:, tt_i * 512:(tt_i + 1) * 512],
                    start=True, stop=True))
                S.sig("tensor", "tp")
                S.wait("vector", "tp")
                S.op("vector", lambda e, pm=pm, dtile=dtile: e.tensor_scalar(
                    t_sp1[:, :], pm[:, :], t_bdt[:, dtile:dtile + 1], None, ALU.add))
                v = S.sig("vector", "vd")
                if dtile == 0 and tt_i == 0:
                    dt_vd = {}
                dt_vd[dtile * 4 + tt_i] = v
                S.wait("scalar", "vd", v)
                S.op("scalar", lambda e: e.activation(t_sp2[:, :], t_sp1[:, :], AF.Abs))
                S.drain("scalar")
                S.op("scalar", lambda e: e.activation(t_sp2[:, :], t_sp2[:, :], AF.Exp, scale=-1.0))
                S.drain("scalar")
                S.op("scalar", lambda e: e.activation(t_sp2[:, :], t_sp2[:, :], AF.Ln, bias=1.0))
                S.drain("scalar")
                S.op("scalar", lambda e: e.activation(t_sp1[:, :], t_sp1[:, :], AF.Relu))
                S.sig("scalar", "sa")
                S.wait("vector", "sa")
                S.op("vector", lambda e: e.tensor_tensor(t_sp1[:, :], t_sp1[:, :], t_sp2[:, :], ALU.add))
                v2 = S.sig("vector", "vd")
                S.wait("gpsimd", "vd", v2)
                S.dma("gpsimd", dl_out[dtile * 128:(dtile + 1) * 128,
                                       tt_i * 512:(tt_i + 1) * 512], t_sp1[:, :], sem="st")
                S.wait("vector", "st")
                S.wait("scalar", "st")

        S.wait("tensor", "vd", S.count("vd"))
        for tt_i in range(4):
            pm = p_a[tt_i % 2]
            if tt_i >= 2:
                S.wait("tensor", "vd", rt_vd[tt_i - 2])
            for k in range(16):
                S.op("tensor", lambda e, pm=pm, k=k, tt_i=tt_i: nc.tensor.matmul(
                    pm[0:NEXP, :], t_rw[:, k * NEXP:(k + 1) * NEXP],
                    t_xnb[:, k * TH + tt_i * 512: k * TH + tt_i * 512 + 512],
                    start=(k == 0), stop=(k == 15)))
            S.sig("tensor", "tp")
            S.wait("vector", "tp")
            S.op("vector", lambda e, pm=pm: e.tensor_scalar(
                t_sp1[0:NEXP, :], pm[0:NEXP, :], t_rb[:, 0:1], None, ALU.add))
            v = S.sig("vector", "vd")
            if tt_i == 0:
                rt_vd = {}
            rt_vd[tt_i] = v
            for q in range(4):
                gq = tt_i * 4 + q
                S.wait("tensor", "vd", v)
                if gq >= 2:
                    S.wait("tensor", "vd", rtr_ev[gq - 2])
                S.op("tensor", lambda e, q=q, tt_i=tt_i, gq=gq: nc.tensor.transpose(
                    p_b[gq % 2][0:128, 0:NEXP], t_sp1[0:NEXP, q * 128:(q + 1) * 128],
                    t_id[0:NEXP, 0:NEXP]))
                S.sig("tensor", "tp")
                S.wait("vector", "tp")
                ti = tt_i * 4 + q
                S.op("vector", lambda e, gq=gq, ti=ti: e.tensor_copy(
                    t_sm[:, ti * 16:(ti + 1) * 16], p_b[gq % 2][0:128, 0:NEXP]))
                ve = S.sig("vector", "vd")
                if gq == 0:
                    rtr_ev = {}
                rtr_ev[gq] = ve
        for ti in range(16):
            sm = t_sm[:, ti * 16:(ti + 1) * 16]
            S.drain("vector")
            S.op("vector", lambda e, sm=sm: e.reduce_max(t_mx[:, 0:1], sm, axis=AX.X))
            S.drain("vector")
            S.op("vector", lambda e, sm=sm: e.tensor_scalar(
                sm, sm, t_mx[:, 0:1], None, ALU.subtract))
            S.sig("vector", "vd")
            S.wait("scalar", "vd")
            S.op("scalar", lambda e, sm=sm: e.activation(sm, sm, AF.Exp))
            S.sig("scalar", "sa")
            S.wait("vector", "sa")
            S.op("vector", lambda e, sm=sm: e.reduce_sum(t_mx[:, 1:2], sm, axis=AX.X))
            S.drain("vector")
            S.op("vector", lambda e: e.reciprocal(t_mx[:, 1:2], t_mx[:, 1:2]))
            S.drain("vector")
            S.op("vector", lambda e, sm=sm: e.tensor_scalar(
                sm, sm, t_mx[:, 1:2], None, ALU.mult))
            v = S.sig("vector", "vd")
            S.wait("gpsimd", "vd", v)
            S.dma("gpsimd", s_out[ti * 128:(ti + 1) * 128, :], sm, sem="st")

        S.wait("gpsimd", "st")
        S.build()
    return nc


def build_F(LQ=L_SEQ, BQ=B_SZ):
    nc = make_nc()
    DL = 256
    NBLK = DL // 4
    dT_in = nc.dram_tensor("dT", [BQ, DL, LQ], F32, kind="ExternalInput").ap()
    uT_in = nc.dram_tensor("uT", [BQ, DL, LQ], F32, kind="ExternalInput").ap()
    BT_in = nc.dram_tensor("BTf", [BQ, L1_N, LQ], F32, kind="ExternalInput").ap()
    CT_in = nc.dram_tensor("CTf", [BQ, L1_N, LQ], F32, kind="ExternalInput").ap()
    adn_in = nc.dram_tensor("adn", [128, NBLK], F32, kind="ExternalInput").ap()
    dcol_in = nc.dram_tensor("dcol", [128, 2], F32, kind="ExternalInput").ap()
    G_in = nc.dram_tensor("G", [128, 32 * 128], BF16, kind="ExternalInput").ap()
    y_out = nc.dram_tensor("y", [BQ, DL, LQ], F32, kind="ExternalOutput").ap()
    du_scr = nc.dram_tensor("du_scr", [DL, LQ], F32, kind="Internal").ap()

    HT = LQ // 2
    NQ = max(1, HT // 512)
    QW = HT // NQ
    ctx, S, sbuf, psum = sched_ctx(nc)
    with ctx:
        t_dc = sbuf("t_dc", [128, 2 * LQ])
        t_uc = sbuf("t_uc", [128, 2 * LQ])
        t_ud = sbuf("t_ud", [128, 2 * LQ])
        t_Br = sbuf("t_Br", [128, LQ])
        t_Cr = sbuf("t_Cr", [128, LQ])
        t_dr = sbuf("t_dr", [128, HT])
        t_ur = sbuf("t_ur", [128, HT])
        t_dA = sbuf("t_dA", [128, HT])
        t_dB = sbuf("t_dB", [128, HT])
        t_h = sbuf("t_h", [128, HT])
        t_ch = sbuf("t_ch", [128, HT], BF16)
        t_cr = sbuf("t_cr", [128, NBLK])
        t_am = sbuf("t_am", [128, NBLK])
        t_dcol = sbuf("t_dcol", [128, 2])
        t_G = sbuf("t_G", [128, 32 * 128], BF16)
        t_y = sbuf("t_y", [128, HT])
        t_z = sbuf("t_z", [128, 1])
        p_y = [psum(f"p_y{i}", [128, 512]) for i in range(4)]

        for s in ("ld", "la", "lb", "lg", "lr", "lu", "vd", "sa", "tp", "st", "du"):
            S.sem(s)

        S.dma("sync", t_am[:], adn_in, sem="la")
        S.dma("sync", t_dcol[:], dcol_in, sem="la")
        S.dma("gpsimd", t_G[:], G_in, sem="lg")
        S.op("vector", lambda e: e.memset(t_z[:], 0.0))
        S.sig("vector", "vd")
        S.wait("scalar", "la", 32)
        S.op("scalar", lambda e: e.activation(t_am[:], t_am[:], AF.Exp))
        S.sig("scalar", "sa")
        S.wait("vector", "sa")
        S.op("vector", lambda e: e.tensor_scalar(t_am[:], t_am[:], -1.0, None, ALU.mult))
        S.sig("vector", "vd")
        am_vd = S.count("vd")

        rec = {}
        nlr = 0
        nlu = 0
        for b in range(BQ):
            if b >= 1:
                S.wait("sync", "vd", rec["comp_done"])
            for i in range(2):
                S.dma("sync", t_dc[:, i * LQ:(i + 1) * LQ],
                      dT_in[b, i * 128:(i + 1) * 128, :], sem="ld")
                S.dma("sync", t_uc[:, i * LQ:(i + 1) * LQ],
                      uT_in[b, i * 128:(i + 1) * 128, :], sem="ld")
            if b >= 1:
                S.wait("sync", "vd", rec["bc_free"])
            S.dma("sync", t_Br[:],
                  BT_in[b].unsqueeze(0).broadcast_to([4, L1_N, LQ]), sem="lb")
            S.dma("sync", t_Cr[:],
                  CT_in[b].unsqueeze(0).broadcast_to([4, L1_N, LQ]), sem="lb")
            S.wait("vector", "ld")
            if b >= 1:
                S.wait("vector", "lu", rec["lu_end"])
            for i in range(2):
                S.op("vector", lambda e, i=i: e.tensor_tensor(
                    t_ud[:, i * LQ:(i + 1) * LQ], t_dc[:, i * LQ:(i + 1) * LQ],
                    t_uc[:, i * LQ:(i + 1) * LQ], ALU.mult))
            S.sig("vector", "vd")
            S.wait("gpsimd", "vd")
            if b >= 1:
                S.wait("gpsimd", "lu", rec["lu_end"])
            for i in range(2):
                S.dma("gpsimd", du_scr[i * 128:(i + 1) * 128, :],
                      t_ud[:, i * LQ:(i + 1) * LQ], sem="du")
            S.wait("vector", "du", 32 * (b + 1))
            S.wait("vector", "la", 32)
            for i in range(2):
                S.op("vector", lambda e, i=i: e.tensor_scalar(
                    t_ud[:, i * LQ:(i + 1) * LQ], t_uc[:, i * LQ:(i + 1) * LQ],
                    t_dcol[:, i:i + 1], None, ALU.mult))
            S.sig("vector", "vd")
            rec["comp_done"] = S.count("vd")

            for ht in range(2):
                t0 = ht * HT
                for j in range(NBLK):
                    first = (b == 0 and ht == 0 and j == 0)
                    if not first:
                        S.wait("sync", "sa", rec["dA_sa"])
                    nlr += 16
                    S.dma("sync", t_dr[:],
                          dT_in[b, 4 * j:4 * j + 4, t0:t0 + HT].unsqueeze(1)
                          .broadcast_to([4, 32, HT]), sem="lr")
                    S.wait("sync", "du", 32 * (b + 1))
                    if not first:
                        S.wait("sync", "vd", rec["dB_done"])
                    nlu += 16
                    S.dma("sync", t_ur[:],
                          du_scr[4 * j:4 * j + 4, t0:t0 + HT].unsqueeze(1)
                          .broadcast_to([4, 32, HT]), sem="lu")
                    S.wait("scalar", "lr", nlr)
                    if first:
                        S.wait("scalar", "vd", am_vd)
                    if not first:
                        S.wait("scalar", "vd", rec["scan_done"])
                    S.op("scalar", lambda e, j=j: e.activation(
                        t_dA[:], t_dr[:], AF.Exp, scale=t_am[:, j:j + 1]))
                    S.sig("scalar", "sa")
                    rec["dA_sa"] = S.count("sa")
                    S.wait("vector", "lu", nlu)
                    S.wait("vector", "lb")
                    if not first:
                        S.wait("vector", "tp", rec["glast"])
                    S.op("vector", lambda e, t0=t0: e.tensor_tensor(
                        t_dB[:], t_ur[:], t_Br[:, t0:t0 + HT], ALU.mult))
                    S.sig("vector", "vd")
                    rec["dB_done"] = S.count("vd")
                    S.wait("vector", "sa", rec["dA_sa"])
                    init = t_z[:, 0:1] if ht == 0 else t_cr[:, j:j + 1]
                    S.op("vector", lambda e, init=init: e.tensor_tensor_scan(
                        t_h[:], t_dA[:], t_dB[:], init, ALU.mult, ALU.add))
                    S.sig("vector", "vd")
                    rec["scan_done"] = S.count("vd")
                    if ht == 0:
                        S.op("vector", lambda e, j=j: e.tensor_copy(
                            t_cr[:, j:j + 1], t_h[:, HT - 1:HT]))
                        S.drain("vector")
                    S.op("vector", lambda e, t0=t0: e.tensor_tensor(
                        t_ch[:], t_h[:], t_Cr[:, t0:t0 + HT], ALU.mult))
                    S.sig("vector", "vd")
                    ch_vd = S.count("vd")
                    S.wait("tensor", "vd", ch_vd)
                    jj = j % 32
                    if first:
                        S.wait("tensor", "lg", 16)
                        S.wait("tensor", "vd", am_vd)
                    if jj == 0 and not first:
                        S.wait("tensor", "vd", rec.get("ylast", 0))
                    for q in range(NQ):
                        S.op("tensor", lambda e, jj=jj, q=q: nc.tensor.matmul(
                            p_y[q][:, 0:QW], t_G[:, jj * 128:(jj + 1) * 128],
                            t_ch[:, q * QW:(q + 1) * QW],
                            start=(jj == 0), stop=(jj == 31), skip_group_check=True))
                    S.sig("tensor", "tp")
                    rec["glast"] = S.count("tp")
                    if jj == 31:
                        grp = j // 32
                        S.wait("vector", "tp", rec["glast"])
                        S.wait("vector", "st")
                        for q in range(NQ):
                            S.op("vector", lambda e, q=q, grp=grp, t0=t0: e.tensor_tensor(
                                t_y[:, q * QW:(q + 1) * QW], p_y[q][:, 0:QW],
                                t_ud[:, grp * LQ + t0 + q * QW: grp * LQ + t0 + q * QW + QW],
                                ALU.add))
                        v = S.sig("vector", "vd")
                        rec["ylast"] = v
                        S.wait("gpsimd", "vd", v)
                        S.dma("gpsimd", y_out[b, grp * 128:(grp + 1) * 128, t0:t0 + HT],
                              t_y[:], sem="st")
            rec["bc_free"] = S.count("vd")
            rec["lu_end"] = nlu

        S.wait("gpsimd", "st")
        S.build()
    return nc


def shard_F(delta_fm, xn1_fm, bc_all, inputs):
    A_log = np.asarray(inputs["L1_attn_A_log"], np.float32)
    D_par = np.asarray(inputs["L1_attn_D_param"], np.float32)
    import ml_dtypes
    G = np.zeros((128, 32, 128), np.float32)
    for jj in range(32):
        for dsub in range(4):
            G[dsub * 32:(dsub + 1) * 32, jj, jj * 4 + dsub] = 1.0
    G = np.ascontiguousarray(G.reshape(128, 32 * 128)).astype(ml_dtypes.bfloat16)
    maps = []
    for c in range(NCORES):
        dsl = slice(c * 256, (c + 1) * 256)
        adn = np.ascontiguousarray(
            A_log[dsl].reshape(64, 4, 32).transpose(1, 2, 0).reshape(128, 64))
        dcol = np.ascontiguousarray(D_par[dsl].reshape(2, 128).T)
        maps.append(dict(
            dT=np.ascontiguousarray(delta_fm[:, dsl, :]),
            uT=np.ascontiguousarray(xn1_fm[:, dsl, :]),
            BTf=np.ascontiguousarray(bc_all[:, 0:32, :]),
            CTf=np.ascontiguousarray(bc_all[:, 32:64, :]),
            adn=adn, dcol=dcol, G=G))
    return maps


def build_G():
    nc = make_nc()
    x1_in = nc.dram_tensor("x1", [D_MODEL, TH], F32, kind="ExternalInput").ap()
    xn1_in = nc.dram_tensor("xn1", [D_MODEL, TH], F32, kind="ExternalInput").ap()
    ym_in = nc.dram_tensor("ym", [D_MODEL, TH], F32, kind="ExternalInput").ap()
    sdev_in = nc.dram_tensor("sdev", [128, NEXP * 128], F32, kind="ExternalInput").ap()
    ssl_in = nc.dram_tensor("ssl", [TH, NEXP], F32, kind="ExternalInput").ap()
    ew_in = nc.dram_tensor("ew", [D_MODEL, D_MODEL], F32, kind="ExternalInput").ap()
    eb_in = nc.dram_tensor("eb", [128, 16], F32, kind="ExternalInput").ap()
    out_o = nc.dram_tensor("out", [D_MODEL, TH], F32, kind="ExternalOutput").ap()
    w_scr = nc.dram_tensor("w_scr", [TH, 1], F32, kind="Internal").ap()

    ctx, S, sbuf, psum = sched_ctx(nc)
    with ctx:
        t_sd = sbuf("t_sd", [128, NEXP * 128])
        t_ss = sbuf("t_ss", [128, 16 * NEXP])
        t_msk = sbuf("t_msk", [128, NEXP * 128], I32)
        t_cnt = sbuf("t_cnt", [128, NEXP], I32)
        t_cntf = sbuf("t_cntf", [128, NEXP])
        t_crow = sbuf("t_crow", [1, NEXP])
        t_lo = sbuf("t_lo", [1, NEXP], I32)
        t_hi = sbuf("t_hi", [1, NEXP], I32)
        t_mid = sbuf("t_mid", [1, NEXP], I32)
        t_sel = sbuf("t_sel", [1, NEXP], I32)
        t_seln = sbuf("t_seln", [1, NEXP], I32)
        t_tau = sbuf("t_tau", [128, NEXP])
        t_one = sbuf("t_one", [1, 128])
        t_onec = sbuf("t_onec", [128, 1])
        t_wc = sbuf("t_wc", [128, 1])
        t_wrow = sbuf("t_wrow", [1, TH])
        t_xnb = sbuf("t_xnb", [128, 16 * TH], BF16)
        t_w = sbuf("t_w", [128, 2 * 16 * 128], BF16)
        t_x1 = sbuf("t_x1", [128, TH])
        t_ym = sbuf("t_ym", [128, TH])
        t_eb = sbuf("t_eb", [128, 16])
        t_ev = sbuf("t_ev", [128, TH])
        t_t1 = sbuf("t_t1", [128, 512])
        p_tau = psum("p_tau", [128, 512])
        p_cnt = psum("p_cnt", [1, 512])
        p_w = [psum(f"p_w{i}", [128, 512]) for i in range(4)]
        p_mm = [psum(f"p_mm{i}", [128, 512]) for i in range(2)]

        for s in ("ld", "lg", "lw0", "lw1", "lwr", "vd", "sa", "tp", "st", "wl"):
            S.sem(s)

        S.dma("sync", t_sd[:], sdev_in, sem="ld")
        S.dma("sync", t_ss[:].rearrange("p (a e) -> p a e", e=NEXP),
              ssl_in.rearrange("(a p) e -> p a e", p=128), sem="ld")
        S.dma("sync", t_eb[:], eb_in, sem="ld")
        S.dma("gpsimd", t_xnb[:].rearrange("p (a t) -> p a t", t=TH),
              xn1_in.rearrange("(a p) t -> p a t", p=128), sem="lg")
        S.op("vector", lambda e: e.memset(t_one[:], 1.0))
        S.op("vector", lambda e: e.memset(t_onec[:], 1.0))
        S.op("vector", lambda e: e.memset(t_lo[:], 0))
        S.op("vector", lambda e: e.memset(t_hi[:], 0x7F800000))
        S.sig("vector", "vd")
        S.wait("tensor", "vd")
        S.wait("vector", "ld", 48)

        for it in range(31):
            S.drain("vector")
            S.op("vector", lambda e: e.tensor_tensor(t_mid[:], t_lo[:], t_hi[:], ALU.add))
            S.drain("vector")
            S.op("vector", lambda e: e.tensor_scalar(
                t_mid[:], t_mid[:], 1, None, ALU.logical_shift_right))
            S.sig("vector", "vd")
            S.wait("tensor", "vd")
            S.op("tensor", lambda e: nc.tensor.matmul(
                p_tau[:, 0:NEXP], t_one[:, :], t_mid[:].bitcast(F32),
                start=True, stop=True))
            S.sig("tensor", "tp")
            S.wait("vector", "tp")
            S.op("vector", lambda e: e.tensor_copy(t_tau[:, :], p_tau[:, 0:NEXP]))
            S.drain("vector")
            S.op("vector", lambda e: e.tensor_tensor(
                t_msk[:].rearrange("p (e a) -> p e a", a=128),
                t_sd[:].rearrange("p (e a) -> p e a", a=128),
                t_tau[:, :, None].broadcast_to([128, NEXP, 128]), ALU.is_ge))
            S.drain("vector")
            def _cntred(e):
                with nc.allow_low_precision(reason="int count"):
                    e.tensor_reduce(t_cnt[:, :, None],
                                    t_msk[:].rearrange("p (e a) -> p e a", a=128),
                                    axis=AX.X, op=ALU.add)
            S.op("vector", _cntred)
            S.drain("vector")
            S.op("vector", lambda e: e.tensor_copy(t_cntf[:], t_cnt[:]))
            S.sig("vector", "vd")
            S.wait("tensor", "vd")
            S.op("tensor", lambda e: nc.tensor.matmul(
                p_cnt[:, 0:NEXP], t_onec[:, :], t_cntf[:], start=True, stop=True))
            S.sig("tensor", "tp")
            S.wait("vector", "tp")
            S.op("vector", lambda e: e.tensor_copy(t_crow[:], p_cnt[:, 0:NEXP]))
            S.drain("vector")
            S.op("vector", lambda e: e.tensor_scalar(
                t_sel[:], t_crow[:], float(TOPK), None, ALU.is_ge))
            S.op("vector", lambda e: e.tensor_scalar(
                t_seln[:], t_crow[:], float(TOPK), None, ALU.is_lt))
            S.drain("vector")
            S.op("vector", lambda e: e.copy_predicated(t_lo[:], t_sel[:], t_mid[:]))
            S.drain("vector")
            S.op("vector", lambda e: e.copy_predicated(t_hi[:], t_seln[:], t_mid[:]))
        S.sig("vector", "vd")
        S.wait("tensor", "vd")
        S.op("tensor", lambda e: nc.tensor.matmul(
            p_tau[:, 0:NEXP], t_one[:, :], t_lo[:].bitcast(F32), start=True, stop=True))
        S.sig("tensor", "tp")
        S.wait("vector", "tp")
        S.op("vector", lambda e: e.tensor_copy(t_tau[:, :], p_tau[:, 0:NEXP]))
        S.drain("vector")

        for ti in range(16):
            ssl = t_ss[:, ti * NEXP:(ti + 1) * NEXP]
            S.op("vector", lambda e, ssl=ssl: e.tensor_tensor(
                t_cnt[:, :], ssl, t_tau[:, :], ALU.is_ge))
            S.drain("vector")
            S.op("vector", lambda e: e.tensor_copy(t_cntf[:], t_cnt[:]))
            S.drain("vector")
            S.op("vector", lambda e, ssl=ssl: e.tensor_tensor(
                t_cntf[:], t_cntf[:], ssl, ALU.mult))
            S.drain("vector")
            S.op("vector", lambda e: e.reduce_sum(t_wc[:, 0:1], t_cntf[:], axis=AX.X))
            v = S.sig("vector", "vd")
            S.wait("gpsimd", "vd", v)
            S.dma("gpsimd", w_scr[ti * 128:(ti + 1) * 128, :], t_wc[:, 0:1], sem="wl")
            S.wait("vector", "wl")
        S.wait("sync", "wl")
        S.dma("sync", t_wrow[:], w_scr[:, 0].unsqueeze(0), sem="lwr")
        S.wait("tensor", "lwr", 16)
        for tt_i in range(4):
            S.op("tensor", lambda e, tt_i=tt_i: nc.tensor.matmul(
                p_w[tt_i][:, :], t_one[:, :], t_wrow[0:1, tt_i * 512:(tt_i + 1) * 512],
                start=True, stop=True))
        S.sig("tensor", "tp")
        wrep_tp = S.count("tp")

        S.wait("tensor", "lg")
        ev_vd = {}
        mm_vd = {}
        mm_tp = {}
        nmm = 0
        for do in range(16):
            sw = do % 2
            if do >= 2:
                S.wait("gpsimd", "tp", mm_tp[do - 2])
            S.dma("gpsimd", t_w[:, sw * 2048:(sw + 1) * 2048].rearrange("p (a j) -> p a j", j=128),
                  ew_in[:, do * 128:(do + 1) * 128].rearrange("(a p) j -> p a j", p=128),
                  sem=("lw0" if sw == 0 else "lw1"))
            if do >= 1:
                S.wait("sync", "vd", ev_vd[do - 1])
            S.dma("sync", t_x1[:], x1_in[do * 128:(do + 1) * 128, :], sem="ld")
            S.dma("sync", t_ym[:], ym_in[do * 128:(do + 1) * 128, :], sem="ld")
            S.wait("tensor", "lw0" if sw == 0 else "lw1", 16 * (do // 2 + 1))
            S.wait("tensor", "tp", wrep_tp)
            if do >= 1:
                S.wait("vector", "st")
            for tt_i in range(4):
                pm = p_mm[nmm % 2]
                if nmm >= 2:
                    S.wait("tensor", "vd", mm_vd[nmm - 2])
                for k in range(16):
                    S.op("tensor", lambda e, pm=pm, sw=sw, k=k, tt_i=tt_i:
                         nc.tensor.matmul(pm[:, :],
                                          t_w[:, sw * 2048 + k * 128:sw * 2048 + (k + 1) * 128],
                                          t_xnb[:, k * TH + tt_i * 512: k * TH + tt_i * 512 + 512],
                                          start=(k == 0), stop=(k == 15)))
                S.sig("tensor", "tp")
                S.wait("vector", "tp")
                S.wait("vector", "ld")
                S.op("vector", lambda e, pm=pm, do=do: e.tensor_scalar(
                    t_t1[:, :], pm[:, :], t_eb[:, do:do + 1], None, ALU.add))
                S.drain("vector")
                S.op("vector", lambda e, tt_i=tt_i: e.tensor_tensor(
                    t_t1[:, :], t_t1[:, :], p_w[tt_i][:, :], ALU.mult))
                S.drain("vector")
                S.op("vector", lambda e, tt_i=tt_i: e.tensor_tensor(
                    t_t1[:, :], t_t1[:, :], t_x1[:, tt_i * 512:(tt_i + 1) * 512], ALU.add))
                S.drain("vector")
                S.op("vector", lambda e, tt_i=tt_i: e.tensor_tensor(
                    t_ev[:, tt_i * 512:(tt_i + 1) * 512], t_t1[:, :],
                    t_ym[:, tt_i * 512:(tt_i + 1) * 512], ALU.add))
                v = S.sig("vector", "vd")
                mm_vd[nmm] = v
                nmm += 1
            ev_vd[do] = S.count("vd")
            mm_tp[do] = S.count("tp")
            S.wait("gpsimd", "vd", ev_vd[do])
            S.dma("gpsimd", out_o[do * 128:(do + 1) * 128, :], t_ev[:], sem="st")

        S.wait("gpsimd", "st")
        S.build()
    return nc


def kernel(**inputs):
    import ml_dtypes
    f32 = np.float32
    _ = inputs["L1_attn_z"]

    ncA = build_A()
    outsA = run_launch(ncA, shard_A(inputs), ["o_xnT", "o_z", "o_xbc", "o_dt"])
    xbc_full = np.zeros((B_SZ, L_SEQ, CONV_DIM), f32)
    dt_full = np.zeros((B_SZ, L_SEQ, NHEADS), f32)
    for c in range(NCORES):
        b, h = divmod(c, 2)
        s = h * TH
        xbc_full[b, s:s + TH] = outsA[c]["o_xbc"].T
        dt_full[b, s:s + TH] = outsA[c]["o_dt"].T
    xnT_full = np.concatenate([outsA[c]["o_xnT"] for c in range(NCORES)], axis=1)
    xnT_full = np.ascontiguousarray(xnT_full)

    ncB = build_B()
    outsB = run_launch(ncB, shard_B(xbc_full, dt_full, inputs), ["Y"])
    Yt_fm = np.zeros((B_SZ, D_INNER, L_SEQ), f32)
    for c in range(NCORES):
        b, hh = divmod(c, 2)
        Yt_fm[b, hh * 2048:(hh + 1) * 2048, :] = outsB[c]["Y"].T

    D_par = np.asarray(inputs["L0_attn_D_param"], f32)
    nw_in = np.asarray(inputs["L0_attn_norm_w"], f32)
    dpar = np.ascontiguousarray(np.repeat(D_par, HEADDIM).reshape(32, 128).T)
    nw = np.ascontiguousarray(nw_in.reshape(32, 128).T)
    w_out = np.ascontiguousarray(inputs["L0_attn_W_out"], dtype=f32)
    x0_flat = np.asarray(inputs["L0_attn_u"], f32).reshape(B_SZ * L_SEQ, D_MODEL)
    ncC = build_C()
    mapsC = []
    for c in range(NCORES):
        b, h = divmod(c, 2)
        tok = slice(h * TH, (h + 1) * TH)
        gtok = slice(b * L_SEQ + h * TH, b * L_SEQ + (h + 1) * TH)
        mapsC.append(dict(
            YT=np.ascontiguousarray(Yt_fm[b][:, tok]),
            z=outsA[c]["o_z"],
            Xfm=outsA[c]["o_xbc"][:D_INNER],
            dpar=dpar, nw=nw, w_out=w_out,
            x0T=np.ascontiguousarray(x0_flat[gtok].T)))
    outsC = run_launch(ncC, mapsC, ["r0"])

    ncD = build_D()
    outsD = run_launch(ncD, shard_D(xnT_full, inputs), ["part"])

    wbcT = np.ascontiguousarray(np.asarray(inputs["L1_attn_W_bc"], f32).T)
    wdtT = np.ascontiguousarray(np.asarray(inputs["L1_attn_W_dt"], f32).T)
    bdt = np.ascontiguousarray(np.asarray(inputs["L1_attn_b_dt"], f32).reshape(16, 128).T)
    rw = np.ascontiguousarray(inputs["L1_ffn_router_w"], dtype=f32)
    rb = np.ascontiguousarray(np.asarray(inputs["L1_ffn_router_b"], f32).reshape(-1, 1))
    idf = np.eye(128, dtype=f32)
    ncE = build_E()
    mapsE = []
    for c in range(NCORES):
        tok = slice(c * TH, (c + 1) * TH)
        m = dict(r0=outsC[c]["r0"], wbcT=wbcT, wdtT=wdtT, bdt=bdt, rw=rw, rb=rb, idf=idf)
        for i in range(NCORES):
            m[f"pp{i}"] = np.ascontiguousarray(outsD[i]["part"][:, tok])
        mapsE.append(m)
    outsE = run_launch(ncE, mapsE, ["x1", "xn1", "dl", "bc", "s"])

    delta_fm = np.zeros((B_SZ, D_MODEL, L_SEQ), f32)
    xn1_fm = np.zeros((B_SZ, D_MODEL, L_SEQ), f32)
    bc_all = np.zeros((B_SZ, 64, L_SEQ), f32)
    for c in range(NCORES):
        b, h = divmod(c, 2)
        tok = slice(h * TH, (h + 1) * TH)
        delta_fm[b][:, tok] = outsE[c]["dl"]
        xn1_fm[b][:, tok] = outsE[c]["xn1"]
        bc_all[b][:, tok] = outsE[c]["bc"]
    S_full = np.concatenate([outsE[c]["s"] for c in range(NCORES)], axis=0)

    ncF = build_F()
    outsF = run_launch(ncF, shard_F(delta_fm, xn1_fm, bc_all, inputs), ["y"])
    ym_fm = np.zeros((B_SZ, D_MODEL, L_SEQ), f32)
    for c in range(NCORES):
        ym_fm[:, c * 256:(c + 1) * 256, :] = outsF[c]["y"]

    sdev = np.ascontiguousarray(
        S_full.reshape(128, 128, NEXP).transpose(1, 2, 0).reshape(128, NEXP * 128))
    ew = np.ascontiguousarray(inputs["L1_ffn_expert_w"], dtype=f32)
    eb = np.ascontiguousarray(np.asarray(inputs["L1_ffn_expert_b"], f32).reshape(16, 128).T)
    ncG = build_G()
    mapsG = []
    for c in range(NCORES):
        b, h = divmod(c, 2)
        tok = slice(h * TH, (h + 1) * TH)
        gtok = slice(c * TH, (c + 1) * TH)
        mapsG.append(dict(
            x1=outsE[c]["x1"], xn1=outsE[c]["xn1"],
            ym=np.ascontiguousarray(ym_fm[b][:, tok]),
            sdev=sdev, ssl=np.ascontiguousarray(S_full[gtok]),
            ew=ew, eb=eb))
    outsG = run_launch(ncG, mapsG, ["out"])

    out = np.zeros((B_SZ * L_SEQ, D_MODEL), f32)
    for c in range(NCORES):
        out[c * TH:(c + 1) * TH] = outsG[c]["out"].T
    return out.reshape(B_SZ, L_SEQ, D_MODEL)


# version 03 (final)
# speedup vs baseline: 1.1186x; 1.1181x over previous
import contextlib
import numpy as np
import concourse.bass as bass
import concourse.mybir as mybir
from concourse.bass_utils import run_bass_kernel_spmd

F32 = mybir.dt.float32
BF16 = mybir.dt.bfloat16
I32 = mybir.dt.int32
AF = mybir.ActivationFunctionType
ALU = mybir.AluOpType
AX = mybir.AxisListType

B_SZ = 4
L_SEQ = 4096
D_MODEL = 2048
D_INNER = 4096
D_CONV = 4
CONV_DIM = 4352
D_IN_PROJ = 8512
NHEADS = 64
HEADDIM = 64
CHUNK = 64
SSD_N = 128
TH = L_SEQ // 2
TE = TH + 3
NCORES = 8

DT_RANK = 64
L1_N = 32
NEXP = 16
TOPK = 1024

ENGS = ("sync", "scalar", "vector", "tensor", "gpsimd")


class Sched:

    def __init__(self, nc, ctx):
        self.nc = nc
        self.items = {e: [] for e in ENGS}
        self.sems = {}
        self.counts = {}
        self.ctx = ctx

    def sem(self, name):
        self.sems[name] = self.ctx.enter_context(self.nc.semaphore(name))
        self.counts[name] = 0
        return name

    def count(self, sem):
        return self.counts[sem]

    def op(self, eng, fn):
        self.items[eng].append(fn)

    def sig(self, eng, sem, n=1):
        h = self.sems[sem]
        self.counts[sem] += n
        self.items[eng].append(lambda e: e.drain().then_inc(h, n))
        return self.counts[sem]

    def dma(self, eng, out, in_, sem=None, **kw):
        h = self.sems[sem] if sem else None
        if sem:
            self.counts[sem] += 16
        def fn(e, out=out, in_=in_, h=h, kw=kw):
            inst = e.dma_start(out=out, in_=in_, **kw)
            if h is not None:
                inst.then_inc(h, 16)
        self.items[eng].append(fn)
        return self.counts[sem] if sem else None

    def wait(self, eng, sem, val=None):
        h = self.sems[sem]
        v = self.counts[sem] if val is None else val
        if v > 0:
            self.items[eng].append(lambda e: e.wait_ge(h, v))

    def drain(self, eng):
        self.items[eng].append(lambda e: e.drain())

    def build(self):
        block = self.ctx.enter_context(self.nc.Block())
        for engname in ENGS:
            items = self.items[engname]
            def mk(items):
                def f(eng):
                    for fn in items:
                        fn(eng)
                return f
            getattr(block, engname)(mk(items))


def make_nc():
    return bass.Bass("TRN2", target_bir_lowering=False)


def sched_ctx(nc):
    ctx = contextlib.ExitStack()
    S = Sched(nc, ctx)
    sbuf = lambda n, s, d=F32: ctx.enter_context(nc.sbuf_tensor(n, s, d))
    psum = lambda n, s, d=F32: ctx.enter_context(nc.psum_tensor(n, s, d))
    return ctx, S, sbuf, psum


def run_launch(nc, in_maps, out_names):
    res = run_bass_kernel_spmd(nc, in_maps, core_ids=list(range(NCORES)))
    return [{k: np.asarray(r[k]) for k in out_names} for r in res.results]


def id128_bf():
    import ml_dtypes
    return np.eye(128, dtype=ml_dtypes.bfloat16)


def build_A():
    nc = make_nc()
    x_in = nc.dram_tensor("x", [TE, D_MODEL], F32, kind="ExternalInput").ap()
    w_in = nc.dram_tensor("w_in", [D_MODEL, D_IN_PROJ], F32, kind="ExternalInput").ap()
    conv_w = nc.dram_tensor("conv_w", [CONV_DIM, D_CONV], F32, kind="ExternalInput").ap()
    conv_b = nc.dram_tensor("conv_b", [CONV_DIM, 1], F32, kind="ExternalInput").ap()
    dt_bias = nc.dram_tensor("dt_bias", [NHEADS, 1], F32, kind="ExternalInput").ap()
    iden = nc.dram_tensor("iden", [128, 128], BF16, kind="ExternalInput").ap()

    o_xnT = nc.dram_tensor("o_xnT", [D_MODEL, TH], BF16, kind="ExternalOutput").ap()
    o_z = nc.dram_tensor("o_z", [D_INNER, TH], F32, kind="ExternalOutput").ap()
    o_xbc = nc.dram_tensor("o_xbc", [CONV_DIM, TH], F32, kind="ExternalOutput").ap()
    o_dt = nc.dram_tensor("o_dt", [NHEADS, TH], F32, kind="ExternalOutput").ap()

    NT = 17
    NJ = 67
    TT5 = [(0, 512), (512, 512), (1024, 512), (1536, 512), (2048, 3)]

    ctx, S, sbuf, psum = sched_ctx(nc)
    with ctx:
        t_x = sbuf("t_x", [128, 2 * D_MODEL])
        t_xn = sbuf("t_xn", [128, NT * D_MODEL], BF16)
        t_sq = sbuf("t_sq", [128, NT])
        t_mu = sbuf("t_mu", [128, NT])
        t_va = sbuf("t_va", [128, NT])
        t_rs = sbuf("t_rs", [128, NT])
        t_scr = sbuf("t_scr", [128, D_MODEL])
        t_id = sbuf("t_id", [128, 128], BF16)
        t_xnT = sbuf("t_xnT", [128, 16 * TE], BF16)
        t_w = sbuf("t_w", [128, 2 * 2048], BF16)
        t_strip = [sbuf(f"t_strip{i}", [128, TE]) for i in range(2)]
        t_conv = [sbuf(f"t_conv{i}", [128, TH]) for i in range(2)]
        t_cw = sbuf("t_cw", [128, 34 * D_CONV])
        t_cb = sbuf("t_cb", [128, 34])
        t_dtb = sbuf("t_dtb", [NHEADS, 1])
        p_t = [psum(f"p_t{i}", [128, 1024], BF16) for i in range(2)]
        p_mm = [psum(f"p_mm{i}", [128, 512]) for i in range(2)]

        for s in ("ld", "lxa", "lxb", "lwa", "lwb", "vv", "ss", "tt", "st", "ev"):
            S.sem(s)

        S.dma("sync", t_id[:], iden, sem="ld")
        S.dma("sync", t_cw[:].rearrange("p (a w) -> p a w", w=D_CONV), conv_w.rearrange("(a p) w -> p a w", p=128), sem="ld")
        S.dma("sync", t_cb[:].rearrange("p (a o) -> p a o", o=1), conv_b.rearrange("(a p) o -> p a o", p=128), sem="ld", allow_slow_non_contiguous=True)
        S.dma("sync", t_dtb[:], dt_bias, sem="ld")
        for i in range(NT):
            r = min(128, TE - i * 128)
            if i >= 2:
                S.wait("sync", "vv", 2 * (i - 1))
            S.dma("sync", t_x[0:r, (i % 2) * D_MODEL:(i % 2 + 1) * D_MODEL],
                  x_in[i * 128:i * 128 + r, :], sem=("lxa" if i % 2 == 0 else "lxb"))

        for i in range(NT):
            r = min(128, TE - i * 128)
            xv = t_x[0:r, (i % 2) * D_MODEL:(i % 2 + 1) * D_MODEL]
            S.wait("scalar", "lxa" if i % 2 == 0 else "lxb", 16 * (i // 2 + 1))
            S.wait("scalar", "ld", 64)
            S.op("scalar", lambda e, xv=xv, r=r, i=i: e.activation(
                t_scr[0:r, :], xv, AF.Square, accum_out=t_sq[0:r, i:i + 1]))
            S.sig("scalar", "ss")
            S.wait("vector", "lxa" if i % 2 == 0 else "lxb", 16 * (i // 2 + 1))
            S.wait("vector", "ld", 64)
            S.op("vector", lambda e, xv=xv, r=r, i=i: e.reduce_sum(
                t_mu[0:r, i:i + 1], xv, axis=AX.X))
            S.wait("vector", "ss")
            S.drain("vector")
            S.op("vector", lambda e, r=r, i=i: e.tensor_scalar(
                t_mu[0:r, i:i + 1], t_mu[0:r, i:i + 1], 1.0 / D_MODEL, None, ALU.mult))
            S.op("vector", lambda e, r=r, i=i: e.tensor_scalar(
                t_va[0:r, i:i + 1], t_sq[0:r, i:i + 1], 1.0 / D_MODEL, None, ALU.mult))
            S.drain("vector")
            S.op("vector", lambda e, r=r, i=i: e.scalar_tensor_tensor(
                t_va[0:r, i:i + 1], t_mu[0:r, i:i + 1], t_mu[0:r, i:i + 1],
                t_va[0:r, i:i + 1], ALU.mult, ALU.subtract))
            S.drain("vector")
            S.op("vector", lambda e, r=r, i=i: e.tensor_scalar(
                t_va[0:r, i:i + 1], t_va[0:r, i:i + 1], -1.0, 1e-5, ALU.mult, ALU.add))
            S.drain("vector")
            S.op("vector", lambda e, r=r, i=i: e.reciprocal(
                t_rs[0:r, i:i + 1], t_va[0:r, i:i + 1]))
            S.sig("vector", "vv")
            S.wait("scalar", "vv")
            S.op("scalar", lambda e, r=r, i=i: e.activation(
                t_rs[0:r, i:i + 1], t_rs[0:r, i:i + 1], AF.Sqrt))
            S.sig("scalar", "ss")
            S.wait("vector", "ss")
            S.op("vector", lambda e, xv=xv, r=r, i=i: e.tensor_scalar(
                t_xn[0:r, i * D_MODEL:(i + 1) * D_MODEL], xv,
                t_mu[0:r, i:i + 1], t_rs[0:r, i:i + 1], ALU.subtract, ALU.mult))
            S.sig("vector", "vv")
        ln_vv = S.count("vv")

        S.wait("tensor", "vv", ln_vv)
        nev = 0
        for i in range(NT):
            r = min(128, TE - i * 128)
            for dtile in range(16):
                pb = p_t[nev % 2]
                if nev >= 2:
                    S.wait("tensor", "ev", nev - 1)
                S.op("tensor", lambda e, pb=pb, r=r, i=i, dtile=dtile:
                     nc.tensor.transpose(
                         pb[:, 0:r],
                         t_xn[0:r, i * D_MODEL + dtile * 128: i * D_MODEL + (dtile + 1) * 128],
                         t_id[0:r, 0:r]))
                S.sig("tensor", "tt")
                S.wait("vector", "tt")
                S.op("vector", lambda e, pb=pb, r=r, i=i, dtile=dtile: e.tensor_copy(
                    t_xnT[:, dtile * TE + i * 128: dtile * TE + i * 128 + r], pb[:, 0:r]))
                S.sig("vector", "ev")
                nev += 1
        xnT_ev = S.count("ev")

        S.wait("gpsimd", "ev", xnT_ev)
        for dtile in range(16):
            S.dma("gpsimd", o_xnT[dtile * 128:(dtile + 1) * 128, :],
                  t_xnT[:, dtile * TE + 3: dtile * TE + TE], sem="st")

        tt_after_mm = {}
        st_after = {}
        nmm = 0
        for j in range(NJ):
            jw = 128 if j < 66 else 64
            sj = j % 2
            strip = t_strip[sj]
            cvb = t_conv[sj]
            if j >= 2:
                S.wait("gpsimd", "tt", tt_after_mm[j - 2])
            S.dma("gpsimd", t_w[0:128, sj * 2048: sj * 2048 + 16 * jw].rearrange(
                      "p (a j) -> p a j", j=jw),
                  w_in[:, j * 128:j * 128 + jw].rearrange("(a p) j -> p a j", p=128),
                  sem=("lwa" if sj == 0 else "lwb"))
            S.wait("tensor", "lwa" if sj == 0 else "lwb", 16 * (j // 2 + 1))
            if j == 0:
                S.wait("tensor", "ev", xnT_ev)
            if j >= 2:
                S.wait("vector", "st")
            for ti, (tq, tw) in enumerate(TT5):
                pm = p_mm[nmm % 2]
                if nmm >= 2:
                    S.wait("tensor", "ev", xnT_ev + nmm - 1)
                for k in range(16):
                    S.op("tensor", lambda e, pm=pm, jw=jw, tw=tw, sj=sj, k=k, tq=tq:
                         nc.tensor.matmul(pm[0:jw, 0:tw],
                                          t_w[:, sj * 2048 + k * jw: sj * 2048 + (k + 1) * jw],
                                          t_xnT[:, k * TE + tq: k * TE + tq + tw],
                                          start=(k == 0), stop=(k == 15)))
                S.sig("tensor", "tt")
                S.wait("vector", "tt")
                S.op("vector", lambda e, pm=pm, strip=strip, jw=jw, tq=tq, tw=tw:
                     e.tensor_copy(strip[0:jw, tq:tq + tw], pm[0:jw, 0:tw]))
                S.sig("vector", "ev")
                nmm += 1
            tt_after_mm[j] = S.count("tt")
            S.drain("vector")
            if j < 32:
                S.wait("gpsimd", "ev")
                st_after[j] = S.dma("gpsimd", o_z[j * 128:(j + 1) * 128, :],
                                    strip[:, 3:TE], sem="st")
            elif j < 66:
                cj = j - 32
                S.op("vector", lambda e, strip=strip, cvb=cvb, cj=cj: e.tensor_scalar(
                    cvb[:], strip[:, 0:TH], t_cw[:, cj * 4:cj * 4 + 1], None, ALU.mult))
                for w in range(1, 4):
                    S.drain("vector")
                    S.op("vector", lambda e, strip=strip, cvb=cvb, cj=cj, w=w:
                         e.scalar_tensor_tensor(cvb[:], strip[:, w:w + TH],
                                                t_cw[:, cj * 4 + w:cj * 4 + w + 1],
                                                cvb[:], ALU.mult, ALU.add))
                S.drain("vector")
                S.op("vector", lambda e, cvb=cvb, cj=cj: e.tensor_scalar(
                    cvb[:], cvb[:], t_cb[:, cj:cj + 1], None, ALU.add))
                S.sig("vector", "vv")
                S.wait("scalar", "vv")
                S.op("scalar", lambda e, cvb=cvb: e.activation(cvb[:], cvb[:], AF.Silu))
                S.sig("scalar", "ss")
                S.wait("gpsimd", "ss")
                st_after[j] = S.dma("gpsimd", o_xbc[cj * 128:(cj + 1) * 128, :],
                                    cvb[:], sem="st")
            else:
                S.wait("vector", "st")
                sp0 = t_conv[0]
                sp1 = t_conv[1]
                S.op("vector", lambda e, strip=strip, sp0=sp0: e.tensor_scalar(
                    sp0[0:64, 0:TH], strip[0:64, 3:TE], t_dtb[:, 0:1], None, ALU.add))
                S.sig("vector", "vv")
                S.wait("scalar", "vv")
                S.wait("scalar", "st")
                S.op("scalar", lambda e, sp0=sp0, sp1=sp1: e.activation(sp1[0:64, 0:TH], sp0[0:64, 0:TH], AF.Abs))
                S.drain("scalar")
                S.op("scalar", lambda e, sp1=sp1: e.activation(sp1[0:64, 0:TH], sp1[0:64, 0:TH], AF.Exp, scale=-1.0))
                S.drain("scalar")
                S.op("scalar", lambda e, sp1=sp1: e.activation(sp1[0:64, 0:TH], sp1[0:64, 0:TH], AF.Ln, bias=1.0))
                S.drain("scalar")
                S.op("scalar", lambda e, sp0=sp0: e.activation(sp0[0:64, 0:TH], sp0[0:64, 0:TH], AF.Relu))
                S.sig("scalar", "ss")
                S.wait("vector", "ss")
                S.op("vector", lambda e, sp0=sp0, sp1=sp1: e.tensor_tensor(
                    sp0[0:64, 0:TH], sp0[0:64, 0:TH], sp1[0:64, 0:TH], ALU.add))
                S.sig("vector", "vv")
                S.wait("gpsimd", "vv")
                st_after[j] = S.dma("gpsimd", o_dt[:], sp0[0:64, 0:TH], sem="st")

        S.wait("gpsimd", "st")
        S.build()
    return nc


def shard_A(inputs):
    x = np.asarray(inputs["L0_attn_u"], np.float32).reshape(B_SZ * L_SEQ, D_MODEL)
    w_in = np.ascontiguousarray(inputs["L0_attn_W_in"], dtype=np.float32)
    conv_w = np.ascontiguousarray(inputs["L0_attn_conv_w"], dtype=np.float32)
    conv_b = np.ascontiguousarray(inputs["L0_attn_conv_b"], dtype=np.float32).reshape(-1, 1)
    dt_bias = np.ascontiguousarray(inputs["L0_attn_dt_bias"], dtype=np.float32).reshape(-1, 1)
    iden = id128_bf()
    maps = []
    for c in range(NCORES):
        b, h = divmod(c, 2)
        start = b * L_SEQ + h * TH
        xs = np.zeros((TE, D_MODEL), np.float32)
        lo = max(b * L_SEQ, start - 3)
        xs[3 - (start - lo):] = x[lo:start + TH]
        maps.append(dict(x=xs, w_in=w_in, conv_w=conv_w, conv_b=conv_b,
                         dt_bias=dt_bias, iden=iden))
    return maps


def kernel(**inputs):
    raise NotImplementedError("under construction")


def build_B(LQ=L_SEQ):
    nc = make_nc()
    NC64 = LQ // CHUNK
    HL = 32
    HP = 2048
    X_in = nc.dram_tensor("X", [LQ, HP], F32, kind="ExternalInput").ap()
    dtT_in = nc.dram_tensor("dtT", [HL, LQ], F32, kind="ExternalInput").ap()
    dtk_in = nc.dram_tensor("dtk", [LQ, HL], F32, kind="ExternalInput").ap()
    BT_in = nc.dram_tensor("BT", [SSD_N, LQ], F32, kind="ExternalInput").ap()
    CT_in = nc.dram_tensor("CT", [SSD_N, LQ], F32, kind="ExternalInput").ap()
    Btk_in = nc.dram_tensor("Btk", [LQ, SSD_N], F32, kind="ExternalInput").ap()
    alog_in = nc.dram_tensor("alog", [HL, 1], F32, kind="ExternalInput").ap()
    mask_in = nc.dram_tensor("mask", [CHUNK, HP], F32, kind="ExternalInput").ap()
    idf_in = nc.dram_tensor("idf", [128, 128], F32, kind="ExternalInput").ap()
    Y_out = nc.dram_tensor("Y", [LQ, HP], F32, kind="ExternalOutput").ap()
    a_scr = nc.dram_tensor("a_scr", [HL, LQ], F32, kind="Internal").ap()

    ctx, S, sbuf, psum = sched_ctx(nc)
    with ctx:
        t_X = sbuf("t_X", [CHUNK, HP])
        t_dtT = sbuf("t_dtT", [HL, LQ])
        t_ac = sbuf("t_ac", [HL, LQ])
        t_ds = sbuf("t_ds", [HL, LQ])
        t_dtk = sbuf("t_dtk", [CHUNK, NC64 * HL])
        t_BT = sbuf("t_BT", [SSD_N, LQ], BF16)
        t_CT = sbuf("t_CT", [SSD_N, LQ], BF16)
        t_Btk = sbuf("t_Btk", [CHUNK, NC64 * SSD_N], BF16)
        t_acT = sbuf("t_acT", [CHUNK, NC64 * HL])
        t_dsT = sbuf("t_dsT", [CHUNK, NC64 * HL])
        t_odT = sbuf("t_odT", [CHUNK, NC64 * HL])
        t_al = sbuf("t_al", [HL, 2])
        t_zer = sbuf("t_zer", [HL, CHUNK])
        t_one = sbuf("t_one", [1, 128])
        t_id = sbuf("t_id", [128, 128])
        t_msk = sbuf("t_msk", [CHUNK, HP], BF16)
        t_Xdt = sbuf("t_Xdt", [CHUNK, HP], BF16)
        t_Xdc = sbuf("t_Xdc", [CHUNK, HP], BF16)
        t_t2m = sbuf("t_t2m", [CHUNK, HP])
        t_arg = sbuf("t_arg", [CHUNK, HP])
        t_L = sbuf("t_L", [CHUNK, HP])
        t_Mt = sbuf("t_Mt", [CHUNK, HP], BF16)
        t_CB = sbuf("t_CB", [CHUNK, CHUNK])
        t_Y = sbuf("t_Y", [CHUNK, HP])
        t_tmp = sbuf("t_tmp", [CHUNK, 512])
        t_H = sbuf("t_H", [SSD_N, HP])
        t_Hb = sbuf("t_Hb", [SSD_N, HP], BF16)
        t_Ar = sbuf("t_Ar", [1, HP])
        t_Er = sbuf("t_Er", [1, HP])
        pb = [psum(f"pb{i}", [128, 512]) for i in range(8)]

        for s in ("ld", "lg", "lx", "la", "vd", "sa", "tp", "sy", "as_"):
            S.sem(s)

        S.dma("sync", t_dtT[:], dtT_in, sem="ld")
        S.dma("sync", t_al[:, 0:1], alog_in, sem="ld")
        S.dma("sync", t_id[:], idf_in, sem="ld")
        S.dma("sync", t_dtk[:].rearrange("p (a h) -> p a h", h=HL),
              dtk_in.rearrange("(a p) h -> p a h", p=CHUNK), sem="ld")
        S.dma("gpsimd", t_msk[:], mask_in, sem="lg")
        S.dma("gpsimd", t_BT[:], BT_in, sem="lg")
        S.dma("gpsimd", t_CT[:], CT_in, sem="lg")
        S.dma("gpsimd", t_Btk[:].rearrange("p (a n) -> p a n", n=SSD_N),
              Btk_in.rearrange("(a p) n -> p a n", p=CHUNK), sem="lg")
        ld_all = S.count("ld")
        lg_all = S.count("lg")

        S.op("vector", lambda e: e.memset(t_zer[:], 0.0))
        S.op("vector", lambda e: e.memset(t_one[:], 1.0))
        S.op("vector", lambda e: e.memset(t_H[:], 0.0))
        S.op("vector", lambda e: e.memset(t_Hb[:], 0.0))
        S.wait("scalar", "ld", ld_all)
        S.wait("scalar", "lg", lg_all)
        S.op("scalar", lambda e: e.activation(t_al[:, 1:2], t_al[:, 0:1], AF.Exp))
        S.sig("scalar", "sa")
        S.wait("vector", "sa")
        S.op("vector", lambda e: e.tensor_scalar(t_al[:, 1:2], t_al[:, 1:2], -1.0, None, ALU.mult))
        S.drain("vector")
        S.op("vector", lambda e: e.tensor_scalar(t_dtT[:], t_dtT[:], t_al[:, 1:2], None, ALU.mult))
        S.drain("vector")
        for c in range(NC64):
            sl0 = c * CHUNK
            S.op("vector", lambda e, sl0=sl0: e.tensor_tensor_scan(
                t_ac[:, sl0:sl0 + CHUNK], t_dtT[:, sl0:sl0 + CHUNK], t_zer[:],
                0.0, ALU.add, ALU.add))
        S.sig("vector", "vd")
        for c in range(NC64):
            sl0 = c * CHUNK
            S.op("vector", lambda e, sl0=sl0: e.tensor_scalar(
                t_ds[:, sl0:sl0 + CHUNK], t_ac[:, sl0:sl0 + CHUNK],
                t_ac[:, sl0 + CHUNK - 1:sl0 + CHUNK], None, ALU.subtract))
        S.sig("vector", "vd")
        S.wait("gpsimd", "vd")
        S.dma("gpsimd", a_scr, t_ac[:], sem="as_")
        S.wait("scalar", "vd")
        S.op("scalar", lambda e: e.activation(t_dtT[:], t_ac[:], AF.Exp))
        S.drain("scalar")
        S.op("scalar", lambda e: e.activation(t_ds[:], t_ds[:], AF.Exp, scale=-1.0))
        S.sig("scalar", "sa")
        S.wait("tensor", "sa")
        S.wait("tensor", "ld", ld_all)
        S.wait("tensor", "lg", lg_all)
        tr_vd = []
        ntr = 0
        for (srct, dst) in ((t_ac, t_acT), (t_ds, t_dsT), (t_dtT, t_odT)):
            for c in range(NC64):
                bank = pb[ntr % 2]
                if ntr >= 2:
                    S.wait("tensor", "vd", tr_vd[ntr - 2])
                S.op("tensor", lambda e, srct=srct, c=c, bank=bank:
                     nc.tensor.transpose(bank[0:CHUNK, 0:HL],
                                         srct[:, c * CHUNK:(c + 1) * CHUNK],
                                         t_id[0:HL, 0:HL]))
                S.sig("tensor", "tp")
                S.wait("vector", "tp")
                S.op("vector", lambda e, dst=dst, c=c, bank=bank: e.tensor_copy(
                    dst[:, c * HL:(c + 1) * HL], bank[0:CHUNK, 0:HL]))
                tr_vd.append(S.sig("vector", "vd"))
                ntr += 1
        pre_vd = S.count("vd")

        rec = {"ydiag": {}, "states": {}, "comb": {}, "hupd": {}, "mt": {},
               "arow": {}, "odf": {}, "t1": {}, "yst": {}, "lexp": {},
               "erep": {}, "ercp": {}, "xdt": {}}
        for c in range(NC64):
            sl0 = c * CHUNK
            if c >= 1:
                S.wait("sync", "vd", rec["xdt"][c - 1])
            S.dma("sync", t_X[:], X_in[sl0:sl0 + CHUNK, :], sem="lx")
            S.wait("sync", "as_", 16)
            if c >= 1:
                S.wait("sync", "vd", rec["ercp"][c - 1])
            S.dma("sync", t_Ar[0:1, :].rearrange("o (h l) -> o h l", l=CHUNK),
                  a_scr[:, sl0:sl0 + CHUNK].unsqueeze(0), sem="la")
            rec["arow"][c] = S.count("la")

            S.wait("vector", "lx")
            if c >= 1:
                S.wait("vector", "tp", rec["ydiag"][c - 1])
            dtk_sl = t_dtk[:, c * HL:(c + 1) * HL]
            S.op("vector", lambda e, dtk_sl=dtk_sl: e.tensor_tensor(
                t_Xdt[:].rearrange("l (h p) -> l h p", p=HEADDIM),
                t_X[:].rearrange("l (h p) -> l h p", p=HEADDIM),
                dtk_sl[:, :, None].broadcast_to([CHUNK, HL, HEADDIM]), ALU.mult))
            S.drain("vector")
            if c >= 1:
                S.wait("vector", "tp", rec["states"][c - 1])
            dsT_sl = t_dsT[:, c * HL:(c + 1) * HL]
            S.op("vector", lambda e, dsT_sl=dsT_sl: e.tensor_tensor(
                t_Xdc[:].rearrange("l (h p) -> l h p", p=HEADDIM),
                t_Xdt[:].rearrange("l (h p) -> l h p", p=HEADDIM),
                dsT_sl[:, :, None].broadcast_to([CHUNK, HL, HEADDIM]), ALU.mult))
            S.sig("vector", "vd")
            rec["xdt"][c] = S.count("vd")
            acT_sl = t_acT[:, c * HL:(c + 1) * HL]
            S.op("vector", lambda e, acT_sl=acT_sl: e.tensor_tensor(
                t_t2m[:].rearrange("l (h p) -> l h p", p=HEADDIM),
                acT_sl[:, :, None].broadcast_to([CHUNK, HL, HEADDIM]),
                t_msk[:].rearrange("l (h p) -> l h p", p=HEADDIM), ALU.subtract))
            S.sig("vector", "vd")

            if c >= 1:
                S.wait("tensor", "vd", rec["hupd"][c - 1])
            S.op("tensor", lambda e, sl0=sl0: nc.tensor.matmul(
                pb[4][0:CHUNK, 0:CHUNK], t_BT[:, sl0:sl0 + CHUNK],
                t_CT[:, sl0:sl0 + CHUNK], start=True, stop=True))
            S.sig("tensor", "tp")
            S.wait("vector", "tp")
            S.op("vector", lambda e: e.tensor_copy(t_CB[:], pb[4][0:CHUNK, 0:CHUNK]))
            S.sig("vector", "vd")

            if c >= 1:
                S.wait("vector", "sa", rec["lexp"][c - 1])
            S.wait("tensor", "la", rec["arow"][c])
            arg_vd = [0, 0]
            for i in range(4):
                bank = pb[2 + i % 2]
                if c >= 1 and i < 2:
                    S.wait("tensor", "vd", rec["hupd"][c - 1])
                if i >= 2:
                    S.wait("tensor", "vd", arg_vd[i - 2])
                S.op("tensor", lambda e, bank=bank, i=i: nc.tensor.matmul(
                    bank[0:CHUNK, :], t_one[:, 0:CHUNK],
                    t_Ar[0:1, i * 512:(i + 1) * 512],
                    start=True, stop=True))
                S.sig("tensor", "tp")
                S.wait("vector", "tp")
                S.op("vector", lambda e, bank=bank, i=i: e.tensor_tensor(
                    t_arg[:, i * 512:(i + 1) * 512], bank[0:CHUNK, :],
                    t_t2m[:, i * 512:(i + 1) * 512], ALU.subtract))
                v = S.sig("vector", "vd")
                if i < 2:
                    arg_vd[i] = v
            rec["t1"][c] = S.count("tp")

            S.wait("scalar", "vd")
            S.op("scalar", lambda e: e.activation(t_L[:], t_arg[:], AF.Exp))
            S.sig("scalar", "sa")
            rec["lexp"][c] = S.count("sa")
            S.wait("scalar", "la", rec["arow"][c])
            S.wait("scalar", "tp", rec["t1"][c])
            S.op("scalar", lambda e: e.activation(t_Ar[0:1, :], t_Ar[0:1, :], AF.Exp))
            S.sig("scalar", "sa")
            rec["odf"][c] = S.count("sa")

            if c >= 1:
                S.wait("vector", "tp", rec["erep"][c - 1])
            S.wait("vector", "sa", rec["odf"][c])
            S.op("vector", lambda e: e.tensor_copy(
                t_Er[:].rearrange("o (h p) -> o h p", p=HEADDIM),
                t_Ar[0:1, :].rearrange("o (h l) -> o h l", l=CHUNK)
                [:, :, CHUNK - 1:CHUNK].broadcast_to([1, HL, HEADDIM])))
            S.sig("vector", "vd")
            rec["ercp"][c] = S.count("vd")
            S.op("vector", lambda e: e.tensor_tensor(
                t_Mt[:].rearrange("l (h p) -> l h p", p=HEADDIM),
                t_L[:].rearrange("l (h p) -> l h p", p=HEADDIM),
                t_CB[:, None, :].broadcast_to([CHUNK, HL, CHUNK]), ALU.mult))
            S.sig("vector", "vd")
            rec["mt"][c] = S.count("vd")

            if c >= 1:
                S.wait("vector", "sy", rec["yst"][c - 1])
            S.wait("tensor", "vd", rec["mt"][c])
            comb_vd = [0, 0]
            for i in range(4):
                ydb = pb[0 + i % 2]
                yob = pb[2 + i % 2]
                if c >= 1 and i < 2:
                    S.wait("tensor", "vd", rec["hupd"][c - 1])
                if i >= 2:
                    S.wait("tensor", "vd", comb_vd[i - 2])
                for hh in range(8):
                    h = i * 8 + hh
                    S.op("tensor", lambda e, ydb=ydb, h=h, hh=hh: nc.tensor.matmul(
                        ydb[0:CHUNK, hh * 64:(hh + 1) * 64],
                        t_Mt[:, h * 64:(h + 1) * 64],
                        t_Xdt[:, h * 64:(h + 1) * 64],
                        start=(hh == 0), stop=(hh == 7)))
                S.sig("tensor", "tp")
                S.op("tensor", lambda e, yob=yob, sl0=sl0, i=i: nc.tensor.matmul(
                    yob[0:CHUNK, :], t_CT[:, sl0:sl0 + CHUNK],
                    t_Hb[:, i * 512:(i + 1) * 512], start=True, stop=True))
                S.sig("tensor", "tp")
                S.wait("vector", "tp")
                odT_sl = t_odT[:, c * HL + i * 8: c * HL + (i + 1) * 8]
                S.op("vector", lambda e, yob=yob, odT_sl=odT_sl: e.tensor_tensor(
                    t_tmp[:].rearrange("l (h p) -> l h p", p=HEADDIM),
                    yob[0:CHUNK, :].rearrange("l (h p) -> l h p", p=HEADDIM),
                    odT_sl[:, :, None].broadcast_to([CHUNK, 8, HEADDIM]), ALU.mult))
                S.drain("vector")
                S.op("vector", lambda e, ydb=ydb, i=i: e.tensor_tensor(
                    t_Y[:, i * 512:(i + 1) * 512], t_tmp[:], ydb[0:CHUNK, :], ALU.add))
                v = S.sig("vector", "vd")
                if i < 2:
                    comb_vd[i] = v
            rec["ydiag"][c] = S.count("tp")
            rec["comb"][c] = S.count("vd")

            S.wait("gpsimd", "vd", rec["comb"][c])
            rec["yst"][c] = S.dma("gpsimd", Y_out[sl0:sl0 + CHUNK, :], t_Y[:], sem="sy")

            S.wait("tensor", "vd", S.count("vd"))
            for i in range(4):
                stb = pb[4 + i]
                for hh in range(8):
                    h = i * 8 + hh
                    S.op("tensor", lambda e, stb=stb, h=h, hh=hh, c=c: nc.tensor.matmul(
                        stb[:, hh * 64:(hh + 1) * 64],
                        t_Btk[:, c * SSD_N:(c + 1) * SSD_N],
                        t_Xdc[:, h * 64:(h + 1) * 64],
                        start=(hh == 0), stop=(hh == 7)))
                S.sig("tensor", "tp")
            rec["states"][c] = S.count("tp")
            S.wait("tensor", "vd", rec["comb"][c])
            hup_vd = [0, 0]
            for i in range(4):
                erb = pb[0 + i % 2]
                if i >= 2:
                    S.wait("tensor", "vd", hup_vd[i - 2])
                S.op("tensor", lambda e, erb=erb, i=i: nc.tensor.matmul(
                    erb[:, :], t_one[:, :], t_Er[0:1, i * 512:(i + 1) * 512],
                    start=True, stop=True))
                S.sig("tensor", "tp")
                S.wait("vector", "tp")
                S.op("vector", lambda e, erb=erb, i=i: e.tensor_tensor(
                    t_H[:, i * 512:(i + 1) * 512], t_H[:, i * 512:(i + 1) * 512],
                    erb[:, :], ALU.mult))
                S.drain("vector")
                S.op("vector", lambda e, i=i: e.tensor_tensor(
                    t_H[:, i * 512:(i + 1) * 512], t_H[:, i * 512:(i + 1) * 512],
                    pb[4 + i][:, :], ALU.add))
                S.drain("vector")
                S.op("vector", lambda e, i=i: e.tensor_copy(
                    t_Hb[:, i * 512:(i + 1) * 512], t_H[:, i * 512:(i + 1) * 512]))
                v = S.sig("vector", "vd")
                if i < 2:
                    hup_vd[i] = v
            rec["erep"][c] = S.count("tp")
            rec["hupd"][c] = S.count("vd")

        S.wait("gpsimd", "sy")
        S.build()
    return nc


def shard_B(xbc_full, dt_full, inputs):
    A_log = np.asarray(inputs["L0_attn_A_log"], np.float32)
    D_par = np.asarray(inputs["L0_attn_D_param"], np.float32)
    l = np.arange(CHUNK)
    m = np.where(l[None, :] >= l[:, None], 0.0, -1e9).astype(np.float32)
    mask = np.ascontiguousarray(np.tile(m[:, None, :], (1, 32, 1)).reshape(CHUNK, 2048))
    idf = np.eye(128, dtype=np.float32)
    maps = []
    for c in range(NCORES):
        b, hh = divmod(c, 2)
        X = np.ascontiguousarray(xbc_full[b][:, hh * 2048:(hh + 1) * 2048])
        Bm = xbc_full[b][:, 4096:4224]
        Cm = xbc_full[b][:, 4224:4352]
        dts = dt_full[b][:, hh * 32:(hh + 1) * 32]
        maps.append(dict(
            X=X, dtT=np.ascontiguousarray(dts.T), dtk=np.ascontiguousarray(dts),
            BT=np.ascontiguousarray(Bm.T), CT=np.ascontiguousarray(Cm.T),
            Btk=np.ascontiguousarray(Bm),
            alog=np.ascontiguousarray(A_log[hh * 32:(hh + 1) * 32].reshape(-1, 1)),
            mask=mask, idf=idf))
    return maps


def build_C():
    nc = make_nc()
    YT_in = nc.dram_tensor("YT", [D_INNER, TH], F32, kind="ExternalInput").ap()
    z_in = nc.dram_tensor("z", [D_INNER, TH], F32, kind="ExternalInput").ap()
    X_in = nc.dram_tensor("Xfm", [D_INNER, TH], F32, kind="ExternalInput").ap()
    dp_in = nc.dram_tensor("dpar", [128, 32], F32, kind="ExternalInput").ap()
    nw_in = nc.dram_tensor("nw", [128, 32], F32, kind="ExternalInput").ap()
    wo_in = nc.dram_tensor("w_out", [D_INNER, D_MODEL], F32, kind="ExternalInput").ap()
    x0_in = nc.dram_tensor("x0T", [D_MODEL, TH], F32, kind="ExternalInput").ap()
    r0_out = nc.dram_tensor("r0", [D_MODEL, TH], F32, kind="ExternalOutput").ap()

    NB = 32
    ctx, S, sbuf, psum = sched_ctx(nc)
    with ctx:
        t_Y = sbuf("t_Y", [128, TH])
        t_z = sbuf("t_z", [128, TH])
        t_X = sbuf("t_X", [128, TH])
        t_dp = sbuf("t_dp", [128, 32])
        t_nw = sbuf("t_nw", [128, 32])
        t_yf = sbuf("t_yf", [128, NB * TH], BF16)
        t_sq = [sbuf(f"t_sq{i}", [128, TH]) for i in range(2)]
        t_ms = sbuf("t_ms", [1, TH])
        t_one = sbuf("t_one", [1, 128])
        t_onec = sbuf("t_onec", [128, 1])
        t_w = sbuf("t_w", [128, 2 * NB * 128], BF16)
        t_x0 = t_sq[1]
        t_ev = t_sq[0]
        p_st = [psum(f"p_st{i}", [1, 512]) for i in range(4)]
        p_rp = [psum(f"p_rp{i}", [128, 512]) for i in range(2)]
        p_mm = [psum(f"p_mm{i}", [128, 512]) for i in range(2)]

        for s in ("ld", "lwa", "lwb", "vd", "sa", "tp", "st"):
            S.sem(s)

        S.dma("sync", t_dp[:], dp_in, sem="ld")
        S.dma("sync", t_nw[:], nw_in, sem="ld")
        S.op("vector", lambda e: e.memset(t_one[:], 1.0))
        S.op("vector", lambda e: e.memset(t_onec[:], 1.0))
        S.sig("vector", "vd")
        S.wait("tensor", "vd")

        sq_tp = {}
        for blk in range(NB):
            if blk >= 1:
                S.wait("sync", "sa")
            S.dma("sync", t_Y[:], YT_in[blk * 128:(blk + 1) * 128, :], sem="ld")
            S.dma("sync", t_z[:], z_in[blk * 128:(blk + 1) * 128, :], sem="ld")
            S.dma("sync", t_X[:], X_in[blk * 128:(blk + 1) * 128, :], sem="ld")
            S.wait("vector", "ld")
            S.op("vector", lambda e, blk=blk: e.scalar_tensor_tensor(
                t_Y[:], t_X[:], t_dp[:, blk:blk + 1], t_Y[:], ALU.mult, ALU.add))
            S.sig("vector", "vd")
            S.wait("scalar", "ld")
            S.wait("scalar", "vd")
            S.op("scalar", lambda e: e.activation(t_z[:], t_z[:], AF.Silu))
            S.sig("scalar", "sa")
            S.wait("vector", "sa")
            sqb = t_sq[blk % 2]
            if blk >= 2:
                S.wait("vector", "tp", sq_tp[blk - 2])
            S.op("vector", lambda e, sqb=sqb: e.tensor_tensor(
                sqb[:], t_Y[:], t_z[:], ALU.mult))
            S.sig("vector", "vd")
            S.wait("scalar", "vd")
            S.op("scalar", lambda e, sqb=sqb, blk=blk: e.copy(
                t_yf[:, blk * TH:(blk + 1) * TH], sqb[:]))
            S.drain("scalar")
            S.op("scalar", lambda e, sqb=sqb: e.activation(sqb[:], sqb[:], AF.Square))
            S.sig("scalar", "sa")
            S.wait("tensor", "sa")
            for tt_i in range(4):
                S.op("tensor", lambda e, sqb=sqb, tt_i=tt_i, blk=blk: nc.tensor.matmul(
                    p_st[tt_i][:, :], t_onec[:, :],
                    sqb[:, tt_i * 512:(tt_i + 1) * 512],
                    start=(blk == 0), stop=(blk == NB - 1), skip_group_check=True))
            S.sig("tensor", "tp")
            sq_tp[blk] = S.count("tp")

        S.wait("vector", "tp")
        for tt_i in range(4):
            S.op("vector", lambda e, tt_i=tt_i: e.tensor_scalar(
                t_ms[:, tt_i * 512:(tt_i + 1) * 512], p_st[tt_i][:, :],
                1.0 / D_INNER, 1e-5, ALU.mult, ALU.add))
        S.drain("vector")
        S.op("vector", lambda e: e.reciprocal(t_ms[:], t_ms[:]))
        S.sig("vector", "vd")
        S.wait("scalar", "vd")
        S.op("scalar", lambda e: e.activation(t_ms[:], t_ms[:], AF.Sqrt))
        S.sig("scalar", "sa")

        S.wait("tensor", "sa")
        yn_vd = {}
        rep_done = {}
        for tt_i in range(4):
            if tt_i >= 2:
                S.wait("tensor", "vd", rep_done[tt_i - 2])
            S.op("tensor", lambda e, tt_i=tt_i: nc.tensor.matmul(
                p_rp[tt_i % 2][:, :], t_one[:, :], t_ms[:, tt_i * 512:(tt_i + 1) * 512],
                start=True, stop=True))
            S.sig("tensor", "tp")
            S.wait("vector", "tp")
            for blk in range(NB):
                sl = slice(blk * TH + tt_i * 512, blk * TH + (tt_i + 1) * 512)
                S.op("vector", lambda e, sl=sl, tt_i=tt_i: e.tensor_tensor(
                    t_yf[:, sl], t_yf[:, sl], p_rp[tt_i % 2][:, :], ALU.mult))
                S.drain("vector")
                S.op("vector", lambda e, sl=sl, blk=blk: e.tensor_scalar(
                    t_yf[:, sl], t_yf[:, sl], t_nw[:, blk:blk + 1], None, ALU.mult))
                S.drain("vector")
            rep_done[tt_i] = S.sig("vector", "vd")
        yn_vd[NB - 1] = S.count("vd")

        ev_vd = {}
        nmm = 0
        for do in range(16):
            sw = do % 2
            if do >= 2:
                S.wait("gpsimd", "tp", mm_tp[do - 2])
            S.dma("gpsimd", t_w[:, sw * NB * 128:(sw + 1) * NB * 128]
                  .rearrange("p (a j) -> p a j", j=128),
                  wo_in[:, do * 128:(do + 1) * 128].rearrange("(a p) j -> p a j", p=128),
                  sem=("lwa" if sw == 0 else "lwb"))
            if do == 0:
                S.wait("sync", "tp", sq_tp[NB - 1])
            if do >= 1:
                S.wait("sync", "vd", ev_vd[do - 1])
            S.dma("sync", t_x0[:], x0_in[do * 128:(do + 1) * 128, :], sem="ld")
            S.wait("tensor", "lwa" if sw == 0 else "lwb", 16 * (do // 2 + 1))
            S.wait("tensor", "vd", yn_vd[NB - 1])
            if do >= 1:
                S.wait("vector", "st", st_after_c[do - 1])
            for tt_i in range(4):
                pm = p_mm[nmm % 2]
                if nmm >= 2:
                    S.wait("tensor", "vd", mm_vd[nmm - 2])
                for k in range(NB):
                    S.op("tensor", lambda e, pm=pm, sw=sw, k=k, tt_i=tt_i:
                         nc.tensor.matmul(pm[:, :],
                                          t_w[:, sw * NB * 128 + k * 128:(sw) * NB * 128 + (k + 1) * 128],
                                          t_yf[:, k * TH + tt_i * 512: k * TH + tt_i * 512 + 512],
                                          start=(k == 0), stop=(k == NB - 1)))
                S.sig("tensor", "tp")
                S.wait("vector", "tp")
                S.wait("vector", "ld")
                S.op("vector", lambda e, pm=pm, tt_i=tt_i: e.tensor_tensor(
                    t_ev[:, tt_i * 512:(tt_i + 1) * 512], pm[:, :], t_x0[:, tt_i * 512:(tt_i + 1) * 512], ALU.add))
                v = S.sig("vector", "vd")
                if nmm == 0:
                    mm_vd = {}
                mm_vd[nmm] = v
                nmm += 1
            ev_vd[do] = S.count("vd")
            if do == 0:
                mm_tp = {}
            mm_tp[do] = S.count("tp")
            S.wait("gpsimd", "vd", ev_vd[do])
            if do == 0:
                st_after_c = {}
            st_after_c[do] = S.dma("gpsimd", r0_out[do * 128:(do + 1) * 128, :],
                                   t_ev[:], sem="st")

        S.wait("gpsimd", "st")
        S.build()
    return nc


def kernel(**inputs):
    raise NotImplementedError("under construction")


def build_D(NTG=16):
    nc = make_nc()
    NSL = 2048
    xnT_in = nc.dram_tensor("xnT", [D_MODEL, 16384], BF16, kind="ExternalInput").ap()
    gpk_in = nc.dram_tensor("gpk", [NSL, 1024], I32, kind="ExternalInput").ap()
    gsc_in = nc.dram_tensor("gsc", [NSL, 64], F32, kind="ExternalInput").ap()
    upk_in = nc.dram_tensor("upk", [NSL, 1024], I32, kind="ExternalInput").ap()
    usc_in = nc.dram_tensor("usc", [NSL, 64], F32, kind="ExternalInput").ap()
    dpk_in = nc.dram_tensor("dpk", [D_MODEL, 1024], I32, kind="ExternalInput").ap()
    dsc_in = nc.dram_tensor("dsc", [D_MODEL, 64], F32, kind="ExternalInput").ap()
    idb_in = nc.dram_tensor("idb", [128, 128], BF16, kind="ExternalInput").ap()
    part_out = nc.dram_tensor("part", [D_MODEL, NTG * 1024], F32, kind="ExternalOutput").ap()
    wgt_s = nc.dram_tensor("wgt_s", [D_MODEL, NSL], BF16, kind="Internal").ap()
    wut_s = nc.dram_tensor("wut_s", [D_MODEL, NSL], BF16, kind="Internal").ap()
    wdt_s = nc.dram_tensor("wdt_s", [NSL, D_MODEL], BF16, kind="Internal").ap()

    C1 = 0.044715
    C2 = 0.7978845608028654
    ctx, S, sbuf, psum = sched_ctx(nc)
    with ctx:
        t_pk = sbuf("t_pk", [128, 2 * 1024], I32)
        t_sc = sbuf("t_sc", [128, 64])
        t_w1 = sbuf("t_w1", [128, 1024], I32)
        t_w2 = sbuf("t_w2", [128, 1024], I32)
        t_w3 = sbuf("t_w3", [128, 1024], I32)
        t_dq = [sbuf(f"t_dq{i}", [128, D_MODEL], BF16) for i in range(2)]
        t_id = sbuf("t_id", [128, 128], BF16)
        t_tr = [sbuf(f"t_tr{i}", [128, 16 * 128], BF16) for i in range(2)]
        t_xn = sbuf("t_xn", [128, 16 * 1024], BF16)
        t_it = sbuf("t_it", [128, 16 * 1024], BF16)
        t_ws = [sbuf(f"t_ws{i}", [128, 16 * 128], BF16) for i in range(4)]
        t_g = sbuf("t_g", [128, 2 * 512])
        t_s1 = sbuf("t_s1", [128, 2 * 512])
        t_s3 = sbuf("t_s3", [128, 2 * 512])
        t_ev = sbuf("t_ev", [128, 4 * 512])
        p_t = [psum(f"p_t{i}", [128, 1024], BF16) for i in range(2)]
        p_g = [psum(f"p_g{i}", [128, 512]) for i in range(2)]
        p_u = [psum(f"p_u{i}", [128, 512]) for i in range(2)]
        p_d = [psum(f"p_d{i}", [128, 512]) for i in range(2)]

        for s in ("ld", "lw0", "lw1", "lw2", "lw3", "lx", "vd", "sa", "tp", "st", "sc_"):
            S.sem(s)

        S.dma("sync", t_id[:], idb_in, sem="ld")

        def decode(pk_in, sc_in, scr, scale_mult, tag, _g={"kt": 0, "vd": {}}):
            for rt in range(16):
                sl = rt % 2
                pkv = t_pk[:, sl * 1024:(sl + 1) * 1024]
                S.wait("sync", "vd")
                S.dma("sync", pkv, pk_in[rt * 128:(rt + 1) * 128, :], sem="ld")
                S.dma("sync", t_sc[:], sc_in[rt * 128:(rt + 1) * 128, :], sem="ld")
                S.wait("vector", "ld")
                dq = t_dq[rt % 2]
                if rt >= 2:
                    S.wait("vector", "tp")
                S.op("vector", lambda e: e.tensor_scalar(
                    t_sc[:], t_sc[:], scale_mult, None, ALU.mult))
                S.op("vector", lambda e, pkv=pkv: e.tensor_scalar(
                    t_w1[:], pkv, 7, 22, ALU.bitwise_and, ALU.logical_shift_left))
                S.op("vector", lambda e, pkv=pkv: e.tensor_scalar(
                    t_w2[:], pkv, 8, 28, ALU.bitwise_and, ALU.logical_shift_left))
                S.drain("vector")
                S.op("vector", lambda e: e.tensor_tensor(
                    t_w1[:], t_w1[:], t_w2[:], ALU.bitwise_or))
                S.drain("vector")
                S.op("vector", lambda e, dq=dq: e.tensor_tensor(
                    dq[:, 0:D_MODEL:2].rearrange("p (a b) -> p a b", a=64),
                    t_w1[:].bitcast(F32).rearrange("p (a b) -> p a b", a=64),
                    t_sc[:, :, None].broadcast_to([128, 64, 16]), ALU.mult))
                S.op("vector", lambda e, pkv=pkv: e.tensor_scalar(
                    t_w3[:], pkv, 4, 15, ALU.logical_shift_right, ALU.bitwise_and))
                S.drain("vector")
                S.op("vector", lambda e: e.tensor_scalar(
                    t_w1[:], t_w3[:], 7, 22, ALU.bitwise_and, ALU.logical_shift_left))
                S.op("vector", lambda e: e.tensor_scalar(
                    t_w2[:], t_w3[:], 8, 28, ALU.bitwise_and, ALU.logical_shift_left))
                S.drain("vector")
                S.op("vector", lambda e: e.tensor_tensor(
                    t_w1[:], t_w1[:], t_w2[:], ALU.bitwise_or))
                S.drain("vector")
                S.op("vector", lambda e, dq=dq: e.tensor_tensor(
                    dq[:, 1:D_MODEL:2].rearrange("p (a b) -> p a b", a=64),
                    t_w1[:].bitcast(F32).rearrange("p (a b) -> p a b", a=64),
                    t_sc[:, :, None].broadcast_to([128, 64, 16]), ALU.mult))
                S.sig("vector", "vd")
                S.wait("tensor", "vd")
                grt = _g["kt"]; _g["kt"] += 1
                trb = t_tr[grt % 2]
                if grt >= 2:
                    S.wait("vector", "st")
                for kt in range(16):
                    pt = p_t[kt % 2]
                    gidx = kt - 2 if kt >= 2 else 14 + kt
                    if gidx in _g["vd"] and (kt >= 2 or grt > 0):
                        S.wait("tensor", "vd", _g["vd"][gidx])
                    S.op("tensor", lambda e, dq=dq, kt=kt, pt=pt: nc.tensor.transpose(
                        pt[:, 0:128], dq[:, kt * 128:(kt + 1) * 128], t_id[:]))
                    S.sig("tensor", "tp")
                    S.wait("vector", "tp")
                    S.op("vector", lambda e, trb=trb, pt=pt, kt=kt: e.tensor_copy(
                        trb[:, kt * 128:(kt + 1) * 128], pt[:, 0:128]))
                    _g["vd"][kt] = S.sig("vector", "vd")
                S.wait("gpsimd", "vd", S.count("vd"))
                S.dma("gpsimd", scr[:, rt * 128:(rt + 1) * 128].rearrange(
                          "(a p) n -> p a n", p=128),
                      trb[:].rearrange("p (a n) -> p a n", n=128), sem="st")
                S.sig("vector", "vd")

        decode(gpk_in, gsc_in, wgt_s, float(2.0 ** 126), "g")
        decode(upk_in, usc_in, wut_s, float(2.0 ** 126), "u")
        decode(dpk_in, dsc_in, wdt_s, float(0.5 * 2.0 ** 126), "d")
        ph1_st = S.count("st")
        ph1_vd = S.count("vd")

        S.wait("sync", "st", ph1_st)
        S.wait("scalar", "st", ph1_st)
        rec = {"it": {}, "gu": {}, "pd": {}, "ev": {}, "dev": {}, "dst": {}}
        ws_done = {}
        nws = 0
        nmm = 0
        for tg in range(NTG):
            if tg >= 1:
                S.wait("sync", "tp", rec["gu"][tg - 1])
            S.dma("sync", t_xn[:].rearrange("p (a t) -> p a t", t=1024),
                  xnT_in[:, tg * 1024:(tg + 1) * 1024].rearrange("(a p) t -> p a t", p=128),
                  sem="lx")
            for ntile in range(16):
                ig = nws; wsg = t_ws[nws % 4]; nws += 1
                iu = nws; wsu = t_ws[nws % 4]; nws += 1
                gsem = f"lw{ig % 4}"
                usem = f"lw{iu % 4}"
                if ig - 4 in ws_done:
                    S.wait("scalar", "tp", ws_done[ig - 4])
                if iu - 4 in ws_done:
                    S.wait("scalar", "tp", ws_done[iu - 4])
                S.dma("scalar", wsg[:].rearrange("p (a n) -> p a n", n=128),
                      wgt_s[:, ntile * 128:(ntile + 1) * 128].rearrange("(a p) n -> p a n", p=128),
                      sem=gsem)
                S.dma("scalar", wsu[:].rearrange("p (a n) -> p a n", n=128),
                      wut_s[:, ntile * 128:(ntile + 1) * 128].rearrange("(a p) n -> p a n", p=128),
                      sem=usem)
                S.wait("tensor", gsem)
                S.wait("tensor", usem)
                S.wait("tensor", "lx")
                for tt2 in range(2):
                    pg = p_g[tt2]
                    pu = p_u[tt2]
                    if nmm >= 2:
                        S.wait("tensor", "vd", rec["ev"][nmm - 2])
                    for k in range(16):
                        S.op("tensor", lambda e, pg=pg, wsg=wsg, k=k, tt2=tt2:
                             nc.tensor.matmul(pg[:, :], wsg[:, k * 128:(k + 1) * 128],
                                              t_xn[:, k * 1024 + tt2 * 512: k * 1024 + tt2 * 512 + 512],
                                              start=(k == 0), stop=(k == 15)))
                    for k in range(16):
                        S.op("tensor", lambda e, pu=pu, wsu=wsu, k=k, tt2=tt2:
                             nc.tensor.matmul(pu[:, :], wsu[:, k * 128:(k + 1) * 128],
                                              t_xn[:, k * 1024 + tt2 * 512: k * 1024 + tt2 * 512 + 512],
                                              start=(k == 0), stop=(k == 15)))
                    S.sig("tensor", "tp")
                    gs = slice((nmm % 2) * 512, (nmm % 2) * 512 + 512)
                    S.wait("scalar", "tp")
                    if nmm >= 2:
                        S.wait("scalar", "vd", rec["ev"][nmm - 2])
                    S.op("scalar", lambda e, pg=pg, gs=gs: e.copy(t_g[:, gs], pg[:, :]))
                    S.drain("scalar")
                    S.op("scalar", lambda e, gs=gs: e.activation(t_s1[:, gs], t_g[:, gs], AF.Square))
                    S.sig("scalar", "sa")
                    S.wait("vector", "sa")
                    S.op("vector", lambda e, gs=gs: e.tensor_scalar(
                        t_s1[:, gs], t_s1[:, gs], C1, 1.0, ALU.mult, ALU.add))
                    S.drain("vector")
                    S.op("vector", lambda e, gs=gs: e.tensor_tensor(
                        t_s3[:, gs], t_s1[:, gs], t_g[:, gs], ALU.mult))
                    S.sig("vector", "vd")
                    S.wait("scalar", "vd")
                    S.op("scalar", lambda e, gs=gs: e.activation(t_s3[:, gs], t_s3[:, gs], AF.Tanh, scale=C2))
                    S.sig("scalar", "sa")
                    S.wait("vector", "sa")
                    S.op("vector", lambda e, pu=pu, gs=gs: e.tensor_tensor(
                        t_s1[:, gs], t_g[:, gs], pu[:, :], ALU.mult))
                    S.drain("vector")
                    S.op("vector", lambda e, ntile=ntile, tt2=tt2, gs=gs: e.scalar_tensor_tensor(
                        t_it[:, ntile * 1024 + tt2 * 512: ntile * 1024 + tt2 * 512 + 512],
                        t_s3[:, gs], 1.0, t_s1[:, gs], ALU.add, ALU.mult))
                    v = S.sig("vector", "vd")
                    rec["ev"][nmm] = v
                    nmm += 1
                ws_done[ig] = ws_done[iu] = S.count("tp")
            rec["gu"][tg] = S.count("tp")
            it_vd = S.count("vd")

            S.wait("tensor", "vd", it_vd)
            for htile in range(16):
                idd = nws; wsd = t_ws[nws % 4]; nws += 1
                dsem = f"lw{idd % 4}"
                if idd - 4 in ws_done:
                    S.wait("scalar", "tp", ws_done[idd - 4])
                S.dma("scalar", wsd[:].rearrange("p (a h) -> p a h", h=128),
                      wdt_s[:, htile * 128:(htile + 1) * 128].rearrange("(a p) h -> p a h", p=128),
                      sem=dsem)
                S.wait("tensor", dsem)
                for tt2 in range(2):
                    pd = p_d[tt2]
                    ndn = htile * 2 + tt2
                    if ndn >= 2:
                        S.wait("tensor", "vd", rec["dev"][(tg, ndn - 2)])
                    for k in range(16):
                        S.op("tensor", lambda e, pd=pd, wsd=wsd, k=k, tt2=tt2:
                             nc.tensor.matmul(pd[:, :], wsd[:, k * 128:(k + 1) * 128],
                                              t_it[:, k * 1024 + tt2 * 512: k * 1024 + tt2 * 512 + 512],
                                              start=(k == 0), stop=(k == 15)))
                    S.sig("tensor", "tp")
                    S.wait("vector", "tp")
                    evs = ndn % 4
                    if ndn >= 4 or tg > 0:
                        S.wait("vector", "st")
                    S.op("vector", lambda e, pd=pd, evs=evs: e.tensor_copy(
                        t_ev[:, evs * 512:(evs + 1) * 512], pd[:, :]))
                    v = S.sig("vector", "vd")
                    rec["dev"][(tg, ndn)] = v
                    S.wait("gpsimd", "vd", v)
                    rec["dst"][(tg, ndn)] = S.dma(
                        "gpsimd", part_out[htile * 128:(htile + 1) * 128,
                                           tg * 1024 + tt2 * 512: tg * 1024 + tt2 * 512 + 512],
                        t_ev[:, evs * 512:(evs + 1) * 512], sem="st")
                ws_done[idd] = S.count("tp")

        S.wait("gpsimd", "st")
        S.build()
    return nc


def shard_D(xnT_full, inputs):
    gpk = np.asarray(inputs["L0_ffn_gate_W_packed"], np.int32)
    gsc = np.asarray(inputs["L0_ffn_gate_scales"], np.float32)
    upk = np.asarray(inputs["L0_ffn_up_W_packed"], np.int32)
    usc = np.asarray(inputs["L0_ffn_up_scales"], np.float32)
    dpk = np.asarray(inputs["L0_ffn_down_W_packed"], np.int32)
    dsc = np.asarray(inputs["L0_ffn_down_scales"], np.float32)
    idb = id128_bf()
    maps = []
    for c in range(NCORES):
        maps.append(dict(
            xnT=xnT_full,
            gpk=np.ascontiguousarray(gpk[c * 2048:(c + 1) * 2048]),
            gsc=np.ascontiguousarray(gsc[c * 2048:(c + 1) * 2048]),
            upk=np.ascontiguousarray(upk[c * 2048:(c + 1) * 2048]),
            usc=np.ascontiguousarray(usc[c * 2048:(c + 1) * 2048]),
            dpk=np.ascontiguousarray(dpk[:, c * 1024:(c + 1) * 1024]),
            dsc=np.ascontiguousarray(dsc[:, c * 64:(c + 1) * 64]),
            idb=idb))
    return maps




def build_E():
    nc = make_nc()
    r0_in = nc.dram_tensor("r0", [D_MODEL, TH], F32, kind="ExternalInput").ap()
    parts_in = [nc.dram_tensor(f"pp{i}", [D_MODEL, TH], F32, kind="ExternalInput").ap()
                for i in range(8)]
    wbcT_in = nc.dram_tensor("wbcT", [D_MODEL, 128], F32, kind="ExternalInput").ap()
    wdtT_in = nc.dram_tensor("wdtT", [DT_RANK, D_MODEL], F32, kind="ExternalInput").ap()
    bdt_in = nc.dram_tensor("bdt", [128, 16], F32, kind="ExternalInput").ap()
    rw_in = nc.dram_tensor("rw", [D_MODEL, NEXP], F32, kind="ExternalInput").ap()
    rb_in = nc.dram_tensor("rb", [NEXP, 1], F32, kind="ExternalInput").ap()
    idf_in = nc.dram_tensor("idf", [128, 128], F32, kind="ExternalInput").ap()
    x1_out = nc.dram_tensor("x1", [D_MODEL, TH], F32, kind="ExternalOutput").ap()
    xn1_out = nc.dram_tensor("xn1", [D_MODEL, TH], F32, kind="ExternalOutput").ap()
    dl_out = nc.dram_tensor("dl", [D_MODEL, TH], F32, kind="ExternalOutput").ap()
    bc_out = nc.dram_tensor("bc", [64, TH], F32, kind="ExternalOutput").ap()
    s_out = nc.dram_tensor("s", [TH, NEXP], F32, kind="ExternalOutput").ap()

    ctx, S, sbuf, psum = sched_ctx(nc)
    with ctx:
        t_a = sbuf("t_a", [128, TH])
        t_b = sbuf("t_b", [128, TH])
        t_sqs = [sbuf(f"t_sqs{i}", [128, TH]) for i in range(2)]
        t_arep = sbuf("t_arep", [128, TH])
        t_brep = sbuf("t_brep", [128, TH])
        t_row = sbuf("t_row", [1, 2 * TH])
        t_one = sbuf("t_one", [1, 128])
        t_onec = sbuf("t_onec", [128, 1])
        t_xnb = sbuf("t_xnb", [128, 16 * TH], BF16)
        t_wbc = sbuf("t_wbc", [128, 16 * 128], BF16)
        t_wdt = sbuf("t_wdt", [DT_RANK, 16 * 128], BF16)
        t_dtr = sbuf("t_dtr", [DT_RANK, TH], BF16)
        t_bdt = sbuf("t_bdt", [128, 16])
        t_rw = sbuf("t_rw", [128, 16 * NEXP], BF16)
        t_rb = sbuf("t_rb", [NEXP, 1])
        t_id = sbuf("t_id", [128, 128])
        t_sp1 = sbuf("t_sp1", [128, 512])
        t_sp2 = sbuf("t_sp2", [128, 512])
        t_sm = sbuf("t_sm", [128, 16 * 16])
        t_mx = sbuf("t_mx", [128, 2])
        p_s1 = psum("p_s1", [1, 512])
        p_s2 = psum("p_s2", [1, 512])
        p_a = [psum(f"p_a{i}", [128, 512]) for i in range(2)]
        p_b = [psum(f"p_b{i}", [128, 512]) for i in range(2)]

        for s in ("ld", "lp", "vd", "sa", "tp", "st"):
            S.sem(s)

        S.dma("sync", t_id[:], idf_in, sem="ld")
        S.dma("sync", t_bdt[:], bdt_in, sem="ld")
        S.dma("sync", t_rb[:], rb_in, sem="ld")
        S.dma("gpsimd", t_wbc[:].rearrange("p (a j) -> p a j", j=128),
              wbcT_in.rearrange("(a p) j -> p a j", p=128), sem="lp")
        S.dma("gpsimd", t_wdt[:].rearrange("p (a j) -> p a j", j=128),
              wdtT_in.rearrange("p (a j) -> p a j", j=128), sem="lp")
        S.dma("gpsimd", t_rw[:].rearrange("p (a j) -> p a j", j=NEXP),
              rw_in.rearrange("(a p) j -> p a j", p=128), sem="lp")
        S.op("vector", lambda e: e.memset(t_one[:], 1.0))
        S.op("vector", lambda e: e.memset(t_onec[:], 1.0))
        S.op("vector", lambda e: e.memset(t_row[:], 0.0))
        S.sig("vector", "vd")
        S.wait("tensor", "vd")

        st_p1 = {}
        for blk in range(16):
            if blk >= 1:
                S.wait("sync", "tp", st_p1[blk - 1])
                S.wait("sync", "st")
            S.dma("sync", t_a[:], r0_in[blk * 128:(blk + 1) * 128, :], sem="ld")
            for i in range(8):
                S.dma("sync", t_b[:], parts_in[i][blk * 128:(blk + 1) * 128, :], sem="ld")
                S.wait("vector", "ld")
                S.op("vector", lambda e: e.tensor_tensor(t_a[:], t_a[:], t_b[:], ALU.add))
                S.sig("vector", "vd")
                S.wait("sync", "vd")
            S.wait("gpsimd", "vd")
            S.dma("gpsimd", x1_out[blk * 128:(blk + 1) * 128, :], t_a[:], sem="st")
            sqb = t_sqs[blk % 2]
            if blk >= 2:
                S.wait("scalar", "tp", st_p1[blk - 2])
            S.wait("scalar", "vd")
            S.op("scalar", lambda e, sqb=sqb: e.activation(sqb[:], t_a[:], AF.Square))
            S.sig("scalar", "sa")
            S.wait("tensor", "sa")
            for tt_i in range(4):
                if blk > 0 or tt_i > 0:
                    S.wait("tensor", "vd")
                S.op("tensor", lambda e, tt_i=tt_i: nc.tensor.matmul(
                    p_s1[:, :], t_onec[:, :], t_a[:, tt_i * 512:(tt_i + 1) * 512],
                    start=True, stop=True))
                S.op("tensor", lambda e, sqb=sqb, tt_i=tt_i: nc.tensor.matmul(
                    p_s2[:, :], t_onec[:, :], sqb[:, tt_i * 512:(tt_i + 1) * 512],
                    start=True, stop=True))
                S.sig("tensor", "tp")
                S.wait("vector", "tp")
                sl2 = slice(TH + tt_i * 512, TH + (tt_i + 1) * 512)
                sl = slice(tt_i * 512, (tt_i + 1) * 512)
                S.op("vector", lambda e, sl2=sl2: e.tensor_tensor(
                    t_row[0:1, sl2], t_row[0:1, sl2], p_s1[:, :], ALU.add))
                S.op("vector", lambda e, sl=sl: e.tensor_tensor(
                    t_row[0:1, sl], t_row[0:1, sl], p_s2[:, :], ALU.add))
                S.sig("vector", "vd")
            S.sig("tensor", "tp")
            st_p1[blk] = S.count("tp")

        S.drain("vector")
        S.op("vector", lambda e: e.tensor_scalar(
            t_row[0:1, 0:2 * TH], t_row[0:1, 0:2 * TH], 1.0 / D_MODEL, None, ALU.mult))
        S.drain("vector")
        S.op("vector", lambda e: e.tensor_tensor(
            t_b[0:1, 0:TH], t_row[0:1, TH:2 * TH], t_row[0:1, TH:2 * TH], ALU.mult))
        S.drain("vector")
        S.op("vector", lambda e: e.tensor_tensor(
            t_row[0:1, 0:TH], t_row[0:1, 0:TH], t_b[0:1, 0:TH], ALU.subtract))
        S.drain("vector")
        S.op("vector", lambda e: e.tensor_scalar(
            t_row[0:1, 0:TH], t_row[0:1, 0:TH], 1.0, 1e-5, ALU.mult, ALU.add))
        S.drain("vector")
        S.op("vector", lambda e: e.reciprocal(t_row[0:1, 0:TH], t_row[0:1, 0:TH]))
        S.sig("vector", "vd")
        S.wait("scalar", "vd")
        S.op("scalar", lambda e: e.activation(t_row[0:1, 0:TH], t_row[0:1, 0:TH], AF.Sqrt))
        S.sig("scalar", "sa")
        S.wait("vector", "sa")
        S.op("vector", lambda e: e.tensor_tensor(
            t_row[0:1, TH:2 * TH], t_row[0:1, TH:2 * TH], t_row[0:1, 0:TH], ALU.mult))
        S.drain("vector")
        S.op("vector", lambda e: e.tensor_scalar(
            t_row[0:1, TH:2 * TH], t_row[0:1, TH:2 * TH], -1.0, None, ALU.mult))
        S.sig("vector", "vd")
        rows_vd = S.count("vd")

        S.wait("tensor", "vd", rows_vd)
        rep_tp = {}
        for tt_i in range(4):
            if tt_i >= 2:
                S.wait("tensor", "vd", rep_ev[tt_i - 2])
            S.op("tensor", lambda e, tt_i=tt_i: nc.tensor.matmul(
                p_a[tt_i % 2][:, :], t_one[:, :], t_row[0:1, tt_i * 512:(tt_i + 1) * 512],
                start=True, stop=True))
            S.op("tensor", lambda e, tt_i=tt_i: nc.tensor.matmul(
                p_b[tt_i % 2][:, :], t_one[:, :], t_row[0:1, TH + tt_i * 512:TH + (tt_i + 1) * 512],
                start=True, stop=True))
            S.sig("tensor", "tp")
            rep_tp[tt_i] = S.count("tp")
            S.wait("vector", "tp", rep_tp[tt_i])
            S.op("vector", lambda e, tt_i=tt_i: e.tensor_copy(
                t_arep[:, tt_i * 512:(tt_i + 1) * 512], p_a[tt_i % 2][:, :]))
            S.op("vector", lambda e, tt_i=tt_i: e.tensor_copy(
                t_brep[:, tt_i * 512:(tt_i + 1) * 512], p_b[tt_i % 2][:, :]))
            v = S.sig("vector", "vd")
            if tt_i == 0:
                rep_ev = {}
            rep_ev[tt_i] = v
        rep_all = S.count("vd")
        xn_vd = {}
        for blk in range(16):
            if blk == 0:
                S.wait("sync", "tp", st_p1[15])
                S.wait("sync", "st")
            if blk >= 1:
                S.wait("sync", "vd", xn_vd[blk - 1])
            S.dma("sync", t_a[:], x1_out[blk * 128:(blk + 1) * 128, :], sem="ld")
            S.wait("vector", "ld")
            S.wait("vector", "vd", rep_all)
            S.wait("vector", "st")
            S.op("vector", lambda e: e.tensor_tensor(t_b[:], t_a[:], t_arep[:], ALU.mult))
            S.drain("vector")
            S.op("vector", lambda e: e.tensor_tensor(t_b[:], t_b[:], t_brep[:], ALU.add))
            S.drain("vector")
            S.op("vector", lambda e, blk=blk: e.tensor_copy(
                t_xnb[:, blk * TH:(blk + 1) * TH], t_b[:]))
            xn_vd[blk] = S.sig("vector", "vd")
            S.wait("gpsimd", "vd", xn_vd[blk])
            S.dma("gpsimd", xn1_out[blk * 128:(blk + 1) * 128, :], t_b[:], sem="st")
        xn_all = S.count("vd")

        S.wait("tensor", "vd", xn_all)
        S.wait("tensor", "lp")
        dbc_tp = {}
        for tt_i in range(4):
            pm = p_a[tt_i % 2]
            if tt_i >= 2:
                S.wait("tensor", "vd", dbc_vd[tt_i - 2])
            for k in range(16):
                S.op("tensor", lambda e, pm=pm, k=k, tt_i=tt_i: nc.tensor.matmul(
                    pm[:, :], t_wbc[:, k * 128:(k + 1) * 128],
                    t_xnb[:, k * TH + tt_i * 512: k * TH + tt_i * 512 + 512],
                    start=(k == 0), stop=(k == 15)))
            S.sig("tensor", "tp")
            S.wait("vector", "tp")
            sl = slice(tt_i * 512, (tt_i + 1) * 512)
            S.op("vector", lambda e, pm=pm, sl=sl: e.tensor_copy(t_dtr[:, sl], pm[0:64, :]))
            S.op("vector", lambda e, pm=pm, sl=sl: e.tensor_copy(t_sqs[0][0:64, sl], pm[64:128, :]))
            v = S.sig("vector", "vd")
            if tt_i == 0:
                dbc_vd = {}
            dbc_vd[tt_i] = v
        S.wait("gpsimd", "vd", S.count("vd"))
        S.dma("gpsimd", bc_out[:, :], t_sqs[0][0:64, 0:TH], sem="st")

        dtr_all = S.count("vd")
        S.wait("tensor", "vd", dtr_all)
        for dtile in range(16):
            for tt_i in range(4):
                pm = p_b[(dtile * 4 + tt_i) % 2]
                if dtile * 4 + tt_i >= 2:
                    S.wait("tensor", "vd", dt_vd[dtile * 4 + tt_i - 2])
                S.op("tensor", lambda e, pm=pm, dtile=dtile, tt_i=tt_i: nc.tensor.matmul(
                    pm[:, :], t_wdt[:, dtile * 128:(dtile + 1) * 128],
                    t_dtr[:, tt_i * 512:(tt_i + 1) * 512],
                    start=True, stop=True))
                S.sig("tensor", "tp")
                S.wait("vector", "tp")
                S.op("vector", lambda e, pm=pm, dtile=dtile: e.tensor_scalar(
                    t_sp1[:, :], pm[:, :], t_bdt[:, dtile:dtile + 1], None, ALU.add))
                v = S.sig("vector", "vd")
                if dtile == 0 and tt_i == 0:
                    dt_vd = {}
                dt_vd[dtile * 4 + tt_i] = v
                S.wait("scalar", "vd", v)
                S.op("scalar", lambda e: e.activation(t_sp2[:, :], t_sp1[:, :], AF.Abs))
                S.drain("scalar")
                S.op("scalar", lambda e: e.activation(t_sp2[:, :], t_sp2[:, :], AF.Exp, scale=-1.0))
                S.drain("scalar")
                S.op("scalar", lambda e: e.activation(t_sp2[:, :], t_sp2[:, :], AF.Ln, bias=1.0))
                S.drain("scalar")
                S.op("scalar", lambda e: e.activation(t_sp1[:, :], t_sp1[:, :], AF.Relu))
                S.sig("scalar", "sa")
                S.wait("vector", "sa")
                S.op("vector", lambda e: e.tensor_tensor(t_sp1[:, :], t_sp1[:, :], t_sp2[:, :], ALU.add))
                v2 = S.sig("vector", "vd")
                S.wait("gpsimd", "vd", v2)
                S.dma("gpsimd", dl_out[dtile * 128:(dtile + 1) * 128,
                                       tt_i * 512:(tt_i + 1) * 512], t_sp1[:, :], sem="st")
                S.wait("vector", "st")
                S.wait("scalar", "st")

        S.wait("tensor", "vd", S.count("vd"))
        for tt_i in range(4):
            pm = p_a[tt_i % 2]
            if tt_i >= 2:
                S.wait("tensor", "vd", rt_vd[tt_i - 2])
            for k in range(16):
                S.op("tensor", lambda e, pm=pm, k=k, tt_i=tt_i: nc.tensor.matmul(
                    pm[0:NEXP, :], t_rw[:, k * NEXP:(k + 1) * NEXP],
                    t_xnb[:, k * TH + tt_i * 512: k * TH + tt_i * 512 + 512],
                    start=(k == 0), stop=(k == 15)))
            S.sig("tensor", "tp")
            S.wait("vector", "tp")
            S.op("vector", lambda e, pm=pm: e.tensor_scalar(
                t_sp1[0:NEXP, :], pm[0:NEXP, :], t_rb[:, 0:1], None, ALU.add))
            v = S.sig("vector", "vd")
            if tt_i == 0:
                rt_vd = {}
            rt_vd[tt_i] = v
            for q in range(4):
                gq = tt_i * 4 + q
                S.wait("tensor", "vd", v)
                if gq >= 2:
                    S.wait("tensor", "vd", rtr_ev[gq - 2])
                S.op("tensor", lambda e, q=q, tt_i=tt_i, gq=gq: nc.tensor.transpose(
                    p_b[gq % 2][0:128, 0:NEXP], t_sp1[0:NEXP, q * 128:(q + 1) * 128],
                    t_id[0:NEXP, 0:NEXP]))
                S.sig("tensor", "tp")
                S.wait("vector", "tp")
                ti = tt_i * 4 + q
                S.op("vector", lambda e, gq=gq, ti=ti: e.tensor_copy(
                    t_sm[:, ti * 16:(ti + 1) * 16], p_b[gq % 2][0:128, 0:NEXP]))
                ve = S.sig("vector", "vd")
                if gq == 0:
                    rtr_ev = {}
                rtr_ev[gq] = ve
        for ti in range(16):
            sm = t_sm[:, ti * 16:(ti + 1) * 16]
            S.drain("vector")
            S.op("vector", lambda e, sm=sm: e.reduce_max(t_mx[:, 0:1], sm, axis=AX.X))
            S.drain("vector")
            S.op("vector", lambda e, sm=sm: e.tensor_scalar(
                sm, sm, t_mx[:, 0:1], None, ALU.subtract))
            S.sig("vector", "vd")
            S.wait("scalar", "vd")
            S.op("scalar", lambda e, sm=sm: e.activation(sm, sm, AF.Exp))
            S.sig("scalar", "sa")
            S.wait("vector", "sa")
            S.op("vector", lambda e, sm=sm: e.reduce_sum(t_mx[:, 1:2], sm, axis=AX.X))
            S.drain("vector")
            S.op("vector", lambda e: e.reciprocal(t_mx[:, 1:2], t_mx[:, 1:2]))
            S.drain("vector")
            S.op("vector", lambda e, sm=sm: e.tensor_scalar(
                sm, sm, t_mx[:, 1:2], None, ALU.mult))
            v = S.sig("vector", "vd")
            S.wait("gpsimd", "vd", v)
            S.dma("gpsimd", s_out[ti * 128:(ti + 1) * 128, :], sm, sem="st")

        S.wait("gpsimd", "st")
        S.build()
    return nc


def build_F(LQ=L_SEQ, BQ=B_SZ):
    nc = make_nc()
    DL = 256
    NBLK = DL // 4
    dT_in = nc.dram_tensor("dT", [BQ, DL, LQ], F32, kind="ExternalInput").ap()
    uT_in = nc.dram_tensor("uT", [BQ, DL, LQ], F32, kind="ExternalInput").ap()
    BT_in = nc.dram_tensor("BTf", [BQ, L1_N, LQ], F32, kind="ExternalInput").ap()
    CT_in = nc.dram_tensor("CTf", [BQ, L1_N, LQ], F32, kind="ExternalInput").ap()
    adn_in = nc.dram_tensor("adn", [128, NBLK], F32, kind="ExternalInput").ap()
    dcol_in = nc.dram_tensor("dcol", [128, 2], F32, kind="ExternalInput").ap()
    G_in = nc.dram_tensor("G", [128, 32 * 128], BF16, kind="ExternalInput").ap()
    y_out = nc.dram_tensor("y", [BQ, DL, LQ], F32, kind="ExternalOutput").ap()
    du_scr = nc.dram_tensor("du_scr", [DL, LQ], F32, kind="Internal").ap()

    HT = LQ // 2
    NQ = max(1, HT // 512)
    QW = HT // NQ
    ctx, S, sbuf, psum = sched_ctx(nc)
    with ctx:
        t_ud = sbuf("t_ud", [128, 2 * LQ])
        t_Br = sbuf("t_Br", [128, LQ])
        t_Cr = sbuf("t_Cr", [128, LQ])
        t_dr = sbuf("t_dr", [128, 2 * HT])
        t_ur = sbuf("t_ur", [128, 2 * HT])
        t_dA = sbuf("t_dA", [128, 2 * HT])
        t_dB = sbuf("t_dB", [128, 2 * HT])
        t_h = sbuf("t_h", [128, 2 * HT])
        t_ch = sbuf("t_ch", [128, 2 * HT], BF16)
        t_cr = sbuf("t_cr", [128, NBLK])
        t_am = sbuf("t_am", [128, NBLK])
        t_dcol = sbuf("t_dcol", [128, 2])
        t_G = sbuf("t_G", [128, 32 * 128], BF16)
        t_y = sbuf("t_y", [128, 2 * HT])
        t_z = sbuf("t_z", [128, 1])
        p_y = [psum(f"p_y{i}", [128, 512]) for i in range(4)]

        for s in ("ld", "la", "lb", "lg", "lr0", "lr1", "lu0", "lu1",
                  "vd", "sa", "tp", "st", "du"):
            S.sem(s)

        S.dma("sync", t_am[:], adn_in, sem="la")
        S.dma("sync", t_dcol[:], dcol_in, sem="la")
        S.dma("gpsimd", t_G[:], G_in, sem="lg")
        S.op("vector", lambda e: e.memset(t_z[:], 0.0))
        S.sig("vector", "vd")
        S.wait("scalar", "la", 32)
        S.op("scalar", lambda e: e.activation(t_am[:], t_am[:], AF.Exp))
        S.sig("scalar", "sa")
        S.wait("vector", "sa")
        S.op("vector", lambda e: e.tensor_scalar(t_am[:], t_am[:], -1.0, None, ALU.mult))
        S.sig("vector", "vd")
        am_vd = S.count("vd")

        rec = {"gm": {}, "scan": {}, "dB": {}, "dA": {}}
        gj = 0
        nlr = [0, 0]
        nlu = [0, 0]
        for b in range(BQ):
            first_b = b == 0
            setup_vds = {}
            for i in range(4):
                ti, hh = divmod(i, 2)
                pj = i % 2
                if not first_b:
                    S.wait("sync", "sa", rec["dA"][gj - 1])
                    S.wait("sync", "vd", rec["scan"][gj - 1])
                    S.wait("scalar", "sa", rec["dA"][gj - 1])
                    S.wait("scalar", "vd", rec["scan"][gj - 1])
                if i >= 2:
                    S.wait("sync", "vd", setup_vds[i - 2])
                    S.wait("scalar", "vd", setup_vds[i - 2])
                S.dma("sync", t_dr[:, pj * HT:(pj + 1) * HT],
                      dT_in[b, ti * 128:(ti + 1) * 128, hh * HT:(hh + 1) * HT],
                      sem=("lr0" if pj == 0 else "lr1"))
                nlr[pj] += 16
                S.dma("scalar", t_ur[:, pj * HT:(pj + 1) * HT],
                      uT_in[b, ti * 128:(ti + 1) * 128, hh * HT:(hh + 1) * HT],
                      sem=("lu0" if pj == 0 else "lu1"))
                nlu[pj] += 16
                S.wait("vector", "lr0" if pj == 0 else "lr1", nlr[pj])
                S.wait("vector", "lu0" if pj == 0 else "lu1", nlu[pj])
                if b > 0 or i >= 2:
                    S.wait("vector", "du")
                S.op("vector", lambda e, pj=pj: e.tensor_tensor(
                    t_dB[:, pj * HT:(pj + 1) * HT], t_dr[:, pj * HT:(pj + 1) * HT],
                    t_ur[:, pj * HT:(pj + 1) * HT], ALU.mult))
                S.drain("vector")
                S.op("vector", lambda e, pj=pj, ti=ti, hh=hh: e.tensor_scalar(
                    t_ud[:, ti * LQ + hh * HT: ti * LQ + (hh + 1) * HT],
                    t_ur[:, pj * HT:(pj + 1) * HT], t_dcol[:, ti:ti + 1], None, ALU.mult))
                setup_vds[i] = S.sig("vector", "vd")
                S.wait("gpsimd", "vd")
                if not first_b:
                    S.wait("gpsimd", "lu0", rec["lu0_end"])
                    S.wait("gpsimd", "lu1", rec["lu1_end"])
                S.dma("gpsimd", du_scr[ti * 128:(ti + 1) * 128, hh * HT:(hh + 1) * HT],
                      t_dB[:, pj * HT:(pj + 1) * HT], sem="du")
            setup_vd = S.count("vd")
            S.wait("vector", "la", 32)
            if not first_b:
                S.wait("sync", "vd", rec["tmpl_free"])
            S.dma("sync", t_Br[:], BT_in[b].unsqueeze(0).broadcast_to([4, L1_N, LQ]), sem="lb")
            S.dma("sync", t_Cr[:], CT_in[b].unsqueeze(0).broadcast_to([4, L1_N, LQ]), sem="lb")

            for ht in range(2):
                t0 = ht * HT
                for j in range(NBLK):
                    pj = gj % 2
                    first = gj == 0
                    sl = slice(pj * HT, (pj + 1) * HT)
                    lrs = "lr0" if pj == 0 else "lr1"
                    lus = "lu0" if pj == 0 else "lu1"
                    if gj >= 2:
                        S.wait("sync", "sa", rec["dA"][gj - 2])
                    S.wait("sync", "du", 64 * (b + 1))
                    S.wait("sync", "vd", setup_vds[2 + (gj % 2)])
                    S.dma("sync", t_dr[:, sl],
                          dT_in[b, 4 * j:4 * j + 4, t0:t0 + HT].unsqueeze(1)
                          .broadcast_to([4, 32, HT]), sem=lrs)
                    nlr[pj] += 16
                    S.wait("scalar", "du", 64 * (b + 1))
                    S.wait("scalar", "vd", setup_vds[2 + (gj % 2)])
                    if gj >= 2:
                        S.wait("scalar", "vd", rec["dB"][gj - 2])
                    S.dma("scalar", t_ur[:, sl],
                          du_scr[4 * j:4 * j + 4, t0:t0 + HT].unsqueeze(1)
                          .broadcast_to([4, 32, HT]), sem=lus)
                    nlu[pj] += 16
                    S.wait("scalar", lrs, nlr[pj])
                    if first:
                        S.wait("scalar", "vd", am_vd)
                    if gj >= 2:
                        S.wait("scalar", "vd", rec["scan"][gj - 2])
                    S.op("scalar", lambda e, j=j, sl=sl: e.activation(
                        t_dA[:, sl], t_dr[:, sl], AF.Exp, scale=t_am[:, j:j + 1]))
                    S.sig("scalar", "sa")
                    rec["dA"][gj] = S.count("sa")
                    S.wait("vector", lus, nlu[pj])
                    S.wait("vector", "du", 64 * (b + 1))
                    S.wait("vector", "lb")
                    S.op("vector", lambda e, sl=sl, t0=t0: e.tensor_tensor(
                        t_dB[:, sl], t_ur[:, sl], t_Br[:, t0:t0 + HT], ALU.mult))
                    S.sig("vector", "vd")
                    rec["dB"][gj] = S.count("vd")
                    S.wait("vector", "sa", rec["dA"][gj])
                    init = t_z[:, 0:1] if ht == 0 else t_cr[:, j:j + 1]
                    S.op("vector", lambda e, init=init, sl=sl: e.tensor_tensor_scan(
                        t_h[:, sl], t_dA[:, sl], t_dB[:, sl], init, ALU.mult, ALU.add))
                    S.sig("vector", "vd")
                    rec["scan"][gj] = S.count("vd")
                    if ht == 0:
                        S.op("vector", lambda e, j=j, pj=pj: e.tensor_copy(
                            t_cr[:, j:j + 1], t_h[:, pj * HT + HT - 1: pj * HT + HT]))
                        S.drain("vector")
                    if gj >= 2:
                        S.wait("vector", "tp", rec["gm"][gj - 2])
                    S.op("vector", lambda e, sl=sl, t0=t0: e.tensor_tensor(
                        t_ch[:, sl], t_h[:, sl], t_Cr[:, t0:t0 + HT], ALU.mult))
                    ch_vd = S.sig("vector", "vd")
                    S.wait("tensor", "vd", ch_vd)
                    jj = j % 32
                    if first:
                        S.wait("tensor", "lg", 16)
                    if jj == 0 and not first:
                        S.wait("tensor", "vd", rec.get("ylast", 0))
                    for q in range(NQ):
                        S.op("tensor", lambda e, jj=jj, q=q, sl=sl: nc.tensor.matmul(
                            p_y[q][:, 0:QW], t_G[:, jj * 128:(jj + 1) * 128],
                            t_ch[:, sl][:, q * QW:(q + 1) * QW],
                            start=(jj == 0), stop=(jj == 31), skip_group_check=True))
                    S.sig("tensor", "tp")
                    rec["gm"][gj] = S.count("tp")
                    if jj == 31:
                        grp = j // 32
                        yslot = slice((grp % 2) * HT, (grp % 2) * HT + HT)
                        S.wait("vector", "tp", rec["gm"][gj])
                        S.wait("vector", "st")
                        for q in range(NQ):
                            S.op("vector", lambda e, q=q, grp=grp, t0=t0, yslot=yslot: e.tensor_tensor(
                                t_y[:, yslot][:, q * QW:(q + 1) * QW], p_y[q][:, 0:QW],
                                t_ud[:, grp * LQ + t0 + q * QW: grp * LQ + t0 + q * QW + QW],
                                ALU.add))
                        v = S.sig("vector", "vd")
                        rec["ylast"] = v
                        S.wait("gpsimd", "vd", v)
                        S.dma("gpsimd", y_out[b, grp * 128:(grp + 1) * 128, t0:t0 + HT],
                              t_y[:, yslot], sem="st")
                    gj += 1
            rec["tmpl_free"] = S.count("vd")
            rec["lu0_end"] = nlu[0]
            rec["lu1_end"] = nlu[1]

        S.wait("gpsimd", "st")
        S.build()
    return nc


def shard_F(delta_fm, xn1_fm, bc_all, inputs):
    A_log = np.asarray(inputs["L1_attn_A_log"], np.float32)
    D_par = np.asarray(inputs["L1_attn_D_param"], np.float32)
    import ml_dtypes
    G = np.zeros((128, 32, 128), np.float32)
    for jj in range(32):
        for dsub in range(4):
            G[dsub * 32:(dsub + 1) * 32, jj, jj * 4 + dsub] = 1.0
    G = np.ascontiguousarray(G.reshape(128, 32 * 128)).astype(ml_dtypes.bfloat16)
    maps = []
    for c in range(NCORES):
        dsl = slice(c * 256, (c + 1) * 256)
        adn = np.ascontiguousarray(
            A_log[dsl].reshape(64, 4, 32).transpose(1, 2, 0).reshape(128, 64))
        dcol = np.ascontiguousarray(D_par[dsl].reshape(2, 128).T)
        maps.append(dict(
            dT=np.ascontiguousarray(delta_fm[:, dsl, :]),
            uT=np.ascontiguousarray(xn1_fm[:, dsl, :]),
            BTf=np.ascontiguousarray(bc_all[:, 0:32, :]),
            CTf=np.ascontiguousarray(bc_all[:, 32:64, :]),
            adn=adn, dcol=dcol, G=G))
    return maps


def build_G():
    nc = make_nc()
    x1_in = nc.dram_tensor("x1", [D_MODEL, TH], F32, kind="ExternalInput").ap()
    xn1_in = nc.dram_tensor("xn1", [D_MODEL, TH], F32, kind="ExternalInput").ap()
    ym_in = nc.dram_tensor("ym", [D_MODEL, TH], F32, kind="ExternalInput").ap()
    sdev_in = nc.dram_tensor("sdev", [128, NEXP * 128], F32, kind="ExternalInput").ap()
    ssl_in = nc.dram_tensor("ssl", [TH, NEXP], F32, kind="ExternalInput").ap()
    ew_in = nc.dram_tensor("ew", [D_MODEL, D_MODEL], F32, kind="ExternalInput").ap()
    eb_in = nc.dram_tensor("eb", [128, 16], F32, kind="ExternalInput").ap()
    out_o = nc.dram_tensor("out", [D_MODEL, TH], F32, kind="ExternalOutput").ap()
    w_scr = nc.dram_tensor("w_scr", [TH, 1], F32, kind="Internal").ap()

    ctx, S, sbuf, psum = sched_ctx(nc)
    with ctx:
        t_sd = sbuf("t_sd", [128, NEXP * 128])
        t_ss = sbuf("t_ss", [128, 16 * NEXP])
        t_msk = sbuf("t_msk", [128, NEXP * 128], I32)
        t_cnt = sbuf("t_cnt", [128, NEXP], I32)
        t_cntf = sbuf("t_cntf", [128, NEXP])
        t_crow = sbuf("t_crow", [1, NEXP])
        t_lo = sbuf("t_lo", [1, NEXP], I32)
        t_hi = sbuf("t_hi", [1, NEXP], I32)
        t_mid = sbuf("t_mid", [1, NEXP], I32)
        t_sel = sbuf("t_sel", [1, NEXP], I32)
        t_seln = sbuf("t_seln", [1, NEXP], I32)
        t_tau = sbuf("t_tau", [128, NEXP])
        t_one = sbuf("t_one", [1, 128])
        t_onec = sbuf("t_onec", [128, 1])
        t_wc = sbuf("t_wc", [128, 1])
        t_wrow = sbuf("t_wrow", [1, TH])
        t_xnb = sbuf("t_xnb", [128, 16 * TH], BF16)
        t_w = sbuf("t_w", [128, 2 * 16 * 128], BF16)
        t_x1 = sbuf("t_x1", [128, TH])
        t_ym = sbuf("t_ym", [128, TH])
        t_eb = sbuf("t_eb", [128, 16])
        t_ev = sbuf("t_ev", [128, TH])
        t_t1 = sbuf("t_t1", [128, 512])
        p_tau = psum("p_tau", [128, 512])
        p_cnt = psum("p_cnt", [1, 512])
        p_w = [psum(f"p_w{i}", [128, 512]) for i in range(4)]
        p_mm = [psum(f"p_mm{i}", [128, 512]) for i in range(2)]

        for s in ("ld", "lg", "lw0", "lw1", "lwr", "vd", "sa", "tp", "st", "wl"):
            S.sem(s)

        S.dma("sync", t_sd[:], sdev_in, sem="ld")
        S.dma("sync", t_ss[:].rearrange("p (a e) -> p a e", e=NEXP),
              ssl_in.rearrange("(a p) e -> p a e", p=128), sem="ld")
        S.dma("sync", t_eb[:], eb_in, sem="ld")
        S.dma("gpsimd", t_xnb[:].rearrange("p (a t) -> p a t", t=TH),
              xn1_in.rearrange("(a p) t -> p a t", p=128), sem="lg")
        S.op("vector", lambda e: e.memset(t_one[:], 1.0))
        S.op("vector", lambda e: e.memset(t_onec[:], 1.0))
        S.op("vector", lambda e: e.memset(t_lo[:], 0))
        S.op("vector", lambda e: e.memset(t_hi[:], 0x7F800000))
        S.sig("vector", "vd")
        S.wait("tensor", "vd")
        S.wait("vector", "ld", 48)

        for it in range(31):
            S.drain("vector")
            S.op("vector", lambda e: e.tensor_tensor(t_mid[:], t_lo[:], t_hi[:], ALU.add))
            S.drain("vector")
            S.op("vector", lambda e: e.tensor_scalar(
                t_mid[:], t_mid[:], 1, None, ALU.logical_shift_right))
            S.sig("vector", "vd")
            S.wait("tensor", "vd")
            S.op("tensor", lambda e: nc.tensor.matmul(
                p_tau[:, 0:NEXP], t_one[:, :], t_mid[:].bitcast(F32),
                start=True, stop=True))
            S.sig("tensor", "tp")
            S.wait("vector", "tp")
            S.op("vector", lambda e: e.tensor_copy(t_tau[:, :], p_tau[:, 0:NEXP]))
            S.drain("vector")
            S.op("vector", lambda e: e.tensor_tensor(
                t_msk[:].rearrange("p (e a) -> p e a", a=128),
                t_sd[:].rearrange("p (e a) -> p e a", a=128),
                t_tau[:, :, None].broadcast_to([128, NEXP, 128]), ALU.is_ge))
            S.drain("vector")
            def _cntred(e):
                with nc.allow_low_precision(reason="int count"):
                    e.tensor_reduce(t_cnt[:, :, None],
                                    t_msk[:].rearrange("p (e a) -> p e a", a=128),
                                    axis=AX.X, op=ALU.add)
            S.op("vector", _cntred)
            S.drain("vector")
            S.op("vector", lambda e: e.tensor_copy(t_cntf[:], t_cnt[:]))
            S.sig("vector", "vd")
            S.wait("tensor", "vd")
            S.op("tensor", lambda e: nc.tensor.matmul(
                p_cnt[:, 0:NEXP], t_onec[:, :], t_cntf[:], start=True, stop=True))
            S.sig("tensor", "tp")
            S.wait("vector", "tp")
            S.op("vector", lambda e: e.tensor_copy(t_crow[:], p_cnt[:, 0:NEXP]))
            S.drain("vector")
            S.op("vector", lambda e: e.tensor_scalar(
                t_sel[:], t_crow[:], float(TOPK), None, ALU.is_ge))
            S.op("vector", lambda e: e.tensor_scalar(
                t_seln[:], t_crow[:], float(TOPK), None, ALU.is_lt))
            S.drain("vector")
            S.op("vector", lambda e: e.copy_predicated(t_lo[:], t_sel[:], t_mid[:]))
            S.drain("vector")
            S.op("vector", lambda e: e.copy_predicated(t_hi[:], t_seln[:], t_mid[:]))
        S.sig("vector", "vd")
        S.wait("tensor", "vd")
        S.op("tensor", lambda e: nc.tensor.matmul(
            p_tau[:, 0:NEXP], t_one[:, :], t_lo[:].bitcast(F32), start=True, stop=True))
        S.sig("tensor", "tp")
        S.wait("vector", "tp")
        S.op("vector", lambda e: e.tensor_copy(t_tau[:, :], p_tau[:, 0:NEXP]))
        S.drain("vector")

        for ti in range(16):
            ssl = t_ss[:, ti * NEXP:(ti + 1) * NEXP]
            S.op("vector", lambda e, ssl=ssl: e.tensor_tensor(
                t_cnt[:, :], ssl, t_tau[:, :], ALU.is_ge))
            S.drain("vector")
            S.op("vector", lambda e: e.tensor_copy(t_cntf[:], t_cnt[:]))
            S.drain("vector")
            S.op("vector", lambda e, ssl=ssl: e.tensor_tensor(
                t_cntf[:], t_cntf[:], ssl, ALU.mult))
            S.drain("vector")
            S.op("vector", lambda e: e.reduce_sum(t_wc[:, 0:1], t_cntf[:], axis=AX.X))
            v = S.sig("vector", "vd")
            S.wait("gpsimd", "vd", v)
            S.dma("gpsimd", w_scr[ti * 128:(ti + 1) * 128, :], t_wc[:, 0:1], sem="wl")
            S.wait("vector", "wl")
        S.wait("sync", "wl")
        S.dma("sync", t_wrow[:], w_scr[:, 0].unsqueeze(0), sem="lwr")
        S.wait("tensor", "lwr", 16)
        for tt_i in range(4):
            S.op("tensor", lambda e, tt_i=tt_i: nc.tensor.matmul(
                p_w[tt_i][:, :], t_one[:, :], t_wrow[0:1, tt_i * 512:(tt_i + 1) * 512],
                start=True, stop=True))
        S.sig("tensor", "tp")
        wrep_tp = S.count("tp")

        S.wait("tensor", "lg")
        ev_vd = {}
        mm_vd = {}
        mm_tp = {}
        nmm = 0
        for do in range(16):
            sw = do % 2
            if do >= 2:
                S.wait("gpsimd", "tp", mm_tp[do - 2])
            S.dma("gpsimd", t_w[:, sw * 2048:(sw + 1) * 2048].rearrange("p (a j) -> p a j", j=128),
                  ew_in[:, do * 128:(do + 1) * 128].rearrange("(a p) j -> p a j", p=128),
                  sem=("lw0" if sw == 0 else "lw1"))
            if do >= 1:
                S.wait("sync", "vd", ev_vd[do - 1])
            S.dma("sync", t_x1[:], x1_in[do * 128:(do + 1) * 128, :], sem="ld")
            S.dma("sync", t_ym[:], ym_in[do * 128:(do + 1) * 128, :], sem="ld")
            S.wait("tensor", "lw0" if sw == 0 else "lw1", 16 * (do // 2 + 1))
            S.wait("tensor", "tp", wrep_tp)
            if do >= 1:
                S.wait("vector", "st")
            for tt_i in range(4):
                pm = p_mm[nmm % 2]
                if nmm >= 2:
                    S.wait("tensor", "vd", mm_vd[nmm - 2])
                for k in range(16):
                    S.op("tensor", lambda e, pm=pm, sw=sw, k=k, tt_i=tt_i:
                         nc.tensor.matmul(pm[:, :],
                                          t_w[:, sw * 2048 + k * 128:sw * 2048 + (k + 1) * 128],
                                          t_xnb[:, k * TH + tt_i * 512: k * TH + tt_i * 512 + 512],
                                          start=(k == 0), stop=(k == 15)))
                S.sig("tensor", "tp")
                S.wait("vector", "tp")
                S.wait("vector", "ld")
                S.op("vector", lambda e, pm=pm, do=do: e.tensor_scalar(
                    t_t1[:, :], pm[:, :], t_eb[:, do:do + 1], None, ALU.add))
                S.drain("vector")
                S.op("vector", lambda e, tt_i=tt_i: e.tensor_tensor(
                    t_t1[:, :], t_t1[:, :], p_w[tt_i][:, :], ALU.mult))
                S.drain("vector")
                S.op("vector", lambda e, tt_i=tt_i: e.tensor_tensor(
                    t_t1[:, :], t_t1[:, :], t_x1[:, tt_i * 512:(tt_i + 1) * 512], ALU.add))
                S.drain("vector")
                S.op("vector", lambda e, tt_i=tt_i: e.tensor_tensor(
                    t_ev[:, tt_i * 512:(tt_i + 1) * 512], t_t1[:, :],
                    t_ym[:, tt_i * 512:(tt_i + 1) * 512], ALU.add))
                v = S.sig("vector", "vd")
                mm_vd[nmm] = v
                nmm += 1
            ev_vd[do] = S.count("vd")
            mm_tp[do] = S.count("tp")
            S.wait("gpsimd", "vd", ev_vd[do])
            S.dma("gpsimd", out_o[do * 128:(do + 1) * 128, :], t_ev[:], sem="st")

        S.wait("gpsimd", "st")
        S.build()
    return nc


def kernel(**inputs):
    import ml_dtypes
    f32 = np.float32
    _ = inputs["L1_attn_z"]

    ncA = build_A()
    outsA = run_launch(ncA, shard_A(inputs), ["o_xnT", "o_z", "o_xbc", "o_dt"])
    xbc_full = np.zeros((B_SZ, L_SEQ, CONV_DIM), f32)
    dt_full = np.zeros((B_SZ, L_SEQ, NHEADS), f32)
    for c in range(NCORES):
        b, h = divmod(c, 2)
        s = h * TH
        xbc_full[b, s:s + TH] = outsA[c]["o_xbc"].T
        dt_full[b, s:s + TH] = outsA[c]["o_dt"].T
    xnT_full = np.concatenate([outsA[c]["o_xnT"] for c in range(NCORES)], axis=1)
    xnT_full = np.ascontiguousarray(xnT_full)

    ncB = build_B()
    outsB = run_launch(ncB, shard_B(xbc_full, dt_full, inputs), ["Y"])
    Yt_fm = np.zeros((B_SZ, D_INNER, L_SEQ), f32)
    for c in range(NCORES):
        b, hh = divmod(c, 2)
        Yt_fm[b, hh * 2048:(hh + 1) * 2048, :] = outsB[c]["Y"].T

    D_par = np.asarray(inputs["L0_attn_D_param"], f32)
    nw_in = np.asarray(inputs["L0_attn_norm_w"], f32)
    dpar = np.ascontiguousarray(np.repeat(D_par, HEADDIM).reshape(32, 128).T)
    nw = np.ascontiguousarray(nw_in.reshape(32, 128).T)
    w_out = np.ascontiguousarray(inputs["L0_attn_W_out"], dtype=f32)
    x0_flat = np.asarray(inputs["L0_attn_u"], f32).reshape(B_SZ * L_SEQ, D_MODEL)
    ncC = build_C()
    mapsC = []
    for c in range(NCORES):
        b, h = divmod(c, 2)
        tok = slice(h * TH, (h + 1) * TH)
        gtok = slice(b * L_SEQ + h * TH, b * L_SEQ + (h + 1) * TH)
        mapsC.append(dict(
            YT=np.ascontiguousarray(Yt_fm[b][:, tok]),
            z=outsA[c]["o_z"],
            Xfm=outsA[c]["o_xbc"][:D_INNER],
            dpar=dpar, nw=nw, w_out=w_out,
            x0T=np.ascontiguousarray(x0_flat[gtok].T)))
    outsC = run_launch(ncC, mapsC, ["r0"])

    ncD = build_D()
    outsD = run_launch(ncD, shard_D(xnT_full, inputs), ["part"])

    wbcT = np.ascontiguousarray(np.asarray(inputs["L1_attn_W_bc"], f32).T)
    wdtT = np.ascontiguousarray(np.asarray(inputs["L1_attn_W_dt"], f32).T)
    bdt = np.ascontiguousarray(np.asarray(inputs["L1_attn_b_dt"], f32).reshape(16, 128).T)
    rw = np.ascontiguousarray(inputs["L1_ffn_router_w"], dtype=f32)
    rb = np.ascontiguousarray(np.asarray(inputs["L1_ffn_router_b"], f32).reshape(-1, 1))
    idf = np.eye(128, dtype=f32)
    ncE = build_E()
    mapsE = []
    for c in range(NCORES):
        tok = slice(c * TH, (c + 1) * TH)
        m = dict(r0=outsC[c]["r0"], wbcT=wbcT, wdtT=wdtT, bdt=bdt, rw=rw, rb=rb, idf=idf)
        for i in range(NCORES):
            m[f"pp{i}"] = np.ascontiguousarray(outsD[i]["part"][:, tok])
        mapsE.append(m)
    outsE = run_launch(ncE, mapsE, ["x1", "xn1", "dl", "bc", "s"])

    delta_fm = np.zeros((B_SZ, D_MODEL, L_SEQ), f32)
    xn1_fm = np.zeros((B_SZ, D_MODEL, L_SEQ), f32)
    bc_all = np.zeros((B_SZ, 64, L_SEQ), f32)
    for c in range(NCORES):
        b, h = divmod(c, 2)
        tok = slice(h * TH, (h + 1) * TH)
        delta_fm[b][:, tok] = outsE[c]["dl"]
        xn1_fm[b][:, tok] = outsE[c]["xn1"]
        bc_all[b][:, tok] = outsE[c]["bc"]
    S_full = np.concatenate([outsE[c]["s"] for c in range(NCORES)], axis=0)

    ncF = build_F()
    outsF = run_launch(ncF, shard_F(delta_fm, xn1_fm, bc_all, inputs), ["y"])
    ym_fm = np.zeros((B_SZ, D_MODEL, L_SEQ), f32)
    for c in range(NCORES):
        ym_fm[:, c * 256:(c + 1) * 256, :] = outsF[c]["y"]

    sdev = np.ascontiguousarray(
        S_full.reshape(128, 128, NEXP).transpose(1, 2, 0).reshape(128, NEXP * 128))
    ew = np.ascontiguousarray(inputs["L1_ffn_expert_w"], dtype=f32)
    eb = np.ascontiguousarray(np.asarray(inputs["L1_ffn_expert_b"], f32).reshape(16, 128).T)
    ncG = build_G()
    mapsG = []
    for c in range(NCORES):
        b, h = divmod(c, 2)
        tok = slice(h * TH, (h + 1) * TH)
        gtok = slice(c * TH, (c + 1) * TH)
        mapsG.append(dict(
            x1=outsE[c]["x1"], xn1=outsE[c]["xn1"],
            ym=np.ascontiguousarray(ym_fm[b][:, tok]),
            sdev=sdev, ssl=np.ascontiguousarray(S_full[gtok]),
            ew=ew, eb=eb))
    outsG = run_launch(ncG, mapsG, ["out"])

    out = np.zeros((B_SZ * L_SEQ, D_MODEL), f32)
    for c in range(NCORES):
        out[c * TH:(c + 1) * TH] = outsG[c]["out"].T
    return out.reshape(B_SZ, L_SEQ, D_MODEL)
